# version 03
# baseline (speedup 1.0000x reference)
import contextlib
import numpy as np
import concourse.bass as bass
import concourse.mybir as mybir
from concourse.bass_utils import run_bass_kernel_spmd

F32 = mybir.dt.float32
BF16 = mybir.dt.bfloat16
AF = mybir.ActivationFunctionType
ALU = mybir.AluOpType

NCORES = 8
T = 1024
CH = 128
NC_CH = T // CH
DM = 2048
DI = 4096
NH = 64
HD = 64
NG = 8
DS = 128
CONV_DIM = 6144
EPS = 1e-5
PEN = -30000.0
TCOL = T + 3
CSROW = T + 16

DEBUG_TAPS = ()


class Builder:
    def __init__(self, nc, taps=()):
        self.nc = nc
        self.taps = set(taps)
        self.stack = contextlib.ExitStack()
        self.sems = {}
        self.cnt = {}
        self.lastwait = {}
        for s in ("te", "ve", "sc", "cc", "out"):
            self.sems[s] = self.stack.enter_context(nc.semaphore(s))
            self.cnt[s] = 0
        self.t = {}

    def ev(self, instr, sem, n=1):
        self.cnt[sem] += n
        instr.then_inc(self.sems[sem], n)
        return (sem, self.cnt[sem])

    def now(self, sem):
        return (sem, self.cnt[sem])

    def wait(self, eng, engname, *evs):
        for e in evs:
            if e is None:
                continue
            sem, val = e
            if val <= 0:
                continue
            if self.lastwait.get((engname, sem), -1) >= val:
                continue
            eng.wait_ge(self.sems[sem], val)
            self.lastwait[(engname, sem)] = val

    def dsem(self, name):
        if name not in self.sems:
            self.sems[name] = self.stack.enter_context(self.nc.semaphore(name))
            self.cnt[name] = 0
        return name

    def dev(self, ins, sem):
        self.dsem(sem)
        return self.ev(ins, sem, 16)

    def wait_all_dma(self, eng, engname="sy"):
        for s, v in list(self.cnt.items()):
            if s in ("te", "ve", "sc", "gp"):
                continue
            self.wait(eng, engname, (s, v))

    def vchain(self, ins):
        e = self.ev(ins, "ve")
        self.nc.vector.wait_ge(self.sems["ve"], e[1])
        self.lastwait[("ve", "ve")] = e[1]
        return e

    def barrier(self):
        self.nc.all_engine_barrier()
        for s in ("te", "ve", "sc"):
            for e in ("te", "ve", "sc", "gp", "sy"):
                self.lastwait[(e, s)] = self.cnt[s]

    def sb(self, name, shape, dtype, scope=None):
        t = (scope or self.stack).enter_context(self.nc.sbuf_tensor(name, shape, dtype))
        self.t[name] = t
        return t

    def ps(self, name, shape, dtype, scope=None):
        t = (scope or self.stack).enter_context(self.nc.psum_tensor(name, shape, dtype))
        self.t[name] = t
        return t

    def dram(self, name, shape, dtype, tap=False):
        kind = "ExternalOutput" if (tap and name in self.taps) else "Internal"
        t = self.nc.dram_tensor(name, shape, dtype, kind=kind)
        self.t[name] = t
        return t


def build_program(taps=(), phases=99):
    nc = bass.Bass("TRN2", target_bir_lowering=False)
    B = Builder(nc, taps)
    t = B.t
    te, ve, sc, gp, sy = nc.tensor, nc.vector, nc.scalar, nc.gpsimd, nc.sync

    inputs = {
        "xfm": ([DM, TCOL], F32),
        "xres": ([T, DM], F32),
        "winT": ([DM, 10304], F32),
        "woutT": ([DI, DM], F32),
        "w1T": ([DM, DM], F32),
        "w3T": ([DM, DM], F32),
        "w2T": ([DM, DM], F32),
        "wrT": ([DM, 8], F32),
        "convw": ([CONV_DIM, 4], F32),
        "convb": ([CONV_DIM, 1], F32),
        "acol": ([NH, 1], F32),
        "drep": ([128, NH], F32),
        "dtbcol": ([NH, 1], F32),
        "gcol": ([128, 32], F32),
        "ident": ([128, 128], F32),
        "pen": ([128, 128], F32),
        "onescol": ([128, 1], BF16),
        "sendmask": ([128, 1], F32),
        "recvmask": ([128, 1], F32),
        "ownermask": ([8, 1], F32),
    }
    for nm, (shp, dt_) in inputs.items():
        t[nm] = nc.dram_tensor(nm, shp, dt_, kind="ExternalInput")
    out_ext = nc.dram_tensor("out", [T, DM], F32, kind="ExternalOutput")

    xpart_h = B.dram("xpart_h", [DI, T], BF16, tap=True)
    b_h = B.dram("b_h", [NG * DS, T], BF16, tap=True)
    c_h = B.dram("c_h", [NG * DS, T], BF16, tap=True)
    cs_h = B.dram("cs_h", [NH, CSROW], F32, tap=True)
    csT_h = B.dram("csT_h", [NC_CH, 128, NH], F32)
    gcs_h = B.dram("gcs_h", [NC_CH, NH], F32, tap=True)
    p_h = B.dram("p_h", [NC_CH, 128, DI], BF16, tap=True)
    y_h = B.dram("y_h", [T, DI], BF16, tap=True)
    yn_h = B.dram("yn_h", [DI, T], BF16, tap=True)
    x1_h = B.dram("x1_h", [T, DM], BF16, tap=True)
    x1lo_h = B.dram("x1lo_h", [T, DM], BF16, tap=True)
    x1f_h = B.dram("x1f_h", [T, DM], F32, tap=True)
    rstd_h = B.dram("rstd_h", [NG, T], F32, tap=True)
    wg_h = B.dram("wg_h", [128, 8 * NC_CH], BF16, tap=True)
    g_h = B.dram("g_h", [8, DM], F32, tap=True)
    o8_h = B.dram("o8_h", [8, DM], F32, tap=True)
    h1_h = B.dram("h1_h", [DM, T], BF16, tap=True)
    h2_h = B.dram("h2_h", [T, DM], BF16, tap=True)
    flush_h = B.dram("flush_h", [128, 4], F32)
    ccin = B.dram("ccin", [128, DI], BF16)
    ccout = B.dram("ccout", [128, DI], BF16)
    gin = B.dram("gin", [8, DM], F32)
    gout = B.dram("gout", [8, DM], F32)

    dt_s = B.sb("dt_s", [NH, T], F32)
    adt_s = B.sb("adt_s", [NH, T], F32)
    cs_s = B.sb("cs_s", [NH, CSROW], F32)
    dtT_s = B.sb("dtT_s", [128, NC_CH * NH], BF16)
    csT_s = B.sb("csT_s", [128, NC_CH * NH], F32)
    negcsT_s = B.sb("negcsT_s", [128, NC_CH * NH], F32)
    expcsT_s = B.sb("expcsT_s", [128, NC_CH * NH], F32)
    ident_s = B.sb("ident_s", [128, 128], F32)
    convw_s = B.sb("convw_s", [128, 48 * 4], F32)
    convb_s = B.sb("convb_s", [128, 48], F32)
    acol_s = B.sb("acol_s", [NH, 1], F32)
    drep_s = B.sb("drep_s", [128, NH], F32)
    dtb_s = B.sb("dtb_s", [NH, 1], F32)
    gcol_s = B.sb("gcol_s", [128, 32], F32)
    ones_s = B.sb("ones_s", [128, 1], BF16)
    smask_s = B.sb("smask_s", [128, 1], F32)
    rmask_s = B.sb("rmask_s", [128, 1], F32)
    omask_s = B.sb("omask_s", [8, 1], F32)
    wr_s = B.sb("wr_s", [128, 16 * 8], F32)
    eps_s = B.sb("eps_s", [128, 1], F32)
    main_scope = contextlib.ExitStack()
    xfm_s = B.sb("xfm_s", [128, 16 * TCOL], BF16, main_scope)

    for nm, dst in (("ident", ident_s), ("acol", acol_s), ("drep", drep_s),
                    ("dtbcol", dtb_s), ("gcol", gcol_s), ("onescol", ones_s),
                    ("sendmask", smask_s), ("recvmask", rmask_s), ("ownermask", omask_s)):
        B.dev(sy.dma_start(out=dst[:, :], in_=t[nm].ap()), "cst")
    B.dev(sy.dma_start(out=convb_s[:, :],
                      in_=t["convb"].ap().rearrange("(c p) o -> p (c o)", p=128),
                      allow_slow_non_contiguous=True), "cst")
    B.dev(sy.dma_start(out=convw_s[:].rearrange("p (c k) -> p c k", k=4),
                      in_=t["convw"].ap().rearrange("(c p) k -> p c k", p=128)), "cst")
    B.dev(sy.dma_start(out=wr_s[:].rearrange("p (k e) -> p k e", e=8),
                      in_=t["wrT"].ap().rearrange("(k p) e -> p k e", p=128)), "cst")
    ev_consts = B.now("cst")
    B.ev(nc.vector.memset(eps_s[:, :], EPS), "ve")
    ev_xfm = B.dev(gp.dma_start(
        out=xfm_s[:].rearrange("p (k c) -> p k c", c=TCOL),
        in_=t["xfm"].ap().rearrange("(k p) c -> p k c", p=128)), "xfm")

    p1 = contextlib.ExitStack()
    wbuf = [B.sb(f"wbuf{i}", [128, 16 * 128], BF16, p1) for i in range(2)]
    cvb = [B.sb(f"cvb{i}", [128, TCOL], BF16, p1) for i in range(2)]
    cvo = [B.sb(f"cvo{i}", [128, T], BF16, p1) for i in range(2)]
    dtraw_s = B.sb("dtraw_s", [NH, TCOL], F32, p1)
    dttmp_s = B.sb("dttmp_s", [NH, T], F32, p1)
    pp = [B.ps(f"pp{i}", [128, 512], F32, p1) for i in range(2)]

    NT = ((0, 343), (343, 342), (685, 342))
    wlast_use = [None, None]
    cv_last_use = [None, None]
    cvo_last_use = [None, None]
    pp_last = [None, None]

    B.wait(te, "te", ev_xfm)
    for fo in range(49):
        par = fo % 2
        cols = 128 if fo < 48 else 64
        src = bass.AP(t["winT"], fo * 128, [[10304, 128], [128 * 10304, 16], [1, cols]])
        B.wait(gp, "gp", wlast_use[par])
        ev_w = B.dev(gp.dma_start(
            out=wbuf[par][:].rearrange("p (k f) -> p k f", f=128)[:, :, 0:cols], in_=src), f"wb{par}")
        B.wait(te, "te", ev_w)
        for nt in range(3):
            c0, ncols = NT[nt]
            ppar = (fo * 3 + nt) % 2
            B.wait(te, "te", pp_last[ppar])
            for k in range(16):
                ins = te.matmul(pp[ppar][0:cols, 0:ncols],
                                wbuf[par][:, k * 128:k * 128 + cols],
                                xfm_s[:, k * TCOL + c0: k * TCOL + c0 + ncols],
                                start=(k == 0), stop=(k == 15))
            ev_mm = B.ev(ins, "te")
            if nt == 2:
                wlast_use[par] = ev_mm
            if fo < 48:
                B.wait(sc, "sc", ev_mm, cv_last_use[par] if nt == 0 else None)
                ev_ev = B.ev(sc.copy(cvb[par][:, c0:c0 + ncols], pp[ppar][:, 0:ncols]), "sc")
            else:
                B.wait(sc, "sc", ev_mm)
                ev_ev = B.ev(sc.copy(dtraw_s[:, c0:c0 + ncols], pp[ppar][0:64, 0:ncols]), "sc")
            pp_last[ppar] = ev_ev
        if fo < 48:
            B.wait(ve, "ve", ev_ev, cvo_last_use[par], ev_consts if fo == 0 else None)
            ve.tensor_scalar(cvo[par][:, :], cvb[par][:, 0:T],
                             convw_s[:, fo * 4: fo * 4 + 1],
                             convb_s[:, fo: fo + 1], ALU.mult, ALU.add)
            for k in range(1, 4):
                ins = ve.scalar_tensor_tensor(cvo[par][:, :], cvb[par][:, k:k + T],
                                              convw_s[:, fo * 4 + k: fo * 4 + k + 1],
                                              cvo[par][:, :], ALU.mult, ALU.add)
            ev_conv = B.ev(ins, "ve")
            cv_last_use[par] = ev_conv
            B.wait(sc, "sc", ev_conv)
            ev_silu = B.ev(sc.activation(out=cvo[par][:, :], in_=cvo[par][:, :], func=AF.Silu), "sc")
            B.wait(sy, "sy", ev_silu)
            if fo < 32:
                dst = xpart_h.ap()[fo * 128:(fo + 1) * 128, :]
            elif fo < 40:
                dst = b_h.ap()[(fo - 32) * 128:(fo - 31) * 128, :]
            else:
                dst = c_h.ap()[(fo - 40) * 128:(fo - 39) * 128, :]
            cvo_last_use[par] = B.dev(sy.dma_start(out=dst, in_=cvo[par][:, :]), f"cv{par}")
        else:
            B.wait(sc, "sc", ev_ev, ev_consts)
            sc.activation(out=dt_s[:, :], in_=dtraw_s[:, 3:TCOL], func=AF.Relu, bias=dtb_s[:, :])
            sc.activation(out=dttmp_s[:, :], in_=dtraw_s[:, 3:TCOL], func=AF.Abs, bias=dtb_s[:, :])
            sc.activation(out=dttmp_s[:, :], in_=dttmp_s[:, :], func=AF.Exp, scale=-1.0)
            ev_l = B.ev(sc.activation(out=dttmp_s[:, :], in_=dttmp_s[:, :], func=AF.Ln, bias=1.0), "sc")
            B.wait(ve, "ve", ev_l)
            B.ev(ve.tensor_add(dt_s[:, :], dt_s[:, :], dttmp_s[:, :]), "ve")

    if "dt_tap" in B.taps:
        dt_tap = B.dram("dt_tap", [NH, T], F32, tap=True)
        B.wait(sy, "sy", B.now("ve"))
        B.dev(sy.dma_start(out=dt_tap.ap(), in_=dt_s[:, :]), "tap")
    B.barrier()
    p1.close()
    if phases <= 1:
        B.wait_all_dma(sy)
        return nc, B

    p2 = contextlib.ExitStack()
    tps = B.ps("tps", [128, 128], F32, p2)
    g8_s = B.sb("g8_s", [NH, 8], F32, p2)
    g8b_s = B.sb("g8b_s", [NH, 8], F32, p2)
    dtTf_s = B.sb("dtTf_s", [128, NH], F32, p2)

    ve.tensor_scalar_mul(adt_s[:, :], dt_s[:, :], acol_s[:, :])
    for c in range(NC_CH):
        ve.tensor_tensor_scan(cs_s[:, c * CH:(c + 1) * CH], adt_s[:, c * CH:(c + 1) * CH],
                              adt_s[:, c * CH:(c + 1) * CH], 0.0, ALU.add, ALU.bypass)
    alast_ap = bass.AP(cs_s, CH - 1, [[CSROW, NH], [CH, NC_CH]])
    B.vchain(ve.tensor_copy(g8_s[:, :], alast_ap))
    ve.tensor_copy(g8b_s[:, 0:1], g8_s[:, 0:1])
    B.vchain(ve.tensor_add(g8b_s[:, 1:8], g8_s[:, 1:8], g8_s[:, 0:7]))
    ve.tensor_copy(g8_s[:, 0:2], g8b_s[:, 0:2])
    B.vchain(ve.tensor_add(g8_s[:, 2:8], g8b_s[:, 2:8], g8b_s[:, 0:6]))
    ve.tensor_copy(g8b_s[:, 0:4], g8_s[:, 0:4])
    B.vchain(ve.tensor_add(g8b_s[:, 4:8], g8_s[:, 4:8], g8_s[:, 0:4]))
    ve.memset(cs_s[:, T:T + 1], 0.0)
    ev_cs = B.ev(ve.tensor_copy(cs_s[:, T + 1:T + 8], g8b_s[:, 0:7]), "ve")
    B.wait(sy, "sy", ev_cs)
    ev_csh = B.dev(sy.dma_start(out=cs_h.ap(), in_=cs_s[:, :]), "csh")

    ev_evac = None
    for c in range(NC_CH):
        B.wait(te, "te", ev_cs if c == 0 else None, ev_evac)
        ev1 = B.ev(te.transpose(tps[:, 0:NH], cs_s[0:NH, c * CH:(c + 1) * CH], ident_s[0:NH, 0:NH]), "te")
        B.wait(sc, "sc", ev1)
        ev2 = B.ev(sc.copy(csT_s[:, c * NH:(c + 1) * NH], tps[:, 0:NH]), "sc")
        B.wait(te, "te", ev2)
        ev3 = B.ev(te.transpose(tps[:, 0:NH], dt_s[0:NH, c * CH:(c + 1) * CH], ident_s[0:NH, 0:NH]), "te")
        B.wait(sc, "sc", ev3)
        ev4 = B.ev(sc.copy(dtTf_s[:, 0:NH], tps[:, 0:NH]), "sc")
        ev_evac = ev4
        B.wait(ve, "ve", ev4)
        B.ev(ve.tensor_copy(dtT_s[:, c * NH:(c + 1) * NH], dtTf_s[:, 0:NH]), "ve")
    ev_csT_done = B.now("sc")
    B.wait(sc, "sc", ev_csT_done)
    sc.activation(out=expcsT_s[:, :], in_=csT_s[:, :], func=AF.Exp)
    B.ev(sc.mul(negcsT_s[:, :], csT_s[:, :], -1.0), "sc")
    B.wait(sy, "sy", ev_csT_done)
    ev_csTh = B.dev(sy.dma_start(out=csT_h.ap().rearrange("c s h -> s c h"),
                                 in_=csT_s[:].rearrange("s (c h) -> s c h", h=NH)), "cth")
    B.wait(te, "te", ev_cs, ev_evac)
    ev_gt = B.ev(te.transpose(tps[0:8, 0:NH], cs_s[0:NH, T:T + 8], ident_s[0:NH, 0:NH]), "te")
    B.wait(sc, "sc", ev_gt)
    ev_gc8 = B.ev(sc.copy(dtTf_s[0:8, 0:NH], tps[0:8, 0:NH]), "sc")
    B.wait(sy, "sy", ev_gc8)
    ev_gcsh = B.dev(sy.dma_start(out=gcs_h.ap(), in_=dtTf_s[0:8, 0:NH]), "gch")
    if "cs_tap" in B.taps:
        cs_tap = B.dram("cs_tap", [128, NC_CH * NH], F32, tap=True)
        B.wait(sy, "sy", B.now("sc"))
        B.dev(sy.dma_start(out=cs_tap.ap(), in_=csT_s[:, :]), "tap")
    B.barrier()
    p2.close()
    ev_csT = B.now("sc")
    if phases <= 2:
        B.wait_all_dma(sy)
        return nc, B

    p3 = contextlib.ExitStack()
    xtmA = [B.sb(f"xtmA{i}", [128, DI], BF16, p3) for i in range(2)]
    btm = [B.sb(f"btm{i}", [128, NG * DS], BF16, p3) for i in range(2)]
    xdd = B.sb("xdd", [128, DI], BF16, p3)
    P_s = B.sb("P_s", [128, DI], F32, p3)
    Ptmp = B.sb("Ptmp", [128, DI], F32, p3)
    ab_s = B.sb("ab_s", [128, NH], F32, p3)
    alx_s = B.sb("alx_s", [128, NH], F32, p3)
    dec_s = B.sb("dec_s", [128, NH], F32, p3)
    spsA = [B.ps(f"spsA{i}", [128, 1024], F32, p3) for i in range(2)]

    ev_p = B.ev(ve.memset(P_s[:, :], 0.0), "ve")
    ev_xuse = [None, None]
    ev_buse = [None, None]
    sps_last = [None, None]
    ev_xdduse = None
    ev_abuse = None
    ev_alxprev = None
    for c in range(NC_CH):
        par = c % 2
        if c == 0:
            B.wait(sy, "sy", ("cv0", B.cnt.get("cv0", 0)), ("cv1", B.cnt.get("cv1", 0)))
        B.wait(sy, "sy", ev_xuse[par])
        ev_x = B.dev(sy.dma_start(out=xtmA[par][:, :], in_=xpart_h.ap()[:, c * CH:(c + 1) * CH],
                                  transpose=True), f"xa{par}")
        B.wait(sy, "sy", ev_buse[par])
        ev_b = B.dev(sy.dma_start(out=btm[par][:, :], in_=b_h.ap()[:, c * CH:(c + 1) * CH],
                                  transpose=True), f"ba{par}")
        src = bass.AP(csT_h, (c * 128 + CH - 1) * NH, [[0, 128], [1, NH]])
        B.wait(gp, "gp", ev_csTh, ev_abuse, ev_alxprev)
        ev_ab = B.dev(gp.dma_start(out=ab_s[:, :], in_=src), "ab")
        B.wait(sc, "sc", ev_ab, ev_p if c > 0 else None)
        ev_alx = B.ev(sc.activation(out=alx_s[:, :], in_=ab_s[:, :], func=AF.Exp), "sc")
        ev_alxprev = ev_alx
        B.wait(ve, "ve", ev_ab, ev_csT)
        ev_sub = B.ev(ve.tensor_sub(dec_s[:, :], ab_s[:, :], csT_s[:, c * NH:(c + 1) * NH]), "ve")
        ev_abuse = ev_sub
        B.wait(sc, "sc", ev_sub)
        ev_dec = B.ev(sc.activation(out=dec_s[:, :], in_=dec_s[:, :], func=AF.Exp), "sc")
        dtT_ap = bass.AP(dtT_s, c * NH, [[NC_CH * NH, 128], [1, NH], [0, HD]])
        dec_ap = bass.AP(dec_s, 0, [[NH, 128], [1, NH], [0, HD]])
        B.wait(ve, "ve", ev_x, ev_dec, ev_xdduse)
        xdd3 = xdd[:].rearrange("p (h d) -> p h d", d=HD)
        xtm3 = xtmA[par][:].rearrange("p (h d) -> p h d", d=HD)
        ve.tensor_tensor(xdd3, xtm3, dtT_ap, ALU.mult)
        ev_xdd = B.ev(ve.tensor_tensor(xdd3, xdd3, dec_ap, ALU.mult), "ve")
        ev_xuse[par] = ev_xdd
        B.wait(gp, "gp", ev_p)
        ev_spill = B.dev(gp.dma_start(out=p_h.ap()[c], in_=P_s[:, :]), "sp")
        for sb_i in range(4):
            spar = sb_i % 2
            B.wait(te, "te", ev_xdd, ev_b, sps_last[spar])
            for hh in range(16):
                h = sb_i * 16 + hh
                g = h // 8
                ins = te.matmul(spsA[spar][:, hh * HD:(hh + 1) * HD],
                                btm[par][:, g * DS:(g + 1) * DS],
                                xdd[:, h * HD:(h + 1) * HD], start=True, stop=True)
            ev_mm = B.ev(ins, "te")
            ev_buse[par] = ev_mm
            ev_xdduse = ev_mm
            sl = slice(sb_i * 1024, (sb_i + 1) * 1024)
            alx_ap = bass.AP(alx_s, sb_i * 16, [[NH, 128], [1, 16], [0, HD]])
            B.wait(ve, "ve", ev_mm, ev_alx, ev_spill)
            ve.tensor_tensor(Ptmp[:].rearrange("p (h d) -> p h d", d=HD)[:, sb_i * 16:(sb_i + 1) * 16, :],
                             P_s[:].rearrange("p (h d) -> p h d", d=HD)[:, sb_i * 16:(sb_i + 1) * 16, :],
                             alx_ap, ALU.mult)
            ins = ve.tensor_add(P_s[:, sl], Ptmp[:, sl], spsA[spar][:, :])
            sps_last[spar] = ev_p = B.ev(ins, "ve")

    ev_msk = B.ev(ve.tensor_scalar_mul(Ptmp[:, :], P_s[:, :], smask_s[:, :]), "ve")
    B.wait(gp, "gp", ev_msk)
    ev_in = B.dev(gp.dma_start(out=ccin.ap(), in_=Ptmp[:, :]), "cci")
    B.wait(gp, "gp", ev_in)
    ev_cc = B.ev(gp.collective_compute(
        "AllReduce", ALU.add, replica_groups=[[0, 1], [2, 3], [4, 5], [6, 7]],
        ins=[ccin.ap().opt()], outs=[ccout.ap().opt()]), "cc")
    B.barrier()
    p3.close()
    if phases <= 3:
        B.wait_all_dma(sy)
        B.wait(sy, "sy", ev_cc)
        return nc, B

    p4 = contextlib.ExitStack()
    xtmB = [B.sb(f"xtmB{i}", [128, DI], BF16, p4) for i in range(2)]
    pcb = [B.sb(f"pcb{i}", [128, DI], BF16, p4) for i in range(2)]
    bcs = [B.sb(f"bcs{i}", [128, NG * CH], BF16, p4) for i in range(2)]
    ccs = [B.sb(f"ccs{i}", [128, NG * CH], BF16, p4) for i in range(2)]
    bigb = [B.sb(f"bigb{i}", [128, 32 * CH], F32, p4) for i in range(2)]
    rhs2 = B.sb("rhs2", [128, DI], BF16, p4)
    xdt = B.sb("xdt", [128, DI], BF16, p4)
    p0b = B.sb("p0b", [128, DI], BF16, p4)
    gcb_s = B.sb("gcb_s", [128, NH], F32, p4)
    dcx_s = B.sb("dcx_s", [128, NH], F32, p4)
    ych = [B.sb(f"ych{i}", [128, DI], BF16, p4) for i in range(2)]
    m_s = [B.sb(f"m_s{i}", [128, CH], BF16, p4) for i in range(2)]
    gps = [B.ps(f"gps{i}", [128, CH], F32, p4) for i in range(2)]
    ydp = [B.ps(f"ydp{i}", [128, HD], F32, p4) for i in range(2)]
    yop = [B.ps(f"yop{i}", [128, HD], F32, p4) for i in range(2)]

    B.wait(sy, "sy", ev_cc, ("sp", B.cnt.get("sp", 0)))
    ev_r = B.dev(sy.dma_start(out=p0b[:, :], in_=ccout.ap()), "p0r")
    B.wait(ve, "ve", ev_r)
    ev_p0 = B.ev(ve.tensor_scalar_mul(p0b[:, :], p0b[:, :], rmask_s[:, :]), "ve")

    ev_xuse = [None, None]
    ev_pcuse = [None, None]
    ev_bcuse = [None, None]
    ev_ccuse = [None, None]
    ev_bg_ve = [None, None]
    ev_bg_sc = [None, None]
    ev_yuse = [None, None]
    gps_last = [None, None]
    ydp_last = [None, None]
    yop_last = [None, None]
    m_last = [None, None]
    ev_r2use = None
    ev_xdtuse = None
    ev_gcbuse = None
    ev_r2prev = None

    for c in range(NC_CH):
        par = c % 2
        B.wait(sy, "sy", ev_xuse[par])
        ev_x = B.dev(sy.dma_start(out=xtmB[par][:, :], in_=xpart_h.ap()[:, c * CH:(c + 1) * CH],
                                  transpose=True), f"xb{par}")
        B.wait(sy, "sy", ev_pcuse[par])
        ev_pc = B.dev(sy.dma_start(out=pcb[par][:, :], in_=p_h.ap()[c]), f"pb{par}")
        B.wait(sy, "sy", ev_bcuse[par])
        ev_bc = B.dev(sy.dma_start(
            out=bcs[par][:].rearrange("p (g s) -> p g s", s=CH),
            in_=b_h.ap().rearrange("(g p) t -> p g t", p=128)[:, :, c * CH:(c + 1) * CH]), f"bk{par}")
        B.wait(sy, "sy", ev_ccuse[par])
        ev_cc2 = B.dev(sy.dma_start(
            out=ccs[par][:].rearrange("p (g s) -> p g s", s=CH),
            in_=c_h.ap().rearrange("(g p) t -> p g t", p=128)[:, :, c * CH:(c + 1) * CH]), f"ck{par}")
        src = bass.AP(gcs_h, c * NH, [[0, 128], [1, NH]])
        B.wait(gp, "gp", ev_gcsh if c == 0 else None, ev_gcbuse)
        ev_gc = B.dev(gp.dma_start(out=gcb_s[:, :], in_=src), "gcb")
        B.wait(sc, "sc", ev_gc, ev_r2prev)
        ev_dc = B.ev(sc.activation(out=dcx_s[:, :], in_=gcb_s[:, :], func=AF.Exp), "sc")
        ev_gcbuse = ev_dc
        dc_ap = bass.AP(dcx_s, 0, [[NH, 128], [1, NH], [0, HD]])
        B.wait(ve, "ve", ev_dc, ev_p0, ev_r2use)
        ev_r2 = B.ev(ve.tensor_tensor(rhs2[:].rearrange("p (h d) -> p h d", d=HD),
                                      p0b[:].rearrange("p (h d) -> p h d", d=HD), dc_ap, ALU.mult), "ve")
        ev_r2prev = ev_r2
        dtT_ap = bass.AP(dtT_s, c * NH, [[NC_CH * NH, 128], [1, NH], [0, HD]])
        B.wait(ve, "ve", ev_x, ev_xdtuse)
        ev_xd = B.ev(ve.tensor_tensor(xdt[:].rearrange("p (h d) -> p h d", d=HD),
                                      xtmB[par][:].rearrange("p (h d) -> p h d", d=HD),
                                      dtT_ap, ALU.mult), "ve")
        ev_bg = [None, None]
        for half in range(2):
            B.wait(gp, "gp", ev_bg_ve[half], ev_bg_sc[half])
            off = (half * 32) * CSROW + c * CH
            src1 = bass.AP(cs_h, off, [[0, 128], [CSROW, 32], [1, CH]])
            B.dev(gp.dma_start(out=bigb[half][:].rearrange("p (h l) -> p h l", l=CH),
                               in_=src1), f"bg{half}")
            srcp = bass.AP(t["pen"], 0, [[128, 128], [0, 32], [1, CH]])
            B.dev(gp.dma_start(out=bigb[half][:].rearrange("p (h l) -> p h l", l=CH),
                               in_=srcp, accum_op=ALU.add), f"bg{half}")
            ev_bg[half] = B.dev(gp.dma_start(out=flush_h.ap()[:, half:half + 1],
                                             in_=bigb[half][:, 0:1],
                                             allow_slow_non_contiguous=True), f"bg{half}")
        ev_sc_mm = None
        ev_y = None
        for h in range(NH):
            g = h // 8
            hh = h % 32
            half = h // 32
            gpar = g % 2
            hpar = h % 2
            if h % 8 == 0:
                B.wait(te, "te", ev_bc, ev_cc2, gps_last[gpar])
                ev_sc_mm = B.ev(te.matmul(gps[gpar][:, :], bcs[par][:, g * CH:(g + 1) * CH],
                                          ccs[par][:, g * CH:(g + 1) * CH], start=True, stop=True), "te")
                ev_bcuse[par] = ev_sc_mm
            B.wait(sc, "sc", ev_bg[half], ev_csT if c == 0 else None)
            ev_exp = B.ev(sc.activation(out=bigb[half][:, hh * CH:(hh + 1) * CH],
                                        in_=bigb[half][:, hh * CH:(hh + 1) * CH],
                                        func=AF.Exp, bias=negcsT_s[:, c * NH + h: c * NH + h + 1]), "sc")
            ev_bg_sc[half] = ev_exp
            B.wait(ve, "ve", ev_sc_mm, ev_exp, m_last[hpar])
            ev_m = B.ev(ve.tensor_tensor(m_s[hpar][:, :], gps[gpar][:, :],
                                         bigb[half][:, hh * CH:(hh + 1) * CH], ALU.mult), "ve")
            ev_bg_ve[half] = ev_m
            if h % 8 == 7:
                gps_last[gpar] = ev_m
            B.wait(te, "te", ev_pc, ev_r2, yop_last[hpar])
            te.matmul(yop[hpar][:, :], ccs[par][:, g * CH:(g + 1) * CH],
                      pcb[par][:, h * HD:(h + 1) * HD], start=True, stop=False)
            ev_yo = B.ev(te.matmul(yop[hpar][:, :], ccs[par][:, g * CH:(g + 1) * CH],
                                   rhs2[:, h * HD:(h + 1) * HD], start=False, stop=False), "te")
            ev_pcuse[par] = ev_yo
            ev_ccuse[par] = ev_yo
            ev_r2use = ev_yo
            B.wait(ve, "ve", ev_yo)
            ev_sc_y = B.ev(ve.tensor_scalar_mul(yop[hpar][:, :], yop[hpar][:, :],
                                                expcsT_s[:, c * NH + h: c * NH + h + 1]), "ve")
            B.wait(te, "te", ev_m, ev_xd, ev_sc_y)
            ev_yd = B.ev(te.matmul(yop[hpar][:, :], m_s[hpar][:, :],
                                   xdt[:, h * HD:(h + 1) * HD], start=False, stop=True), "te")
            ev_xdtuse = ev_yd
            m_last[hpar] = ev_yd
            B.wait(ve, "ve", ev_yd, ev_yuse[par] if h == 0 else None)
            ins = ve.scalar_tensor_tensor(ych[par][:, h * HD:(h + 1) * HD],
                                          xtmB[par][:, h * HD:(h + 1) * HD],
                                          drep_s[:, h:h + 1],
                                          yop[hpar][:, :], ALU.mult, ALU.add)
            ev_y = B.ev(ins, "ve")
            ydp_last[hpar] = ev_y
            yop_last[hpar] = ev_y
        ev_xuse[par] = ev_y
        B.wait(sy, "sy", ev_y)
        ev_yuse[par] = B.dev(sy.dma_start(out=y_h.ap()[c * CH:(c + 1) * CH, :], in_=ych[par][:, :]), f"yo{par}")
        if "bigb_tap" in B.taps and c == 7:
            bigb_tap = B.dram("bigb_tap", [128, NH * CH], F32, tap=True)
            B.wait(sy, "sy", B.now("ve"))
            B.dev(sy.dma_start(out=bigb_tap.ap()[:, 0:32 * CH], in_=bigb[0][:, :]), "tap")
            B.dev(sy.dma_start(out=bigb_tap.ap()[:, 32 * CH:], in_=bigb[1][:, :]), "tap")
        if "r2_tap" in B.taps and c == 7:
            r2_tap = B.dram("r2_tap", [128, DI], BF16, tap=True)
            p0_tap = B.dram("p0_tap", [128, DI], BF16, tap=True)
            dc_tap = B.dram("dc_tap", [128, NH], F32, tap=True)
            B.wait(sy, "sy", B.now("ve"))
            B.dev(sy.dma_start(out=r2_tap.ap(), in_=rhs2[:, :]), "tap")
            B.dev(sy.dma_start(out=p0_tap.ap(), in_=p0b[:, :]), "tap")
            B.dev(sy.dma_start(out=dc_tap.ap(), in_=dcx_s[:, :]), "tap")

    B.barrier()
    p4.close()
    if phases <= 4:
        B.wait_all_dma(sy)
        return nc, B

    p5 = contextlib.ExitStack()
    p5ps = contextlib.ExitStack()
    yfm = B.sb("yfm", [128, 32 * T], BF16, p5)
    zw = [B.sb(f"zw{i}", [128, 16 * 128], BF16, p5ps) for i in range(2)]
    zs = [B.sb(f"zs{i}", [128, T], BF16, p5ps) for i in range(2)]
    sqt = B.sb("sqt", [128, T], BF16, p5ps)
    v_s = B.sb("v_s", [1, T], F32, p5ps)
    rstd8 = B.sb("rstd8", [NG, T], F32, p5ps)
    rb = [B.sb(f"rb{i}", [128, T], F32, p5ps) for i in range(2)]
    zp = [B.ps(f"zp{i}", [128, 512], F32, p5ps) for i in range(2)]
    stp = B.ps("stp", [1, 1024], F32, p5ps)

    zw_use = [None, None]
    zs_use = [None, None]
    zp_last = [None, None]
    rb_use = [None, None]
    rbg_use = [None, None]
    ev_sqtuse = None
    ev_gate = {}
    ev_norm = {}
    ev_ytgate = {}
    for zc in range(32):
        par = zc % 2
        if zc == 0:
            B.wait(sy, "sy", ("yo0", B.cnt.get("yo0", 0)), ("yo1", B.cnt.get("yo1", 0)))
        B.wait(sy, "sy", ev_ytgate.get(zc - 2))
        ev_yt = B.dev(sy.dma_start(out=yfm[:, zc * T:(zc + 1) * T],
                                   in_=y_h.ap()[:, zc * 128:(zc + 1) * 128], transpose=True), f"yt{par}")
        src = bass.AP(t["winT"], 6208 + zc * 128, [[10304, 128], [128 * 10304, 16], [1, 128]])
        B.wait(gp, "gp", zw_use[par])
        ev_w = B.dev(gp.dma_start(out=zw[par][:].rearrange("p (k f) -> p k f", f=128), in_=src), f"zw{par}")
        B.wait(te, "te", ev_w)
        ev_sl = None
        for nt in range(2):
            B.wait(te, "te", zp_last[nt])
            for k in range(16):
                ins = te.matmul(zp[nt][:, :], zw[par][:, k * 128:(k + 1) * 128],
                                xfm_s[:, k * TCOL + 3 + nt * 512: k * TCOL + 3 + (nt + 1) * 512],
                                start=(k == 0), stop=(k == 15))
            ev_mm = B.ev(ins, "te")
            if nt == 1:
                zw_use[par] = ev_mm
            B.wait(sc, "sc", ev_mm, zs_use[par] if nt == 0 else None)
            ev_sl = B.ev(sc.activation(out=zs[par][:, nt * 512:(nt + 1) * 512],
                                       in_=zp[nt][:, :], func=AF.Silu), "sc")
            zp_last[nt] = ev_sl
        B.wait(ve, "ve", ev_sl, ev_yt)
        ev_g = B.ev(ve.tensor_mul(yfm[:, zc * T:(zc + 1) * T], yfm[:, zc * T:(zc + 1) * T],
                                  zs[par][:, :]), "ve")
        ev_ytgate[zc] = ev_g
        zs_use[par] = ev_g
        ev_gate[zc] = ev_g
        B.wait(sc, "sc", ev_g, ev_sqtuse)
        ev_sq = B.ev(sc.activation(out=sqt[:, :], in_=yfm[:, zc * T:(zc + 1) * T], func=AF.Square), "sc")
        B.wait(te, "te", ev_sq)
        for nt in range(2):
            ins = te.matmul(stp[:, nt * 512:(nt + 1) * 512], ones_s[:, :],
                            sqt[:, nt * 512:(nt + 1) * 512], start=(zc % 4 == 0), stop=(zc % 4 == 3),
                            skip_group_check=True)
        ev_st = B.ev(ins, "te")
        ev_sqtuse = ev_st
        if zc % 4 == 3:
            gidx = zc // 4
            gpar = gidx % 2
            B.wait(ve, "ve", ev_st, rbg_use[gpar])
            ev_v = B.ev(ve.tensor_scalar(v_s[:, :], stp[:, :], 1.0 / 512, EPS, ALU.mult, ALU.add), "ve")
            B.wait(sc, "sc", ev_v)
            sc.activation(out=v_s[:, :], in_=v_s[:, :], func=AF.Ln)
            ev_rs = B.ev(sc.activation(out=v_s[:, :], in_=v_s[:, :],
                                       func=AF.Exp, scale=-0.5), "sc")
            B.wait(sy, "sy", ev_rs, rbg_use[gpar])
            ev_rh = B.dev(sy.dma_start(out=rstd_h.ap()[gidx:gidx + 1, :],
                                       in_=v_s[:, :]), f"rs{gpar}")
            B.wait(gp, "gp", ev_rh, rb_use[gpar])
            ev_rb = B.dev(gp.dma_start(out=rb[gpar][:, :],
                                       in_=bass.AP(rstd_h, gidx * T, [[0, 128], [1, T]])), f"rb{gpar}")
            rbg_use[gpar] = ev_rb
            for zc2 in range(gidx * 4, gidx * 4 + 4):
                B.wait(ve, "ve", ev_rb, ev_gate[zc2], ev_consts if gidx == 0 else None)
                ins = ve.scalar_tensor_tensor(yfm[:, zc2 * T:(zc2 + 1) * T], yfm[:, zc2 * T:(zc2 + 1) * T],
                                              gcol_s[:, zc2:zc2 + 1], rb[gpar][:, :], ALU.mult, ALU.mult)
                ev_norm[zc2] = B.ev(ins, "ve")
            rb_use[gpar] = ev_norm[gidx * 4 + 3]
    if "yn_h" in B.taps:
        B.wait(sy, "sy", B.now("ve"))
        for zc in range(32):
            B.dev(sy.dma_start(out=yn_h.ap()[zc * 128:(zc + 1) * 128, :],
                               in_=yfm[:, zc * T:(zc + 1) * T]), "tap")
    B.barrier()
    p5ps.close()
    if phases <= 5:
        B.wait_all_dma(sy)
        return nc, B

    p6 = contextlib.ExitStack()
    r1 = B.sb("r1", [128, 8 * DM], F32, p6)
    wo = [B.sb(f"wo{i}", [128, 512], BF16, p6) for i in range(2)]
    xrs = [B.sb(f"xrs{i}", [128, 512], F32, p6) for i in range(2)]
    mcol = B.sb("mcol", [128, 1], F32, p6)
    scol = B.sb("scol", [128, 1], F32, p6)
    vcol = B.sb("vcol", [128, 1], F32, p6)
    tcol = B.sb("tcol", [128, 1], F32, p6)
    snk = B.sb("snk", [128, DM], BF16, p6)
    x1t = B.sb("x1t", [128, DM], BF16, p6)
    x1lo = B.sb("x1lo", [128, DM], BF16, p6)
    op_ps = [B.ps(f"op_ps{i}", [128, 512], F32, p6) for i in range(8)]

    wo_use = [None, None]
    xrs_use = [None, None]
    ev_r1 = {}
    for dmq in range(4):
        ev_mm = None
        for chx in range(32):
            par = chx % 2
            src = bass.AP(t["woutT"], chx * 128 * DM + dmq * 512, [[DM, 128], [1, 512]])
            B.wait(gp, "gp", wo_use[par])
            ev_w = B.dev(gp.dma_start(out=wo[par][:, :], in_=src), f"wo{par}")
            B.wait(te, "te", ev_w)
            for tt in range(8):
                if chx == 0:
                    B.wait(te, "te", ev_r1.get((dmq - 1, tt)), ev_norm.get(31) if dmq == 0 else None)
                ins = te.matmul(op_ps[tt][:, :], yfm[:, chx * T + tt * 128: chx * T + (tt + 1) * 128],
                                wo[par][:, :], start=(chx == 0), stop=(chx == 31))
            ev_chx = B.ev(ins, "te")
            wo_use[par] = ev_chx
            if chx == 31:
                ev_mm = ev_chx
        for tt in range(8):
            par = tt % 2
            B.wait(sy, "sy", xrs_use[par])
            ev_xr = B.dev(sy.dma_start(out=xrs[par][:, :],
                                       in_=t["xres"].ap()[tt * 128:(tt + 1) * 128,
                                                          dmq * 512:(dmq + 1) * 512]), f"xr{par}")
            B.wait(ve, "ve", ev_mm, ev_xr)
            ev = B.ev(ve.tensor_add(r1[:, tt * DM + dmq * 512: tt * DM + (dmq + 1) * 512],
                                    op_ps[tt][:, :], xrs[par][:, :]), "ve")
            ev_r1[(dmq, tt)] = ev
            xrs_use[par] = ev
    x1t_use = None
    x1lo_use = None
    ev_colsuse = None
    for tt in range(8):
        r1t = r1[:, tt * DM:(tt + 1) * DM]
        B.wait(sc, "sc", ev_r1[(3, tt)], ev_colsuse)
        sc.activation(out=snk[:, :], in_=r1t, func=AF.Copy, accum_out=mcol[:, :])
        sc.activation(out=snk[:, :], in_=r1t, func=AF.Square, accum_out=scol[:, :])
        sc.mul(mcol[:, :], mcol[:, :], 1.0 / DM)
        sc.activation(out=vcol[:, :], in_=scol[:, :], func=AF.Identity, scale=1.0 / DM, bias=eps_s[:, :])
        sc.activation(out=tcol[:, :], in_=mcol[:, :], func=AF.Square)
        sc.activation(out=vcol[:, :], in_=tcol[:, :], func=AF.Identity, scale=-1.0, bias=vcol[:, :])
        sc.activation(out=vcol[:, :], in_=vcol[:, :], func=AF.Ln)
        ev_rs = B.ev(sc.activation(out=vcol[:, :], in_=vcol[:, :], func=AF.Exp, scale=-0.5), "sc")
        B.wait(ve, "ve", ev_rs, x1t_use)
        ev_x1 = B.ev(ve.tensor_scalar(x1t[:, :], r1t, mcol[:, :], vcol[:, :],
                                      ALU.subtract, ALU.mult), "ve")
        ev_x1f = B.ev(ve.tensor_scalar(r1t, r1t, mcol[:, :], vcol[:, :],
                                       ALU.subtract, ALU.mult), "ve")
        B.wait(ve, "ve", x1lo_use)
        ev_lo = B.ev(ve.tensor_sub(x1lo[:, :], r1t, x1t[:, :]), "ve")
        ev_colsuse = ev_lo
        B.wait(sy, "sy", ev_x1, ev_x1f, ev_lo)
        x1t_use = B.dev(sy.dma_start(out=x1_h.ap()[tt * 128:(tt + 1) * 128, :], in_=x1t[:, :]), "x1t")
        x1lo_use = B.dev(sy.dma_start(out=x1lo_h.ap()[tt * 128:(tt + 1) * 128, :], in_=x1lo[:, :]), "x1l")
        B.dev(sy.dma_start(out=x1f_h.ap()[tt * 128:(tt + 1) * 128, :], in_=r1t), "x1f")
    B.barrier()
    p6.close()
    p5.close()
    main_scope.close()
    if phases <= 6:
        B.wait_all_dma(sy)
        return nc, B

    p78 = contextlib.ExitStack()
    o8_s = B.sb("o8_s", [8, DM], F32, p78)
    p7 = contextlib.ExitStack()
    x1fm = B.sb("x1fm", [128, 16 * T], BF16, p7)
    x1lofm = B.sb("x1lofm", [128, 16 * T], BF16, p7)
    h1 = B.sb("h1", [128, 16 * T], BF16, p7)
    h2 = B.sb("h2", [128, 8 * DM], BF16, p7)
    wf = [B.sb(f"wf{i}", [128, 2048], BF16, p7) for i in range(2)]
    wrhl = B.sb("wrhl", [128, 16 * 16], BF16, p7)
    wrhi = B.sb("wrhi", [128, 16 * 8], BF16, p7)
    rt_sb = B.sb("rt_sb", [16, 1024], F32, p7)
    rt_sb2 = B.sb("rt_sb2", [8, 1024], F32, p7)
    lgt = B.sb("lgt", [128, 24], F32, p7)
    lsum = B.sb("lsum", [128, 8], F32, p7)
    e8 = B.sb("e8", [128, 8], F32, p7)
    msk8 = B.sb("msk8", [128, 8], F32, p7)
    tm8 = B.sb("tm8", [128, 8], F32, p7)
    m1c = B.sb("m1c", [128, 1], F32, p7)
    nm1c = B.sb("nm1c", [128, 1], F32, p7)
    m2c = B.sb("m2c", [128, 1], F32, p7)
    e2c = B.sb("e2c", [128, 1], F32, p7)
    rdc = B.sb("rdc", [128, 1], F32, p7)
    wgate = B.sb("wgate", [128, 8 * NC_CH], BF16, p7)
    gl_s = B.sb("gl_s", [8, DM], F32, p7)
    gs_s = B.sb("gs_s", [8, DM], F32, p7)
    gT = B.sb("gT", [128, 16 * 8], BF16, p7)

    ev_xf = {}
    ev_xlo = {}
    B.wait(sy, "sy", ("x1t", B.cnt.get("x1t", 0)), ("x1l", B.cnt.get("x1l", 0)))
    for k in range(16):
        ev_xf[k] = B.dev(sy.dma_start(out=x1fm[:, k * T:(k + 1) * T],
                                      in_=x1_h.ap()[:, k * 128:(k + 1) * 128], transpose=True), "xf")
        ev_xlo[k] = B.dev(sy.dma_start(out=x1lofm[:, k * T:(k + 1) * T],
                                       in_=x1lo_h.ap()[:, k * 128:(k + 1) * 128], transpose=True), "xf")
    B.wait(ve, "ve", ev_consts)
    wr3 = wr_s[:].rearrange("p (k e) -> p k e", e=8)
    wrhi3 = wrhi[:].rearrange("p (k e) -> p k e", e=8)
    wrhl3 = wrhl[:].rearrange("p (k e) -> p k e", e=16)
    B.vchain(ve.tensor_copy(wrhi3, wr3))
    ve.tensor_copy(wrhl3[:, :, 0:8], wrhi3)
    ev_wr = B.ev(ve.tensor_sub(wrhl3[:, :, 8:16], wr3, wrhi3), "ve")

    p7r = contextlib.ExitStack()
    rtA = B.ps("rtA", [16, 512], F32, p7r)
    rtB = B.ps("rtB", [8, 512], F32, p7r)
    trp = B.ps("trp", [128, 128], F32, p7r)
    ev_wguse = None
    for blk in range(2):
        B.wait(te, "te", ev_wr, ev_xf[15], ev_xlo[15])
        for k in range(16):
            te.matmul(rtA[:, :], wrhl[:, k * 16:(k + 1) * 16],
                      x1fm[:, k * T + blk * 512: k * T + (blk + 1) * 512],
                      start=(k == 0), stop=(k == 15))
        for k in range(16):
            ins = te.matmul(rtB[:, :], wrhi[:, k * 8:(k + 1) * 8],
                            x1lofm[:, k * T + blk * 512: k * T + (blk + 1) * 512],
                            start=(k == 0), stop=(k == 15))
        ev_rt = B.ev(ins, "te")
        B.wait(sc, "sc", ev_rt)
        sc.copy(rt_sb[0:16, blk * 512:(blk + 1) * 512], rtA[:, :])
        ev_cp = B.ev(sc.copy(rt_sb2[0:8, blk * 512:(blk + 1) * 512], rtB[:, :]), "sc")
        for q in range(4):
            tt = blk * 4 + q
            B.wait(te, "te", ev_cp, ev_wguse)
            te.transpose(trp[:, 0:16], rt_sb[0:16, blk * 512 + q * 128: blk * 512 + (q + 1) * 128],
                         ident_s[0:16, 0:16])
            ev_t1 = B.ev(te.transpose(trp[:, 16:24], rt_sb2[0:8, blk * 512 + q * 128: blk * 512 + (q + 1) * 128],
                                      ident_s[0:8, 0:8]), "te")
            B.wait(sc, "sc", ev_t1)
            ev_lg = B.ev(sc.copy(lgt[:, :], trp[:, 0:24]), "sc")
            B.wait(ve, "ve", ev_lg)
            B.vchain(ve.tensor_add(lsum[:, :], lgt[:, 0:8], lgt[:, 8:16]))
            B.vchain(ve.tensor_add(lsum[:, :], lsum[:, :], lgt[:, 16:24]))
            B.vchain(ve.reduce_max(m1c[:, :], lsum[:, :], axis=mybir.AxisListType.X))
            B.vchain(ve.tensor_scalar(tm8[:, :], lsum[:, :], m1c[:, :], -1e6, ALU.is_ge, ALU.mult))
            B.vchain(ve.tensor_add(tm8[:, :], tm8[:, :], lsum[:, :]))
            B.vchain(ve.reduce_max(m2c[:, :], tm8[:, :], axis=mybir.AxisListType.X))
            ev_ve1 = B.vchain(ve.tensor_scalar_mul(nm1c[:, :], m1c[:, :], -1.0))
            B.wait(sc, "sc", ev_ve1)
            sc.activation(out=e8[:, :], in_=lsum[:, :], func=AF.Exp, bias=nm1c[:, :])
            ev_e2 = B.ev(sc.activation(out=e2c[:, :], in_=m2c[:, :], func=AF.Exp, bias=nm1c[:, :]), "sc")
            B.wait(ve, "ve", ev_e2)
            B.vchain(ve.tensor_scalar_add(e2c[:, :], e2c[:, :], 1.0))
            B.vchain(ve.reciprocal(rdc[:, :], e2c[:, :]))
            B.vchain(ve.tensor_single_scalar(msk8[:, :], lsum[:, :], m2c[:, :], ALU.is_ge))
            sl = wgate[:, tt * 8:(tt + 1) * 8]
            B.vchain(ve.tensor_mul(msk8[:, :], e8[:, :], msk8[:, :]))
            ev_wg = B.ev(ve.tensor_scalar_mul(sl, msk8[:, :], rdc[:, :]), "ve")
            ev_wguse = ev_wg
    if "wg_h" in B.taps:
        B.wait(sy, "sy", B.now("ve"))
        B.dev(sy.dma_start(out=wg_h.ap(), in_=wgate[:, :]), "tap")
    B.barrier()
    p7r.close()

    p7a = contextlib.ExitStack()
    h1p = [B.ps(f"h1p{i}", [128, 512], F32, p7a) for i in range(2)]
    wf_use = [None, None]
    h1p_last = [None, None]
    for f1 in range(16):
        par = f1 % 2
        src = bass.AP(t["w1T"], f1 * 128, [[2048, 128], [128 * 2048, 16], [1, 128]])
        B.wait(gp, "gp", wf_use[par])
        ev_w = B.dev(gp.dma_start(out=wf[par][:].rearrange("p (k f) -> p k f", f=128), in_=src), f"wf{par}")
        B.wait(te, "te", ev_w)
        for nt in range(2):
            B.wait(te, "te", h1p_last[nt])
            for k in range(16):
                ins = te.matmul(h1p[nt][:, :], wf[par][:, k * 128:(k + 1) * 128],
                                x1fm[:, k * T + nt * 512: k * T + (nt + 1) * 512],
                                start=(k == 0), stop=(k == 15))
            ev_mm = B.ev(ins, "te")
            if nt == 1:
                wf_use[par] = ev_mm
            B.wait(sc, "sc", ev_mm)
            ev_sl = B.ev(sc.activation(out=h1[:, f1 * T + nt * 512: f1 * T + (nt + 1) * 512],
                                       in_=h1p[nt][:, :], func=AF.Silu), "sc")
            h1p_last[nt] = ev_sl
    ev_h1 = B.now("sc")
    B.barrier()
    p7a.close()

    p7b = contextlib.ExitStack()
    h2p = [B.ps(f"h2p{i}", [128, 512], F32, p7b) for i in range(8)]
    wf_use = [None, None]
    ev_h2ev = {}
    for f2q in range(4):
        ev_mm = None
        for k in range(16):
            par = k % 2
            src = bass.AP(t["w3T"], k * 128 * 2048 + f2q * 512, [[2048, 128], [1, 512]])
            B.wait(gp, "gp", wf_use[par])
            ev_w = B.dev(gp.dma_start(out=wf[par][:, 0:512], in_=src), f"wf{par}")
            B.wait(te, "te", ev_w)
            for tt in range(8):
                if k == 0:
                    B.wait(te, "te", ev_h2ev.get((f2q - 1, tt)))
                ins = te.matmul(h2p[tt][:, :], h1[:, k * T + tt * 128: k * T + (tt + 1) * 128],
                                wf[par][:, 0:512], start=(k == 0), stop=(k == 15))
            ev_k = B.ev(ins, "te")
            wf_use[par] = ev_k
            if k == 15:
                ev_mm = ev_k
        for tt in range(8):
            B.wait(sc, "sc", ev_mm)
            ev_h2ev[(f2q, tt)] = B.ev(sc.activation(
                out=h2[:, tt * DM + f2q * 512: tt * DM + (f2q + 1) * 512],
                in_=h2p[tt][:, :], func=AF.Silu), "sc")
    B.barrier()
    p7b.close()

    p7c = contextlib.ExitStack()
    gpp = B.ps("gpp", [8, DM], F32, p7c)
    for tt in range(8):
        for q in range(4):
            ins = te.matmul(gpp[:, q * 512:(q + 1) * 512], wgate[:, tt * 8:(tt + 1) * 8],
                            h2[:, tt * DM + q * 512: tt * DM + (q + 1) * 512],
                            start=(tt == 0), stop=(tt == 7), skip_group_check=True)
    ev_g = B.ev(ins, "te")
    B.wait(sc, "sc", ev_g)
    ev_gl = B.ev(sc.copy(gl_s[:, :], gpp[:, :]), "sc")
    B.wait(gp, "gp", ev_gl)
    ev_gin = B.dev(gp.dma_start(out=gin.ap(), in_=gl_s[:, :]), "gin")
    B.wait(gp, "gp", ev_gin)
    ev_gcc = B.ev(gp.collective_compute(
        "AllReduce", ALU.add, replica_groups=[[0, 1, 2, 3, 4, 5, 6, 7]],
        ins=[gin.ap().opt()], outs=[gout.ap().opt()]), "cc")
    B.wait(sy, "sy", ev_gcc)
    ev_gs = B.dev(sy.dma_start(out=gs_s[:, :], in_=gout.ap()), "gsr")
    if "g_h" in B.taps:
        B.wait(sy, "sy", ev_gs)
        B.dev(sy.dma_start(out=g_h.ap(), in_=gs_s[:, :]), "tap")
    B.barrier()
    p7c.close()

    p7d = contextlib.ExitStack()
    trp2 = B.ps("trp2", [128, 128], F32, p7d)
    o8p = B.ps("o8p", [8, DM], F32, p7d)
    ev_gT = None
    for k in range(16):
        B.wait(te, "te", ev_gs if k == 0 else None, ev_gT)
        ev_t = B.ev(te.transpose(trp2[:, 0:8], gs_s[0:8, k * 128:(k + 1) * 128],
                                 ident_s[0:8, 0:8]), "te")
        B.wait(ve, "ve", ev_t)
        ev_gT = B.ev(ve.tensor_copy(gT[:, k * 8:(k + 1) * 8], trp2[:, 0:8]), "ve")
    wf_use = [None, None]
    ev_mm = None
    for k in range(16):
        par = k % 2
        src = bass.AP(t["w2T"], k * 128 * 2048, [[2048, 128], [1, 2048]])
        B.wait(gp, "gp", wf_use[par])
        ev_w = B.dev(gp.dma_start(out=wf[par][:, :], in_=src), f"wf{par}")
        B.wait(te, "te", ev_w, ev_gT)
        for q in range(4):
            ins = te.matmul(o8p[:, q * 512:(q + 1) * 512], gT[:, k * 8:(k + 1) * 8],
                            wf[par][:, q * 512:(q + 1) * 512], start=(k == 0), stop=(k == 15),
                            skip_group_check=True)
        ev_mm = B.ev(ins, "te")
        wf_use[par] = ev_mm
    B.wait(ve, "ve", ev_mm)
    ev_o8 = B.ev(ve.tensor_scalar_mul(o8_s[:, :], o8p[:, :], omask_s[:, :]), "ve")
    if "o8_h" in B.taps:
        B.wait(sy, "sy", ev_o8)
        B.dev(sy.dma_start(out=o8_h.ap(), in_=o8_s[:, :]), "tap")
    B.barrier()
    p7d.close()
    p7.close()

    p8 = contextlib.ExitStack()
    xtf = [B.sb(f"xtf{i}", [128, DM], F32, p8) for i in range(2)]
    mcol = B.sb("mcol8", [128, 1], F32, p8)
    scol = B.sb("scol8", [128, 1], F32, p8)
    vcol = B.sb("vcol8", [128, 1], F32, p8)
    tcol = B.sb("tcol8", [128, 1], F32, p8)
    snk2 = B.sb("snk2", [128, DM], BF16, p8)
    xtf_use = [None, None]
    ev_colsuse = None
    for tt in range(8):
        par = tt % 2
        if tt == 0:
            B.wait(sy, "sy", ("x1f", B.cnt.get("x1f", 0)))
        B.wait(sy, "sy", xtf_use[par])
        ev_x = B.dev(sy.dma_start(out=xtf[par][:, :],
                                  in_=x1f_h.ap()[tt * 128:(tt + 1) * 128, :]), f"x8{par}")
        if tt == 0:
            B.wait(ve, "ve", ev_x, ev_o8)
            ev_x = B.ev(ve.tensor_add(xtf[par][0:8, :], xtf[par][0:8, :], o8_s[:, :]), "ve")
        B.wait(sc, "sc", ev_x, ev_colsuse)
        sc.activation(out=snk2[:, :], in_=xtf[par][:, :], func=AF.Copy, accum_out=mcol[:, :])
        sc.activation(out=snk2[:, :], in_=xtf[par][:, :], func=AF.Square, accum_out=scol[:, :])
        sc.mul(mcol[:, :], mcol[:, :], 1.0 / DM)
        sc.activation(out=vcol[:, :], in_=scol[:, :], func=AF.Identity, scale=1.0 / DM, bias=eps_s[:, :])
        sc.activation(out=tcol[:, :], in_=mcol[:, :], func=AF.Square)
        sc.activation(out=vcol[:, :], in_=tcol[:, :], func=AF.Identity, scale=-1.0, bias=vcol[:, :])
        sc.activation(out=vcol[:, :], in_=vcol[:, :], func=AF.Ln)
        ev_rs = B.ev(sc.activation(out=vcol[:, :], in_=vcol[:, :], func=AF.Exp, scale=-0.5), "sc")
        B.wait(ve, "ve", ev_rs)
        ev_o = B.ev(ve.tensor_scalar(xtf[par][:, :], xtf[par][:, :], mcol[:, :], vcol[:, :],
                                     ALU.subtract, ALU.mult), "ve")
        ev_colsuse = ev_o
        B.wait(sy, "sy", ev_o)
        xtf_use[par] = B.ev(sy.dma_start(out=out_ext.ap()[tt * 128:(tt + 1) * 128, :],
                                         in_=xtf[par][:, :]), "out", 16)
    sy.wait_ge(B.sems["out"], B.cnt["out"])
    B.wait_all_dma(sy)
    return nc, B


def prepare_inputs(inputs):
    x = np.asarray(inputs["L0_attn_hidden_states"], np.float32)
    W_in = np.asarray(inputs["L0_attn_W_in"], np.float32)
    conv_w = np.asarray(inputs["L0_attn_conv_w"], np.float32)
    conv_b = np.asarray(inputs["L0_attn_conv_b"], np.float32)
    A = np.asarray(inputs["L0_attn_A"], np.float32)
    D = np.asarray(inputs["L0_attn_D"], np.float32)
    dt_bias = np.asarray(inputs["L0_attn_dt_bias"], np.float32)
    norm_w = np.asarray(inputs["L0_attn_norm_weight"], np.float32)
    W_out = np.asarray(inputs["L0_attn_W_out"], np.float32)
    w1 = np.asarray(inputs["L0_ffn_w1"], np.float32)
    w3 = np.asarray(inputs["L0_ffn_w3"], np.float32)
    w2 = np.asarray(inputs["L0_ffn_w2"], np.float32)
    wr = np.asarray(inputs["L0_ffn_wr"], np.float32)

    perm = np.concatenate([np.arange(4096, 10304), np.arange(0, 4096)])
    winT = np.ascontiguousarray(W_in.T[:, perm])
    woutT = np.ascontiguousarray(W_out.T)
    w1T = np.ascontiguousarray(w1.T)
    w3T = np.ascontiguousarray(w3.T)
    w2T = np.ascontiguousarray(w2.T)
    wrT = np.ascontiguousarray(wr.T)
    convb = np.ascontiguousarray(conv_b.reshape(-1, 1))
    acol = np.ascontiguousarray(A.reshape(-1, 1))
    drep = np.ascontiguousarray(np.broadcast_to(D.reshape(1, -1), (128, NH)))
    dtbcol = np.ascontiguousarray(dt_bias.reshape(-1, 1))
    gcol = np.ascontiguousarray(norm_w.reshape(32, 128).T)
    ident = np.eye(128, dtype=np.float32)
    pen = np.tril(np.full((128, 128), PEN, np.float32), -1)
    onescol = np.ones((128, 1), np.float32)
    shared = dict(winT=winT, woutT=woutT, w1T=w1T, w3T=w3T, w2T=w2T, wrT=wrT,
                  convw=conv_w, convb=convb, acol=acol, drep=drep, dtbcol=dtbcol,
                  gcol=gcol, ident=ident, pen=pen, onescol=onescol)

    in_maps = []
    for core in range(NCORES):
        b, hh = divmod(core, 2)
        t0 = hh * T
        halo = (x[b, t0 - 3:t0].T if hh == 1 else np.zeros((DM, 3), np.float32))
        xfm = np.ascontiguousarray(np.concatenate([halo, x[b, t0:t0 + T].T], axis=1))
        m = dict(shared)
        m["xfm"] = xfm
        m["xres"] = np.ascontiguousarray(x[b, t0:t0 + T])
        m["sendmask"] = np.full((128, 1), 1.0 if hh == 0 else 0.0, np.float32)
        m["recvmask"] = np.full((128, 1), 1.0 if hh == 1 else 0.0, np.float32)
        m["ownermask"] = np.full((8, 1), 1.0 if core == 0 else 0.0, np.float32)
        in_maps.append(m)
    return in_maps


_BF16_KEYS = {"onescol"}


def _cast_in_maps(in_maps):
    import ml_dtypes
    out = []
    for m in in_maps:
        mm = {}
        for k, v in m.items():
            if k in _BF16_KEYS:
                mm[k] = np.ascontiguousarray(v.astype(ml_dtypes.bfloat16))
            else:
                mm[k] = np.ascontiguousarray(v)
        out.append(mm)
    return out


def kernel(**inputs):
    in_maps = _cast_in_maps(prepare_inputs(inputs))
    nc, _ = build_program(DEBUG_TAPS)
    res = run_bass_kernel_spmd(nc, in_maps, core_ids=list(range(NCORES)))
    out = np.empty((4, 2048, DM), np.float32)
    for core in range(NCORES):
        b, hh = divmod(core, 2)
        out[b, hh * T:(hh + 1) * T, :] = np.asarray(res.results[core]["out"])
    return out


# version 04
# speedup vs baseline: 1.0770x; 1.0770x over previous
import contextlib
import numpy as np
import concourse.bass as bass
import concourse.mybir as mybir
from concourse.bass_utils import run_bass_kernel_spmd

F32 = mybir.dt.float32
BF16 = mybir.dt.bfloat16
AF = mybir.ActivationFunctionType
ALU = mybir.AluOpType

NCORES = 8
T = 1024
CH = 128
NC_CH = T // CH
DM = 2048
DI = 4096
NH = 64
HD = 64
NG = 8
DS = 128
CONV_DIM = 6144
EPS = 1e-5
PEN = -30000.0
TCOL = T + 3
CSROW = T + 16

DEBUG_TAPS = ()


class Builder:
    def __init__(self, nc, taps=()):
        self.nc = nc
        self.taps = set(taps)
        self.stack = contextlib.ExitStack()
        self.sems = {}
        self.cnt = {}
        self.lastwait = {}
        for s in ("te", "ve", "sc", "cc", "out"):
            self.sems[s] = self.stack.enter_context(nc.semaphore(s))
            self.cnt[s] = 0
        self.t = {}

    def ev(self, instr, sem, n=1):
        self.cnt[sem] += n
        instr.then_inc(self.sems[sem], n)
        return (sem, self.cnt[sem])

    def now(self, sem):
        return (sem, self.cnt[sem])

    def wait(self, eng, engname, *evs):
        for e in evs:
            if e is None:
                continue
            sem, val = e
            if val <= 0:
                continue
            if self.lastwait.get((engname, sem), -1) >= val:
                continue
            eng.wait_ge(self.sems[sem], val)
            self.lastwait[(engname, sem)] = val

    def dsem(self, name):
        if name not in self.sems:
            self.sems[name] = self.stack.enter_context(self.nc.semaphore(name))
            self.cnt[name] = 0
        return name

    def dev(self, ins, sem):
        self.dsem(sem)
        return self.ev(ins, sem, 16)

    def wait_all_dma(self, eng, engname="sy"):
        for s, v in list(self.cnt.items()):
            if s in ("te", "ve", "sc", "gp"):
                continue
            self.wait(eng, engname, (s, v))

    def vchain(self, ins):
        e = self.ev(ins, "ve")
        self.nc.vector.wait_ge(self.sems["ve"], e[1])
        self.lastwait[("ve", "ve")] = e[1]
        return e

    def barrier(self):
        self.nc.all_engine_barrier()
        for s in ("te", "ve", "sc"):
            for e in ("te", "ve", "sc", "gp", "sy"):
                self.lastwait[(e, s)] = self.cnt[s]

    def sb(self, name, shape, dtype, scope=None):
        t = (scope or self.stack).enter_context(self.nc.sbuf_tensor(name, shape, dtype))
        self.t[name] = t
        return t

    def ps(self, name, shape, dtype, scope=None):
        t = (scope or self.stack).enter_context(self.nc.psum_tensor(name, shape, dtype))
        self.t[name] = t
        return t

    def dram(self, name, shape, dtype, tap=False):
        kind = "ExternalOutput" if (tap and name in self.taps) else "Internal"
        t = self.nc.dram_tensor(name, shape, dtype, kind=kind)
        self.t[name] = t
        return t


def build_program(taps=(), phases=99):
    nc = bass.Bass("TRN2", target_bir_lowering=False)
    B = Builder(nc, taps)
    t = B.t
    te, ve, sc, gp, sy = nc.tensor, nc.vector, nc.scalar, nc.gpsimd, nc.sync

    inputs = {
        "xfm": ([DM, TCOL], F32),
        "xres": ([T, DM], F32),
        "winT": ([DM, 10304], F32),
        "woutT": ([DI, DM], F32),
        "w1T": ([DM, DM], F32),
        "w3T": ([DM, DM], F32),
        "w2T": ([DM, DM], F32),
        "wrT": ([DM, 8], F32),
        "convw": ([CONV_DIM, 4], F32),
        "convb": ([CONV_DIM, 1], F32),
        "acol": ([NH, 1], F32),
        "drep": ([128, NH], F32),
        "dtbcol": ([NH, 1], F32),
        "gcol": ([128, 32], F32),
        "ident": ([128, 128], F32),
        "pen": ([128, 128], F32),
        "onescol": ([128, 1], BF16),
        "sendmask": ([128, 1], F32),
        "recvmask": ([128, 1], F32),
        "ownermask": ([8, 1], F32),
    }
    for nm, (shp, dt_) in inputs.items():
        t[nm] = nc.dram_tensor(nm, shp, dt_, kind="ExternalInput")
    out_ext = nc.dram_tensor("out", [T, DM], F32, kind="ExternalOutput")

    xpart_h = B.dram("xpart_h", [DI, T], BF16, tap=True)
    b_h = B.dram("b_h", [NG * DS, T], BF16, tap=True)
    c_h = B.dram("c_h", [NG * DS, T], BF16, tap=True)
    cs_h = B.dram("cs_h", [NH, CSROW], F32, tap=True)
    csT_h = B.dram("csT_h", [NC_CH, 128, NH], F32)
    gcs_h = B.dram("gcs_h", [NC_CH, NH], F32, tap=True)
    p_h = B.dram("p_h", [NC_CH, 128, DI], BF16, tap=True)
    y_h = B.dram("y_h", [T, DI], BF16, tap=True)
    yn_h = B.dram("yn_h", [DI, T], BF16, tap=True)
    x1_h = B.dram("x1_h", [T, DM], BF16, tap=True)
    x1lo_h = B.dram("x1lo_h", [T, DM], BF16, tap=True)
    x1f_h = B.dram("x1f_h", [T, DM], F32, tap=True)
    rstd_h = B.dram("rstd_h", [NG, T], F32, tap=True)
    wg_h = B.dram("wg_h", [128, 8 * NC_CH], BF16, tap=True)
    g_h = B.dram("g_h", [8, DM], F32, tap=True)
    o8_h = B.dram("o8_h", [8, DM], F32, tap=True)
    h1_h = B.dram("h1_h", [DM, T], BF16, tap=True)
    h2_h = B.dram("h2_h", [T, DM], BF16, tap=True)
    flush_h = B.dram("flush_h", [128, 4], F32)
    ccin = B.dram("ccin", [128, DI], BF16)
    ccout = B.dram("ccout", [128, DI], BF16)
    gin = B.dram("gin", [8, DM], F32)
    gout = B.dram("gout", [8, DM], F32)

    dt_s = B.sb("dt_s", [NH, T], F32)
    adt_s = B.sb("adt_s", [NH, T], F32)
    cs_s = B.sb("cs_s", [NH, CSROW], F32)
    dtT_s = B.sb("dtT_s", [128, NC_CH * NH], BF16)
    csT_s = B.sb("csT_s", [128, NC_CH * NH], F32)
    negcsT_s = B.sb("negcsT_s", [128, NC_CH * NH], F32)
    expcsT_s = B.sb("expcsT_s", [128, NC_CH * NH], F32)
    ident_s = B.sb("ident_s", [128, 128], F32)
    convw_s = B.sb("convw_s", [128, 48 * 4], F32)
    convb_s = B.sb("convb_s", [128, 48], F32)
    acol_s = B.sb("acol_s", [NH, 1], F32)
    drep_s = B.sb("drep_s", [128, NH], F32)
    dtb_s = B.sb("dtb_s", [NH, 1], F32)
    gcol_s = B.sb("gcol_s", [128, 32], F32)
    ones_s = B.sb("ones_s", [128, 1], BF16)
    smask_s = B.sb("smask_s", [128, 1], F32)
    rmask_s = B.sb("rmask_s", [128, 1], F32)
    omask_s = B.sb("omask_s", [8, 1], F32)
    wr_s = B.sb("wr_s", [128, 16 * 8], F32)
    eps_s = B.sb("eps_s", [128, 1], F32)
    main_scope = contextlib.ExitStack()
    xfm_s = B.sb("xfm_s", [128, 16 * TCOL], BF16, main_scope)

    for nm, dst in (("ident", ident_s), ("acol", acol_s), ("drep", drep_s),
                    ("dtbcol", dtb_s), ("gcol", gcol_s), ("onescol", ones_s),
                    ("sendmask", smask_s), ("recvmask", rmask_s), ("ownermask", omask_s)):
        B.dev(sy.dma_start(out=dst[:, :], in_=t[nm].ap()), "cst")
    B.dev(sy.dma_start(out=convb_s[:, :],
                      in_=t["convb"].ap().rearrange("(c p) o -> p (c o)", p=128),
                      allow_slow_non_contiguous=True), "cst")
    B.dev(sy.dma_start(out=convw_s[:].rearrange("p (c k) -> p c k", k=4),
                      in_=t["convw"].ap().rearrange("(c p) k -> p c k", p=128)), "cst")
    B.dev(sy.dma_start(out=wr_s[:].rearrange("p (k e) -> p k e", e=8),
                      in_=t["wrT"].ap().rearrange("(k p) e -> p k e", p=128)), "cst")
    ev_consts = B.now("cst")
    B.ev(nc.vector.memset(eps_s[:, :], EPS), "ve")
    ev_xfm = B.dev(gp.dma_start(
        out=xfm_s[:].rearrange("p (k c) -> p k c", c=TCOL),
        in_=t["xfm"].ap().rearrange("(k p) c -> p k c", p=128)), "xfm")

    p1 = contextlib.ExitStack()
    wbuf = [B.sb(f"wbuf{i}", [128, 16 * 128], BF16, p1) for i in range(2)]
    cvb = [B.sb(f"cvb{i}", [128, TCOL], BF16, p1) for i in range(2)]
    cvo = [B.sb(f"cvo{i}", [128, T], BF16, p1) for i in range(2)]
    dtraw_s = B.sb("dtraw_s", [NH, TCOL], F32, p1)
    dttmp_s = B.sb("dttmp_s", [NH, T], F32, p1)
    pp = [B.ps(f"pp{i}", [128, 512], F32, p1) for i in range(2)]

    NT = ((0, 343), (343, 342), (685, 342))
    wlast_use = [None, None]
    cv_last_use = [None, None]
    cvo_last_use = [None, None]
    pp_last = [None, None]

    B.wait(te, "te", ev_xfm)
    for fo in range(49):
        par = fo % 2
        cols = 128 if fo < 48 else 64
        src = bass.AP(t["winT"], fo * 128, [[10304, 128], [128 * 10304, 16], [1, cols]])
        B.wait(gp, "gp", wlast_use[par])
        ev_w = B.dev(gp.dma_start(
            out=wbuf[par][:].rearrange("p (k f) -> p k f", f=128)[:, :, 0:cols], in_=src), f"wb{par}")
        B.wait(te, "te", ev_w)
        for nt in range(3):
            c0, ncols = NT[nt]
            ppar = (fo * 3 + nt) % 2
            B.wait(te, "te", pp_last[ppar])
            for k in range(16):
                ins = te.matmul(pp[ppar][0:cols, 0:ncols],
                                wbuf[par][:, k * 128:k * 128 + cols],
                                xfm_s[:, k * TCOL + c0: k * TCOL + c0 + ncols],
                                start=(k == 0), stop=(k == 15))
            ev_mm = B.ev(ins, "te")
            if nt == 2:
                wlast_use[par] = ev_mm
            if fo < 48:
                B.wait(sc, "sc", ev_mm, cv_last_use[par] if nt == 0 else None)
                ev_ev = B.ev(sc.copy(cvb[par][:, c0:c0 + ncols], pp[ppar][:, 0:ncols]), "sc")
            else:
                B.wait(sc, "sc", ev_mm)
                ev_ev = B.ev(sc.copy(dtraw_s[:, c0:c0 + ncols], pp[ppar][0:64, 0:ncols]), "sc")
            pp_last[ppar] = ev_ev
        if fo < 48:
            B.wait(ve, "ve", ev_ev, cvo_last_use[par], ev_consts if fo == 0 else None)
            ve.tensor_scalar(cvo[par][:, :], cvb[par][:, 0:T],
                             convw_s[:, fo * 4: fo * 4 + 1],
                             convb_s[:, fo: fo + 1], ALU.mult, ALU.add)
            for k in range(1, 4):
                ins = ve.scalar_tensor_tensor(cvo[par][:, :], cvb[par][:, k:k + T],
                                              convw_s[:, fo * 4 + k: fo * 4 + k + 1],
                                              cvo[par][:, :], ALU.mult, ALU.add)
            ev_conv = B.ev(ins, "ve")
            cv_last_use[par] = ev_conv
            B.wait(sc, "sc", ev_conv)
            ev_silu = B.ev(sc.activation(out=cvo[par][:, :], in_=cvo[par][:, :], func=AF.Silu), "sc")
            B.wait(sy, "sy", ev_silu)
            if fo < 32:
                dst = xpart_h.ap()[fo * 128:(fo + 1) * 128, :]
            elif fo < 40:
                dst = b_h.ap()[(fo - 32) * 128:(fo - 31) * 128, :]
            else:
                dst = c_h.ap()[(fo - 40) * 128:(fo - 39) * 128, :]
            cvo_last_use[par] = B.dev(sy.dma_start(out=dst, in_=cvo[par][:, :]), f"cv{par}")
        else:
            B.wait(sc, "sc", ev_ev, ev_consts)
            sc.activation(out=dt_s[:, :], in_=dtraw_s[:, 3:TCOL], func=AF.Relu, bias=dtb_s[:, :])
            sc.activation(out=dttmp_s[:, :], in_=dtraw_s[:, 3:TCOL], func=AF.Abs, bias=dtb_s[:, :])
            sc.activation(out=dttmp_s[:, :], in_=dttmp_s[:, :], func=AF.Exp, scale=-1.0)
            ev_l = B.ev(sc.activation(out=dttmp_s[:, :], in_=dttmp_s[:, :], func=AF.Ln, bias=1.0), "sc")
            B.wait(ve, "ve", ev_l)
            B.ev(ve.tensor_add(dt_s[:, :], dt_s[:, :], dttmp_s[:, :]), "ve")

    if "dt_tap" in B.taps:
        dt_tap = B.dram("dt_tap", [NH, T], F32, tap=True)
        B.wait(sy, "sy", B.now("ve"))
        B.dev(sy.dma_start(out=dt_tap.ap(), in_=dt_s[:, :]), "tap")
    B.barrier()
    p1.close()
    if phases <= 1:
        B.wait_all_dma(sy)
        return nc, B

    p2 = contextlib.ExitStack()
    tps = B.ps("tps", [128, 128], F32, p2)
    g8_s = B.sb("g8_s", [NH, 8], F32, p2)
    g8b_s = B.sb("g8b_s", [NH, 8], F32, p2)
    dtTf_s = B.sb("dtTf_s", [128, NH], F32, p2)

    ve.tensor_scalar_mul(adt_s[:, :], dt_s[:, :], acol_s[:, :])
    for c in range(NC_CH):
        ve.tensor_tensor_scan(cs_s[:, c * CH:(c + 1) * CH], adt_s[:, c * CH:(c + 1) * CH],
                              adt_s[:, c * CH:(c + 1) * CH], 0.0, ALU.add, ALU.bypass)
    alast_ap = bass.AP(cs_s, CH - 1, [[CSROW, NH], [CH, NC_CH]])
    B.vchain(ve.tensor_copy(g8_s[:, :], alast_ap))
    ve.tensor_copy(g8b_s[:, 0:1], g8_s[:, 0:1])
    B.vchain(ve.tensor_add(g8b_s[:, 1:8], g8_s[:, 1:8], g8_s[:, 0:7]))
    ve.tensor_copy(g8_s[:, 0:2], g8b_s[:, 0:2])
    B.vchain(ve.tensor_add(g8_s[:, 2:8], g8b_s[:, 2:8], g8b_s[:, 0:6]))
    ve.tensor_copy(g8b_s[:, 0:4], g8_s[:, 0:4])
    B.vchain(ve.tensor_add(g8b_s[:, 4:8], g8_s[:, 4:8], g8_s[:, 0:4]))
    ve.memset(cs_s[:, T:T + 1], 0.0)
    ev_cs = B.ev(ve.tensor_copy(cs_s[:, T + 1:T + 8], g8b_s[:, 0:7]), "ve")
    B.wait(sy, "sy", ev_cs)
    ev_csh = B.dev(sy.dma_start(out=cs_h.ap(), in_=cs_s[:, :]), "csh")

    ev_evac = None
    for c in range(NC_CH):
        B.wait(te, "te", ev_cs if c == 0 else None, ev_evac)
        ev1 = B.ev(te.transpose(tps[:, 0:NH], cs_s[0:NH, c * CH:(c + 1) * CH], ident_s[0:NH, 0:NH]), "te")
        B.wait(sc, "sc", ev1)
        ev2 = B.ev(sc.copy(csT_s[:, c * NH:(c + 1) * NH], tps[:, 0:NH]), "sc")
        B.wait(te, "te", ev2)
        ev3 = B.ev(te.transpose(tps[:, 0:NH], dt_s[0:NH, c * CH:(c + 1) * CH], ident_s[0:NH, 0:NH]), "te")
        B.wait(sc, "sc", ev3)
        ev4 = B.ev(sc.copy(dtTf_s[:, 0:NH], tps[:, 0:NH]), "sc")
        ev_evac = ev4
        B.wait(ve, "ve", ev4)
        B.ev(ve.tensor_copy(dtT_s[:, c * NH:(c + 1) * NH], dtTf_s[:, 0:NH]), "ve")
    ev_csT_done = B.now("sc")
    B.wait(sc, "sc", ev_csT_done)
    sc.activation(out=expcsT_s[:, :], in_=csT_s[:, :], func=AF.Exp)
    B.ev(sc.mul(negcsT_s[:, :], csT_s[:, :], -1.0), "sc")
    B.wait(sy, "sy", ev_csT_done)
    ev_csTh = B.dev(sy.dma_start(out=csT_h.ap().rearrange("c s h -> s c h"),
                                 in_=csT_s[:].rearrange("s (c h) -> s c h", h=NH)), "cth")
    B.wait(te, "te", ev_cs, ev_evac)
    ev_gt = B.ev(te.transpose(tps[0:8, 0:NH], cs_s[0:NH, T:T + 8], ident_s[0:NH, 0:NH]), "te")
    B.wait(sc, "sc", ev_gt)
    ev_gc8 = B.ev(sc.copy(dtTf_s[0:8, 0:NH], tps[0:8, 0:NH]), "sc")
    B.wait(sy, "sy", ev_gc8)
    ev_gcsh = B.dev(sy.dma_start(out=gcs_h.ap(), in_=dtTf_s[0:8, 0:NH]), "gch")
    if "cs_tap" in B.taps:
        cs_tap = B.dram("cs_tap", [128, NC_CH * NH], F32, tap=True)
        B.wait(sy, "sy", B.now("sc"))
        B.dev(sy.dma_start(out=cs_tap.ap(), in_=csT_s[:, :]), "tap")
    B.barrier()
    p2.close()
    ev_csT = B.now("sc")
    if phases <= 2:
        B.wait_all_dma(sy)
        return nc, B

    p3 = contextlib.ExitStack()
    xtmA = [B.sb(f"xtmA{i}", [128, DI], BF16, p3) for i in range(2)]
    btm = [B.sb(f"btm{i}", [128, NG * DS], BF16, p3) for i in range(2)]
    xdd = B.sb("xdd", [128, DI], BF16, p3)
    P_s = B.sb("P_s", [128, DI], F32, p3)
    Ptmp = B.sb("Ptmp", [128, DI], F32, p3)
    ab_s = B.sb("ab_s", [128, NH], F32, p3)
    alx_s = B.sb("alx_s", [128, NH], F32, p3)
    dec_s = B.sb("dec_s", [128, NH], F32, p3)
    spsA = [B.ps(f"spsA{i}", [128, 1024], F32, p3) for i in range(2)]

    ev_p = B.ev(ve.memset(P_s[:, :], 0.0), "ve")
    ev_xuse = [None, None]
    ev_buse = [None, None]
    sps_last = [None, None]
    ev_xdduse = None
    ev_abuse = None
    ev_alxprev = None
    for c in range(NC_CH):
        par = c % 2
        if c == 0:
            B.wait(sy, "sy", ("cv0", B.cnt.get("cv0", 0)), ("cv1", B.cnt.get("cv1", 0)))
        B.wait(sy, "sy", ev_xuse[par])
        ev_x = B.dev(sy.dma_start(out=xtmA[par][:, :], in_=xpart_h.ap()[:, c * CH:(c + 1) * CH],
                                  transpose=True), f"xa{par}")
        B.wait(sy, "sy", ev_buse[par])
        ev_b = B.dev(sy.dma_start(out=btm[par][:, :], in_=b_h.ap()[:, c * CH:(c + 1) * CH],
                                  transpose=True), f"ba{par}")
        src = bass.AP(csT_h, (c * 128 + CH - 1) * NH, [[0, 128], [1, NH]])
        B.wait(gp, "gp", ev_csTh, ev_abuse, ev_alxprev)
        ev_ab = B.dev(gp.dma_start(out=ab_s[:, :], in_=src), "ab")
        B.wait(sc, "sc", ev_ab, ev_p if c > 0 else None)
        ev_alx = B.ev(sc.activation(out=alx_s[:, :], in_=ab_s[:, :], func=AF.Exp), "sc")
        ev_alxprev = ev_alx
        B.wait(ve, "ve", ev_ab, ev_csT)
        ev_sub = B.ev(ve.tensor_sub(dec_s[:, :], ab_s[:, :], csT_s[:, c * NH:(c + 1) * NH]), "ve")
        ev_abuse = ev_sub
        B.wait(sc, "sc", ev_sub)
        ev_dec = B.ev(sc.activation(out=dec_s[:, :], in_=dec_s[:, :], func=AF.Exp), "sc")
        dtT_ap = bass.AP(dtT_s, c * NH, [[NC_CH * NH, 128], [1, NH], [0, HD]])
        dec_ap = bass.AP(dec_s, 0, [[NH, 128], [1, NH], [0, HD]])
        B.wait(ve, "ve", ev_x, ev_dec, ev_xdduse)
        xdd3 = xdd[:].rearrange("p (h d) -> p h d", d=HD)
        xtm3 = xtmA[par][:].rearrange("p (h d) -> p h d", d=HD)
        ve.tensor_tensor(xdd3, xtm3, dtT_ap, ALU.mult)
        ev_xdd = B.ev(ve.tensor_tensor(xdd3, xdd3, dec_ap, ALU.mult), "ve")
        ev_xuse[par] = ev_xdd
        B.wait(gp, "gp", ev_p)
        ev_spill = B.dev(gp.dma_start(out=p_h.ap()[c], in_=P_s[:, :]), "sp")
        for sb_i in range(4):
            spar = sb_i % 2
            B.wait(te, "te", ev_xdd, ev_b, sps_last[spar])
            for hh in range(16):
                h = sb_i * 16 + hh
                g = h // 8
                ins = te.matmul(spsA[spar][:, hh * HD:(hh + 1) * HD],
                                btm[par][:, g * DS:(g + 1) * DS],
                                xdd[:, h * HD:(h + 1) * HD], start=True, stop=True)
            ev_mm = B.ev(ins, "te")
            ev_buse[par] = ev_mm
            ev_xdduse = ev_mm
            sl = slice(sb_i * 1024, (sb_i + 1) * 1024)
            alx_ap = bass.AP(alx_s, sb_i * 16, [[NH, 128], [1, 16], [0, HD]])
            B.wait(ve, "ve", ev_mm, ev_alx, ev_spill)
            ve.tensor_tensor(Ptmp[:].rearrange("p (h d) -> p h d", d=HD)[:, sb_i * 16:(sb_i + 1) * 16, :],
                             P_s[:].rearrange("p (h d) -> p h d", d=HD)[:, sb_i * 16:(sb_i + 1) * 16, :],
                             alx_ap, ALU.mult)
            ins = ve.tensor_add(P_s[:, sl], Ptmp[:, sl], spsA[spar][:, :])
            sps_last[spar] = ev_p = B.ev(ins, "ve")

    ev_msk = B.ev(ve.tensor_scalar_mul(Ptmp[:, :], P_s[:, :], smask_s[:, :]), "ve")
    B.wait(gp, "gp", ev_msk)
    ev_in = B.dev(gp.dma_start(out=ccin.ap(), in_=Ptmp[:, :]), "cci")
    B.wait(gp, "gp", ev_in)
    ev_cc = B.ev(gp.collective_compute(
        "AllReduce", ALU.add, replica_groups=[[0, 1], [2, 3], [4, 5], [6, 7]],
        ins=[ccin.ap().opt()], outs=[ccout.ap().opt()]), "cc")
    B.barrier()
    p3.close()
    if phases <= 3:
        B.wait_all_dma(sy)
        B.wait(sy, "sy", ev_cc)
        return nc, B

    p4 = contextlib.ExitStack()
    xtmB = [B.sb(f"xtmB{i}", [128, DI], BF16, p4) for i in range(2)]
    pcb = [B.sb(f"pcb{i}", [128, DI], BF16, p4) for i in range(2)]
    bcs = [B.sb(f"bcs{i}", [128, NG * CH], BF16, p4) for i in range(2)]
    ccs = [B.sb(f"ccs{i}", [128, NG * CH], BF16, p4) for i in range(2)]
    bigb = [B.sb(f"bigb{i}", [128, 32 * CH], F32, p4) for i in range(2)]
    rhs2 = B.sb("rhs2", [128, DI], BF16, p4)
    xdt = B.sb("xdt", [128, DI], BF16, p4)
    p0b = B.sb("p0b", [128, DI], BF16, p4)
    gcb_s = B.sb("gcb_s", [128, NH], F32, p4)
    dcx_s = B.sb("dcx_s", [128, NH], F32, p4)
    ych = [B.sb(f"ych{i}", [128, DI], BF16, p4) for i in range(2)]
    expb = [B.sb(f"expb{i}", [128, 32 * CH], BF16, p4) for i in range(2)]
    gb = [B.sb(f"gb{i}", [128, CH], BF16, p4) for i in range(2)]
    m8 = [B.sb(f"m8_{i}", [128, 8 * CH], BF16, p4) for i in range(2)]
    tmp8 = B.sb("tmp8", [128, 512], BF16, p4)
    gps = [B.ps(f"gps{i}", [128, CH], F32, p4) for i in range(2)]
    yps = [B.ps(f"yps{i}", [128, 512], F32, p4) for i in range(2)]

    B.wait(sy, "sy", ev_cc, ("sp", B.cnt.get("sp", 0)))
    ev_r = B.dev(sy.dma_start(out=p0b[:, :], in_=ccout.ap()), "p0r")
    B.wait(ve, "ve", ev_r)
    ev_p0 = B.ev(ve.tensor_scalar_mul(p0b[:, :], p0b[:, :], rmask_s[:, :]), "ve")

    ev_xuse = [None, None]
    ev_pcuse = [None, None]
    ev_bcuse = [None, None]
    ev_ccuse = [None, None]
    ev_bg_sc = [None, None]
    ev_expb_use = [None, None]
    ev_yuse = [None, None]
    gps_free = [None, None]
    gb_free = [None, None]
    m8_free = [None, None]
    yps_free = [None, None]
    ev_r2use = None
    ev_xdtuse = None
    ev_gcbuse = None
    ev_r2prev = None

    for c in range(NC_CH):
        par = c % 2
        B.wait(sy, "sy", ev_xuse[par])
        ev_x = B.dev(sy.dma_start(out=xtmB[par][:, :], in_=xpart_h.ap()[:, c * CH:(c + 1) * CH],
                                  transpose=True), f"xb{par}")
        B.wait(sy, "sy", ev_pcuse[par])
        ev_pc = B.dev(sy.dma_start(out=pcb[par][:, :], in_=p_h.ap()[c]), f"pb{par}")
        B.wait(sy, "sy", ev_bcuse[par])
        ev_bc = B.dev(sy.dma_start(
            out=bcs[par][:].rearrange("p (g s) -> p g s", s=CH),
            in_=b_h.ap().rearrange("(g p) t -> p g t", p=128)[:, :, c * CH:(c + 1) * CH]), f"bk{par}")
        B.wait(sy, "sy", ev_ccuse[par])
        ev_cc2 = B.dev(sy.dma_start(
            out=ccs[par][:].rearrange("p (g s) -> p g s", s=CH),
            in_=c_h.ap().rearrange("(g p) t -> p g t", p=128)[:, :, c * CH:(c + 1) * CH]), f"ck{par}")
        src = bass.AP(gcs_h, c * NH, [[0, 128], [1, NH]])
        B.wait(gp, "gp", ev_gcsh if c == 0 else None, ev_gcbuse)
        ev_gc = B.dev(gp.dma_start(out=gcb_s[:, :], in_=src), "gcb")
        B.wait(sc, "sc", ev_gc, ev_r2prev)
        ev_dc = B.ev(sc.activation(out=dcx_s[:, :], in_=gcb_s[:, :], func=AF.Exp), "sc")
        ev_gcbuse = ev_dc
        dc_ap = bass.AP(dcx_s, 0, [[NH, 128], [1, NH], [0, HD]])
        B.wait(ve, "ve", ev_dc, ev_p0, ev_r2use)
        ev_r2 = B.ev(ve.tensor_tensor(rhs2[:].rearrange("p (h d) -> p h d", d=HD),
                                      p0b[:].rearrange("p (h d) -> p h d", d=HD), dc_ap, ALU.mult), "ve")
        ev_r2prev = ev_r2
        dtT_ap = bass.AP(dtT_s, c * NH, [[NC_CH * NH, 128], [1, NH], [0, HD]])
        B.wait(ve, "ve", ev_x, ev_xdtuse)
        ev_xd = B.ev(ve.tensor_tensor(xdt[:].rearrange("p (h d) -> p h d", d=HD),
                                      xtmB[par][:].rearrange("p (h d) -> p h d", d=HD),
                                      dtT_ap, ALU.mult), "ve")
        ev_bg = [None, None]
        for half in range(2):
            B.wait(gp, "gp", ev_bg_sc[half])
            off = (half * 32) * CSROW + c * CH
            src1 = bass.AP(cs_h, off, [[0, 128], [CSROW, 32], [1, CH]])
            B.dev(gp.dma_start(out=bigb[half][:].rearrange("p (h l) -> p h l", l=CH),
                               in_=src1), f"bg{half}")
            srcp = bass.AP(t["pen"], 0, [[128, 128], [0, 32], [1, CH]])
            B.dev(gp.dma_start(out=bigb[half][:].rearrange("p (h l) -> p h l", l=CH),
                               in_=srcp, accum_op=ALU.add), f"bg{half}")
            ev_bg[half] = B.dev(gp.dma_start(out=flush_h.ap()[:, half:half + 1],
                                             in_=bigb[half][:, 0:1],
                                             allow_slow_non_contiguous=True), f"bg{half}")
        ev_y = None
        for g in range(NG):
            half = g // 4
            gpar = g % 2
            gsl = slice(g * CH, (g + 1) * CH)
            B.wait(te, "te", ev_bc, ev_cc2, gps_free[gpar])
            ev_sc_mm = B.ev(te.matmul(gps[gpar][:, :], bcs[par][:, gsl],
                                      ccs[par][:, gsl], start=True, stop=True), "te")
            ev_bcuse[par] = ev_sc_mm
            B.wait(sc, "sc", ev_sc_mm, gb_free[gpar])
            ev_gb = B.ev(sc.copy(gb[gpar][:, :], gps[gpar][:, :]), "sc")
            gps_free[gpar] = ev_gb
            B.wait(sc, "sc", ev_bg[half], ev_expb_use[half], ev_csT if c == 0 else None)
            for hh8 in range(8):
                h = g * 8 + hh8
                slot = h % 32
                ins = sc.activation(out=expb[half][:, slot * CH:(slot + 1) * CH],
                                    in_=bigb[half][:, slot * CH:(slot + 1) * CH],
                                    func=AF.Exp, bias=negcsT_s[:, c * NH + h: c * NH + h + 1])
            ev_exp8 = B.ev(ins, "sc")
            ev_bg_sc[half] = ev_exp8
            grep_ap = bass.AP(gb[gpar], 0, [[CH, 128], [0, 8], [1, CH]])
            B.wait(ve, "ve", ev_gb, ev_exp8, m8_free[gpar])
            ev_m = B.ev(ve.tensor_tensor(m8[gpar][:].rearrange("p (h l) -> p h l", l=CH),
                                         grep_ap,
                                         expb[half][:].rearrange("p (h l) -> p h l", l=CH)[:, (g % 4) * 8:(g % 4) * 8 + 8, :],
                                         ALU.mult), "ve")
            gb_free[gpar] = ev_m
            ev_expb_use[half] = ev_m
            B.wait(te, "te", ev_pc, ev_r2, yps_free[gpar])
            for hh8 in range(8):
                h = g * 8 + hh8
                te.matmul(yps[gpar][:, hh8 * HD:(hh8 + 1) * HD], ccs[par][:, gsl],
                          pcb[par][:, h * HD:(h + 1) * HD], start=(hh8 == 0), stop=False)
                ins = te.matmul(yps[gpar][:, hh8 * HD:(hh8 + 1) * HD], ccs[par][:, gsl],
                                rhs2[:, h * HD:(h + 1) * HD], start=False, stop=False)
            ev_yo = B.ev(ins, "te")
            ev_pcuse[par] = ev_yo
            ev_ccuse[par] = ev_yo
            ev_r2use = ev_yo
            ecs_ap = bass.AP(expcsT_s, c * NH + g * 8, [[NC_CH * NH, 128], [1, 8], [0, HD]])
            B.wait(ve, "ve", ev_yo)
            ev_sc_y = B.ev(ve.tensor_tensor(yps[gpar][:].rearrange("p (h d) -> p h d", d=HD),
                                            yps[gpar][:].rearrange("p (h d) -> p h d", d=HD),
                                            ecs_ap, ALU.mult), "ve")
            B.wait(te, "te", ev_m, ev_xd, ev_sc_y)
            for hh8 in range(8):
                h = g * 8 + hh8
                ins = te.matmul(yps[gpar][:, hh8 * HD:(hh8 + 1) * HD],
                                m8[gpar][:, hh8 * CH:(hh8 + 1) * CH],
                                xdt[:, h * HD:(h + 1) * HD], start=False, stop=True)
            ev_yd = B.ev(ins, "te")
            ev_xdtuse = ev_yd
            m8_free[gpar] = ev_yd
            drep_ap = bass.AP(drep_s, g * 8, [[NH, 128], [1, 8], [0, HD]])
            B.wait(ve, "ve", ev_yd, ev_yuse[par] if g == 0 else None)
            ve.tensor_tensor(tmp8[:].rearrange("p (h d) -> p h d", d=HD),
                             xtmB[par][:].rearrange("p (h d) -> p h d", d=HD)[:, g * 8:(g + 1) * 8, :],
                             drep_ap, ALU.mult)
            ins = ve.tensor_add(ych[par][:, g * 512:(g + 1) * 512], tmp8[:, :], yps[gpar][:, :])
            ev_y = B.ev(ins, "ve")
            yps_free[gpar] = ev_y
        ev_xuse[par] = ev_y
        B.wait(sy, "sy", ev_y)
        ev_yuse[par] = B.dev(sy.dma_start(out=y_h.ap()[c * CH:(c + 1) * CH, :], in_=ych[par][:, :]), f"yo{par}")
        if "bigb_tap" in B.taps and c == 7:
            bigb_tap = B.dram("bigb_tap", [128, NH * CH], F32, tap=True)
            B.wait(sy, "sy", B.now("ve"))
            B.dev(sy.dma_start(out=bigb_tap.ap()[:, 0:32 * CH], in_=bigb[0][:, :]), "tap")
            B.dev(sy.dma_start(out=bigb_tap.ap()[:, 32 * CH:], in_=bigb[1][:, :]), "tap")
        if "r2_tap" in B.taps and c == 7:
            r2_tap = B.dram("r2_tap", [128, DI], BF16, tap=True)
            p0_tap = B.dram("p0_tap", [128, DI], BF16, tap=True)
            dc_tap = B.dram("dc_tap", [128, NH], F32, tap=True)
            B.wait(sy, "sy", B.now("ve"))
            B.dev(sy.dma_start(out=r2_tap.ap(), in_=rhs2[:, :]), "tap")
            B.dev(sy.dma_start(out=p0_tap.ap(), in_=p0b[:, :]), "tap")
            B.dev(sy.dma_start(out=dc_tap.ap(), in_=dcx_s[:, :]), "tap")

    B.barrier()
    p4.close()
    if phases <= 4:
        B.wait_all_dma(sy)
        return nc, B

    p5 = contextlib.ExitStack()
    p5ps = contextlib.ExitStack()
    yfm = B.sb("yfm", [128, 32 * T], BF16, p5)
    zw = [B.sb(f"zw{i}", [128, 16 * 128], BF16, p5ps) for i in range(2)]
    zs = [B.sb(f"zs{i}", [128, T], BF16, p5ps) for i in range(2)]
    sqt = B.sb("sqt", [128, T], BF16, p5ps)
    v_s = B.sb("v_s", [1, T], F32, p5ps)
    rstd8 = B.sb("rstd8", [NG, T], F32, p5ps)
    rb = [B.sb(f"rb{i}", [128, T], F32, p5ps) for i in range(2)]
    zp = [B.ps(f"zp{i}", [128, 512], F32, p5ps) for i in range(2)]
    stp = B.ps("stp", [1, 1024], F32, p5ps)

    zw_use = [None, None]
    zs_use = [None, None]
    zp_last = [None, None]
    rb_use = [None, None]
    rbg_use = [None, None]
    ev_sqtuse = None
    ev_gate = {}
    ev_norm = {}
    ev_ytgate = {}
    for zc in range(32):
        par = zc % 2
        if zc == 0:
            B.wait(sy, "sy", ("yo0", B.cnt.get("yo0", 0)), ("yo1", B.cnt.get("yo1", 0)))
        B.wait(sy, "sy", ev_ytgate.get(zc - 2))
        ev_yt = B.dev(sy.dma_start(out=yfm[:, zc * T:(zc + 1) * T],
                                   in_=y_h.ap()[:, zc * 128:(zc + 1) * 128], transpose=True), f"yt{par}")
        src = bass.AP(t["winT"], 6208 + zc * 128, [[10304, 128], [128 * 10304, 16], [1, 128]])
        B.wait(gp, "gp", zw_use[par])
        ev_w = B.dev(gp.dma_start(out=zw[par][:].rearrange("p (k f) -> p k f", f=128), in_=src), f"zw{par}")
        B.wait(te, "te", ev_w)
        ev_sl = None
        for nt in range(2):
            B.wait(te, "te", zp_last[nt])
            for k in range(16):
                ins = te.matmul(zp[nt][:, :], zw[par][:, k * 128:(k + 1) * 128],
                                xfm_s[:, k * TCOL + 3 + nt * 512: k * TCOL + 3 + (nt + 1) * 512],
                                start=(k == 0), stop=(k == 15))
            ev_mm = B.ev(ins, "te")
            if nt == 1:
                zw_use[par] = ev_mm
            B.wait(sc, "sc", ev_mm, zs_use[par] if nt == 0 else None)
            ev_sl = B.ev(sc.activation(out=zs[par][:, nt * 512:(nt + 1) * 512],
                                       in_=zp[nt][:, :], func=AF.Silu), "sc")
            zp_last[nt] = ev_sl
        B.wait(ve, "ve", ev_sl, ev_yt)
        ev_g = B.ev(ve.tensor_mul(yfm[:, zc * T:(zc + 1) * T], yfm[:, zc * T:(zc + 1) * T],
                                  zs[par][:, :]), "ve")
        ev_ytgate[zc] = ev_g
        zs_use[par] = ev_g
        ev_gate[zc] = ev_g
        B.wait(sc, "sc", ev_g, ev_sqtuse)
        ev_sq = B.ev(sc.activation(out=sqt[:, :], in_=yfm[:, zc * T:(zc + 1) * T], func=AF.Square), "sc")
        B.wait(te, "te", ev_sq)
        for nt in range(2):
            ins = te.matmul(stp[:, nt * 512:(nt + 1) * 512], ones_s[:, :],
                            sqt[:, nt * 512:(nt + 1) * 512], start=(zc % 4 == 0), stop=(zc % 4 == 3),
                            skip_group_check=True)
        ev_st = B.ev(ins, "te")
        ev_sqtuse = ev_st
        if zc % 4 == 3:
            gidx = zc // 4
            gpar = gidx % 2
            B.wait(ve, "ve", ev_st, rbg_use[gpar])
            ev_v = B.ev(ve.tensor_scalar(v_s[:, :], stp[:, :], 1.0 / 512, EPS, ALU.mult, ALU.add), "ve")
            B.wait(sc, "sc", ev_v)
            sc.activation(out=v_s[:, :], in_=v_s[:, :], func=AF.Ln)
            ev_rs = B.ev(sc.activation(out=v_s[:, :], in_=v_s[:, :],
                                       func=AF.Exp, scale=-0.5), "sc")
            B.wait(sy, "sy", ev_rs, rbg_use[gpar])
            ev_rh = B.dev(sy.dma_start(out=rstd_h.ap()[gidx:gidx + 1, :],
                                       in_=v_s[:, :]), f"rs{gpar}")
            B.wait(gp, "gp", ev_rh, rb_use[gpar])
            ev_rb = B.dev(gp.dma_start(out=rb[gpar][:, :],
                                       in_=bass.AP(rstd_h, gidx * T, [[0, 128], [1, T]])), f"rb{gpar}")
            rbg_use[gpar] = ev_rb
            for zc2 in range(gidx * 4, gidx * 4 + 4):
                B.wait(ve, "ve", ev_rb, ev_gate[zc2], ev_consts if gidx == 0 else None)
                ins = ve.scalar_tensor_tensor(yfm[:, zc2 * T:(zc2 + 1) * T], yfm[:, zc2 * T:(zc2 + 1) * T],
                                              gcol_s[:, zc2:zc2 + 1], rb[gpar][:, :], ALU.mult, ALU.mult)
                ev_norm[zc2] = B.ev(ins, "ve")
            rb_use[gpar] = ev_norm[gidx * 4 + 3]
    if "yn_h" in B.taps:
        B.wait(sy, "sy", B.now("ve"))
        for zc in range(32):
            B.dev(sy.dma_start(out=yn_h.ap()[zc * 128:(zc + 1) * 128, :],
                               in_=yfm[:, zc * T:(zc + 1) * T]), "tap")
    B.barrier()
    p5ps.close()
    if phases <= 5:
        B.wait_all_dma(sy)
        return nc, B

    p6 = contextlib.ExitStack()
    r1 = B.sb("r1", [128, 8 * DM], F32, p6)
    wo = [B.sb(f"wo{i}", [128, 512], BF16, p6) for i in range(2)]
    xrs = [B.sb(f"xrs{i}", [128, 512], F32, p6) for i in range(2)]
    mcol = B.sb("mcol", [128, 1], F32, p6)
    scol = B.sb("scol", [128, 1], F32, p6)
    vcol = B.sb("vcol", [128, 1], F32, p6)
    tcol = B.sb("tcol", [128, 1], F32, p6)
    snk = B.sb("snk", [128, DM], BF16, p6)
    x1t = B.sb("x1t", [128, DM], BF16, p6)
    x1lo = B.sb("x1lo", [128, DM], BF16, p6)
    op_ps = [B.ps(f"op_ps{i}", [128, 512], F32, p6) for i in range(8)]

    wo_use = [None, None]
    xrs_use = [None, None]
    ev_r1 = {}
    for dmq in range(4):
        ev_mm = None
        for chx in range(32):
            par = chx % 2
            src = bass.AP(t["woutT"], chx * 128 * DM + dmq * 512, [[DM, 128], [1, 512]])
            B.wait(gp, "gp", wo_use[par])
            ev_w = B.dev(gp.dma_start(out=wo[par][:, :], in_=src), f"wo{par}")
            B.wait(te, "te", ev_w)
            for tt in range(8):
                if chx == 0:
                    B.wait(te, "te", ev_r1.get((dmq - 1, tt)), ev_norm.get(31) if dmq == 0 else None)
                ins = te.matmul(op_ps[tt][:, :], yfm[:, chx * T + tt * 128: chx * T + (tt + 1) * 128],
                                wo[par][:, :], start=(chx == 0), stop=(chx == 31))
            ev_chx = B.ev(ins, "te")
            wo_use[par] = ev_chx
            if chx == 31:
                ev_mm = ev_chx
        for tt in range(8):
            par = tt % 2
            B.wait(sy, "sy", xrs_use[par])
            ev_xr = B.dev(sy.dma_start(out=xrs[par][:, :],
                                       in_=t["xres"].ap()[tt * 128:(tt + 1) * 128,
                                                          dmq * 512:(dmq + 1) * 512]), f"xr{par}")
            B.wait(ve, "ve", ev_mm, ev_xr)
            ev = B.ev(ve.tensor_add(r1[:, tt * DM + dmq * 512: tt * DM + (dmq + 1) * 512],
                                    op_ps[tt][:, :], xrs[par][:, :]), "ve")
            ev_r1[(dmq, tt)] = ev
            xrs_use[par] = ev
    x1t_use = None
    x1lo_use = None
    ev_colsuse = None
    for tt in range(8):
        r1t = r1[:, tt * DM:(tt + 1) * DM]
        B.wait(sc, "sc", ev_r1[(3, tt)], ev_colsuse)
        sc.activation(out=snk[:, :], in_=r1t, func=AF.Copy, accum_out=mcol[:, :])
        sc.activation(out=snk[:, :], in_=r1t, func=AF.Square, accum_out=scol[:, :])
        sc.mul(mcol[:, :], mcol[:, :], 1.0 / DM)
        sc.activation(out=vcol[:, :], in_=scol[:, :], func=AF.Identity, scale=1.0 / DM, bias=eps_s[:, :])
        sc.activation(out=tcol[:, :], in_=mcol[:, :], func=AF.Square)
        sc.activation(out=vcol[:, :], in_=tcol[:, :], func=AF.Identity, scale=-1.0, bias=vcol[:, :])
        sc.activation(out=vcol[:, :], in_=vcol[:, :], func=AF.Ln)
        ev_rs = B.ev(sc.activation(out=vcol[:, :], in_=vcol[:, :], func=AF.Exp, scale=-0.5), "sc")
        B.wait(ve, "ve", ev_rs, x1t_use)
        ev_x1 = B.ev(ve.tensor_scalar(x1t[:, :], r1t, mcol[:, :], vcol[:, :],
                                      ALU.subtract, ALU.mult), "ve")
        ev_x1f = B.ev(ve.tensor_scalar(r1t, r1t, mcol[:, :], vcol[:, :],
                                       ALU.subtract, ALU.mult), "ve")
        B.wait(ve, "ve", x1lo_use)
        ev_lo = B.ev(ve.tensor_sub(x1lo[:, :], r1t, x1t[:, :]), "ve")
        ev_colsuse = ev_lo
        B.wait(sy, "sy", ev_x1, ev_x1f, ev_lo)
        x1t_use = B.dev(sy.dma_start(out=x1_h.ap()[tt * 128:(tt + 1) * 128, :], in_=x1t[:, :]), "x1t")
        x1lo_use = B.dev(sy.dma_start(out=x1lo_h.ap()[tt * 128:(tt + 1) * 128, :], in_=x1lo[:, :]), "x1l")
        B.dev(sy.dma_start(out=x1f_h.ap()[tt * 128:(tt + 1) * 128, :], in_=r1t), "x1f")
    B.barrier()
    p6.close()
    p5.close()
    main_scope.close()
    if phases <= 6:
        B.wait_all_dma(sy)
        return nc, B

    p78 = contextlib.ExitStack()
    o8_s = B.sb("o8_s", [8, DM], F32, p78)
    p7 = contextlib.ExitStack()
    x1fm = B.sb("x1fm", [128, 16 * T], BF16, p7)
    x1lofm = B.sb("x1lofm", [128, 16 * T], BF16, p7)
    h1 = B.sb("h1", [128, 16 * T], BF16, p7)
    h2 = B.sb("h2", [128, 8 * DM], BF16, p7)
    wf = [B.sb(f"wf{i}", [128, 2048], BF16, p7) for i in range(2)]
    wrhl = B.sb("wrhl", [128, 16 * 16], BF16, p7)
    wrhi = B.sb("wrhi", [128, 16 * 8], BF16, p7)
    rt_sb = B.sb("rt_sb", [16, 1024], F32, p7)
    rt_sb2 = B.sb("rt_sb2", [8, 1024], F32, p7)
    lgt = B.sb("lgt", [128, 24], F32, p7)
    lsum = B.sb("lsum", [128, 8], F32, p7)
    e8 = B.sb("e8", [128, 8], F32, p7)
    msk8 = B.sb("msk8", [128, 8], F32, p7)
    tm8 = B.sb("tm8", [128, 8], F32, p7)
    m1c = B.sb("m1c", [128, 1], F32, p7)
    nm1c = B.sb("nm1c", [128, 1], F32, p7)
    m2c = B.sb("m2c", [128, 1], F32, p7)
    e2c = B.sb("e2c", [128, 1], F32, p7)
    rdc = B.sb("rdc", [128, 1], F32, p7)
    wgate = B.sb("wgate", [128, 8 * NC_CH], BF16, p7)
    gl_s = B.sb("gl_s", [8, DM], F32, p7)
    gs_s = B.sb("gs_s", [8, DM], F32, p7)
    gT = B.sb("gT", [128, 16 * 8], BF16, p7)

    ev_xf = {}
    ev_xlo = {}
    B.wait(sy, "sy", ("x1t", B.cnt.get("x1t", 0)), ("x1l", B.cnt.get("x1l", 0)))
    for k in range(16):
        ev_xf[k] = B.dev(sy.dma_start(out=x1fm[:, k * T:(k + 1) * T],
                                      in_=x1_h.ap()[:, k * 128:(k + 1) * 128], transpose=True), "xf")
        ev_xlo[k] = B.dev(sy.dma_start(out=x1lofm[:, k * T:(k + 1) * T],
                                       in_=x1lo_h.ap()[:, k * 128:(k + 1) * 128], transpose=True), "xf")
    B.wait(ve, "ve", ev_consts)
    wr3 = wr_s[:].rearrange("p (k e) -> p k e", e=8)
    wrhi3 = wrhi[:].rearrange("p (k e) -> p k e", e=8)
    wrhl3 = wrhl[:].rearrange("p (k e) -> p k e", e=16)
    B.vchain(ve.tensor_copy(wrhi3, wr3))
    ve.tensor_copy(wrhl3[:, :, 0:8], wrhi3)
    ev_wr = B.ev(ve.tensor_sub(wrhl3[:, :, 8:16], wr3, wrhi3), "ve")

    p7r = contextlib.ExitStack()
    rtA = B.ps("rtA", [16, 512], F32, p7r)
    rtB = B.ps("rtB", [8, 512], F32, p7r)
    trp = B.ps("trp", [128, 128], F32, p7r)
    ev_wguse = None
    for blk in range(2):
        B.wait(te, "te", ev_wr, ev_xf[15], ev_xlo[15])
        for k in range(16):
            te.matmul(rtA[:, :], wrhl[:, k * 16:(k + 1) * 16],
                      x1fm[:, k * T + blk * 512: k * T + (blk + 1) * 512],
                      start=(k == 0), stop=(k == 15))
        for k in range(16):
            ins = te.matmul(rtB[:, :], wrhi[:, k * 8:(k + 1) * 8],
                            x1lofm[:, k * T + blk * 512: k * T + (blk + 1) * 512],
                            start=(k == 0), stop=(k == 15))
        ev_rt = B.ev(ins, "te")
        B.wait(sc, "sc", ev_rt)
        sc.copy(rt_sb[0:16, blk * 512:(blk + 1) * 512], rtA[:, :])
        ev_cp = B.ev(sc.copy(rt_sb2[0:8, blk * 512:(blk + 1) * 512], rtB[:, :]), "sc")
        for q in range(4):
            tt = blk * 4 + q
            B.wait(te, "te", ev_cp, ev_wguse)
            te.transpose(trp[:, 0:16], rt_sb[0:16, blk * 512 + q * 128: blk * 512 + (q + 1) * 128],
                         ident_s[0:16, 0:16])
            ev_t1 = B.ev(te.transpose(trp[:, 16:24], rt_sb2[0:8, blk * 512 + q * 128: blk * 512 + (q + 1) * 128],
                                      ident_s[0:8, 0:8]), "te")
            B.wait(sc, "sc", ev_t1)
            ev_lg = B.ev(sc.copy(lgt[:, :], trp[:, 0:24]), "sc")
            B.wait(ve, "ve", ev_lg)
            B.vchain(ve.tensor_add(lsum[:, :], lgt[:, 0:8], lgt[:, 8:16]))
            B.vchain(ve.tensor_add(lsum[:, :], lsum[:, :], lgt[:, 16:24]))
            B.vchain(ve.reduce_max(m1c[:, :], lsum[:, :], axis=mybir.AxisListType.X))
            B.vchain(ve.tensor_scalar(tm8[:, :], lsum[:, :], m1c[:, :], -1e6, ALU.is_ge, ALU.mult))
            B.vchain(ve.tensor_add(tm8[:, :], tm8[:, :], lsum[:, :]))
            B.vchain(ve.reduce_max(m2c[:, :], tm8[:, :], axis=mybir.AxisListType.X))
            ev_ve1 = B.vchain(ve.tensor_scalar_mul(nm1c[:, :], m1c[:, :], -1.0))
            B.wait(sc, "sc", ev_ve1)
            sc.activation(out=e8[:, :], in_=lsum[:, :], func=AF.Exp, bias=nm1c[:, :])
            ev_e2 = B.ev(sc.activation(out=e2c[:, :], in_=m2c[:, :], func=AF.Exp, bias=nm1c[:, :]), "sc")
            B.wait(ve, "ve", ev_e2)
            B.vchain(ve.tensor_scalar_add(e2c[:, :], e2c[:, :], 1.0))
            B.vchain(ve.reciprocal(rdc[:, :], e2c[:, :]))
            B.vchain(ve.tensor_single_scalar(msk8[:, :], lsum[:, :], m2c[:, :], ALU.is_ge))
            sl = wgate[:, tt * 8:(tt + 1) * 8]
            B.vchain(ve.tensor_mul(msk8[:, :], e8[:, :], msk8[:, :]))
            ev_wg = B.ev(ve.tensor_scalar_mul(sl, msk8[:, :], rdc[:, :]), "ve")
            ev_wguse = ev_wg
    if "wg_h" in B.taps:
        B.wait(sy, "sy", B.now("ve"))
        B.dev(sy.dma_start(out=wg_h.ap(), in_=wgate[:, :]), "tap")
    B.barrier()
    p7r.close()

    p7a = contextlib.ExitStack()
    h1p = [B.ps(f"h1p{i}", [128, 512], F32, p7a) for i in range(2)]
    wf_use = [None, None]
    h1p_last = [None, None]
    for f1 in range(16):
        par = f1 % 2
        src = bass.AP(t["w1T"], f1 * 128, [[2048, 128], [128 * 2048, 16], [1, 128]])
        B.wait(gp, "gp", wf_use[par])
        ev_w = B.dev(gp.dma_start(out=wf[par][:].rearrange("p (k f) -> p k f", f=128), in_=src), f"wf{par}")
        B.wait(te, "te", ev_w)
        for nt in range(2):
            B.wait(te, "te", h1p_last[nt])
            for k in range(16):
                ins = te.matmul(h1p[nt][:, :], wf[par][:, k * 128:(k + 1) * 128],
                                x1fm[:, k * T + nt * 512: k * T + (nt + 1) * 512],
                                start=(k == 0), stop=(k == 15))
            ev_mm = B.ev(ins, "te")
            if nt == 1:
                wf_use[par] = ev_mm
            B.wait(sc, "sc", ev_mm)
            ev_sl = B.ev(sc.activation(out=h1[:, f1 * T + nt * 512: f1 * T + (nt + 1) * 512],
                                       in_=h1p[nt][:, :], func=AF.Silu), "sc")
            h1p_last[nt] = ev_sl
    ev_h1 = B.now("sc")
    B.barrier()
    p7a.close()

    p7b = contextlib.ExitStack()
    h2p = [B.ps(f"h2p{i}", [128, 512], F32, p7b) for i in range(8)]
    wf_use = [None, None]
    ev_h2ev = {}
    for f2q in range(4):
        ev_mm = None
        for k in range(16):
            par = k % 2
            src = bass.AP(t["w3T"], k * 128 * 2048 + f2q * 512, [[2048, 128], [1, 512]])
            B.wait(gp, "gp", wf_use[par])
            ev_w = B.dev(gp.dma_start(out=wf[par][:, 0:512], in_=src), f"wf{par}")
            B.wait(te, "te", ev_w)
            for tt in range(8):
                if k == 0:
                    B.wait(te, "te", ev_h2ev.get((f2q - 1, tt)))
                ins = te.matmul(h2p[tt][:, :], h1[:, k * T + tt * 128: k * T + (tt + 1) * 128],
                                wf[par][:, 0:512], start=(k == 0), stop=(k == 15))
            ev_k = B.ev(ins, "te")
            wf_use[par] = ev_k
            if k == 15:
                ev_mm = ev_k
        for tt in range(8):
            B.wait(sc, "sc", ev_mm)
            ev_h2ev[(f2q, tt)] = B.ev(sc.activation(
                out=h2[:, tt * DM + f2q * 512: tt * DM + (f2q + 1) * 512],
                in_=h2p[tt][:, :], func=AF.Silu), "sc")
    B.barrier()
    p7b.close()

    p7c = contextlib.ExitStack()
    gpp = B.ps("gpp", [8, DM], F32, p7c)
    for tt in range(8):
        for q in range(4):
            ins = te.matmul(gpp[:, q * 512:(q + 1) * 512], wgate[:, tt * 8:(tt + 1) * 8],
                            h2[:, tt * DM + q * 512: tt * DM + (q + 1) * 512],
                            start=(tt == 0), stop=(tt == 7), skip_group_check=True)
    ev_g = B.ev(ins, "te")
    B.wait(sc, "sc", ev_g)
    ev_gl = B.ev(sc.copy(gl_s[:, :], gpp[:, :]), "sc")
    B.wait(gp, "gp", ev_gl)
    ev_gin = B.dev(gp.dma_start(out=gin.ap(), in_=gl_s[:, :]), "gin")
    B.wait(gp, "gp", ev_gin)
    ev_gcc = B.ev(gp.collective_compute(
        "AllReduce", ALU.add, replica_groups=[[0, 1, 2, 3, 4, 5, 6, 7]],
        ins=[gin.ap().opt()], outs=[gout.ap().opt()]), "cc")
    B.wait(sy, "sy", ev_gcc)
    ev_gs = B.dev(sy.dma_start(out=gs_s[:, :], in_=gout.ap()), "gsr")
    if "g_h" in B.taps:
        B.wait(sy, "sy", ev_gs)
        B.dev(sy.dma_start(out=g_h.ap(), in_=gs_s[:, :]), "tap")
    B.barrier()
    p7c.close()

    p7d = contextlib.ExitStack()
    trp2 = B.ps("trp2", [128, 128], F32, p7d)
    o8p = B.ps("o8p", [8, DM], F32, p7d)
    ev_gT = None
    for k in range(16):
        B.wait(te, "te", ev_gs if k == 0 else None, ev_gT)
        ev_t = B.ev(te.transpose(trp2[:, 0:8], gs_s[0:8, k * 128:(k + 1) * 128],
                                 ident_s[0:8, 0:8]), "te")
        B.wait(ve, "ve", ev_t)
        ev_gT = B.ev(ve.tensor_copy(gT[:, k * 8:(k + 1) * 8], trp2[:, 0:8]), "ve")
    wf_use = [None, None]
    ev_mm = None
    for k in range(16):
        par = k % 2
        src = bass.AP(t["w2T"], k * 128 * 2048, [[2048, 128], [1, 2048]])
        B.wait(gp, "gp", wf_use[par])
        ev_w = B.dev(gp.dma_start(out=wf[par][:, :], in_=src), f"wf{par}")
        B.wait(te, "te", ev_w, ev_gT)
        for q in range(4):
            ins = te.matmul(o8p[:, q * 512:(q + 1) * 512], gT[:, k * 8:(k + 1) * 8],
                            wf[par][:, q * 512:(q + 1) * 512], start=(k == 0), stop=(k == 15),
                            skip_group_check=True)
        ev_mm = B.ev(ins, "te")
        wf_use[par] = ev_mm
    B.wait(ve, "ve", ev_mm)
    ev_o8 = B.ev(ve.tensor_scalar_mul(o8_s[:, :], o8p[:, :], omask_s[:, :]), "ve")
    if "o8_h" in B.taps:
        B.wait(sy, "sy", ev_o8)
        B.dev(sy.dma_start(out=o8_h.ap(), in_=o8_s[:, :]), "tap")
    B.barrier()
    p7d.close()
    p7.close()

    p8 = contextlib.ExitStack()
    xtf = [B.sb(f"xtf{i}", [128, DM], F32, p8) for i in range(2)]
    mcol = B.sb("mcol8", [128, 1], F32, p8)
    scol = B.sb("scol8", [128, 1], F32, p8)
    vcol = B.sb("vcol8", [128, 1], F32, p8)
    tcol = B.sb("tcol8", [128, 1], F32, p8)
    snk2 = B.sb("snk2", [128, DM], BF16, p8)
    xtf_use = [None, None]
    ev_colsuse = None
    for tt in range(8):
        par = tt % 2
        if tt == 0:
            B.wait(sy, "sy", ("x1f", B.cnt.get("x1f", 0)))
        B.wait(sy, "sy", xtf_use[par])
        ev_x = B.dev(sy.dma_start(out=xtf[par][:, :],
                                  in_=x1f_h.ap()[tt * 128:(tt + 1) * 128, :]), f"x8{par}")
        if tt == 0:
            B.wait(ve, "ve", ev_x, ev_o8)
            ev_x = B.ev(ve.tensor_add(xtf[par][0:8, :], xtf[par][0:8, :], o8_s[:, :]), "ve")
        B.wait(sc, "sc", ev_x, ev_colsuse)
        sc.activation(out=snk2[:, :], in_=xtf[par][:, :], func=AF.Copy, accum_out=mcol[:, :])
        sc.activation(out=snk2[:, :], in_=xtf[par][:, :], func=AF.Square, accum_out=scol[:, :])
        sc.mul(mcol[:, :], mcol[:, :], 1.0 / DM)
        sc.activation(out=vcol[:, :], in_=scol[:, :], func=AF.Identity, scale=1.0 / DM, bias=eps_s[:, :])
        sc.activation(out=tcol[:, :], in_=mcol[:, :], func=AF.Square)
        sc.activation(out=vcol[:, :], in_=tcol[:, :], func=AF.Identity, scale=-1.0, bias=vcol[:, :])
        sc.activation(out=vcol[:, :], in_=vcol[:, :], func=AF.Ln)
        ev_rs = B.ev(sc.activation(out=vcol[:, :], in_=vcol[:, :], func=AF.Exp, scale=-0.5), "sc")
        B.wait(ve, "ve", ev_rs)
        ev_o = B.ev(ve.tensor_scalar(xtf[par][:, :], xtf[par][:, :], mcol[:, :], vcol[:, :],
                                     ALU.subtract, ALU.mult), "ve")
        ev_colsuse = ev_o
        B.wait(sy, "sy", ev_o)
        xtf_use[par] = B.ev(sy.dma_start(out=out_ext.ap()[tt * 128:(tt + 1) * 128, :],
                                         in_=xtf[par][:, :]), "out", 16)
    sy.wait_ge(B.sems["out"], B.cnt["out"])
    B.wait_all_dma(sy)
    return nc, B


def prepare_inputs(inputs):
    x = np.asarray(inputs["L0_attn_hidden_states"], np.float32)
    W_in = np.asarray(inputs["L0_attn_W_in"], np.float32)
    conv_w = np.asarray(inputs["L0_attn_conv_w"], np.float32)
    conv_b = np.asarray(inputs["L0_attn_conv_b"], np.float32)
    A = np.asarray(inputs["L0_attn_A"], np.float32)
    D = np.asarray(inputs["L0_attn_D"], np.float32)
    dt_bias = np.asarray(inputs["L0_attn_dt_bias"], np.float32)
    norm_w = np.asarray(inputs["L0_attn_norm_weight"], np.float32)
    W_out = np.asarray(inputs["L0_attn_W_out"], np.float32)
    w1 = np.asarray(inputs["L0_ffn_w1"], np.float32)
    w3 = np.asarray(inputs["L0_ffn_w3"], np.float32)
    w2 = np.asarray(inputs["L0_ffn_w2"], np.float32)
    wr = np.asarray(inputs["L0_ffn_wr"], np.float32)

    perm = np.concatenate([np.arange(4096, 10304), np.arange(0, 4096)])
    winT = np.ascontiguousarray(W_in.T[:, perm])
    woutT = np.ascontiguousarray(W_out.T)
    w1T = np.ascontiguousarray(w1.T)
    w3T = np.ascontiguousarray(w3.T)
    w2T = np.ascontiguousarray(w2.T)
    wrT = np.ascontiguousarray(wr.T)
    convb = np.ascontiguousarray(conv_b.reshape(-1, 1))
    acol = np.ascontiguousarray(A.reshape(-1, 1))
    drep = np.ascontiguousarray(np.broadcast_to(D.reshape(1, -1), (128, NH)))
    dtbcol = np.ascontiguousarray(dt_bias.reshape(-1, 1))
    gcol = np.ascontiguousarray(norm_w.reshape(32, 128).T)
    ident = np.eye(128, dtype=np.float32)
    pen = np.tril(np.full((128, 128), PEN, np.float32), -1)
    onescol = np.ones((128, 1), np.float32)
    shared = dict(winT=winT, woutT=woutT, w1T=w1T, w3T=w3T, w2T=w2T, wrT=wrT,
                  convw=conv_w, convb=convb, acol=acol, drep=drep, dtbcol=dtbcol,
                  gcol=gcol, ident=ident, pen=pen, onescol=onescol)

    in_maps = []
    for core in range(NCORES):
        b, hh = divmod(core, 2)
        t0 = hh * T
        halo = (x[b, t0 - 3:t0].T if hh == 1 else np.zeros((DM, 3), np.float32))
        xfm = np.ascontiguousarray(np.concatenate([halo, x[b, t0:t0 + T].T], axis=1))
        m = dict(shared)
        m["xfm"] = xfm
        m["xres"] = np.ascontiguousarray(x[b, t0:t0 + T])
        m["sendmask"] = np.full((128, 1), 1.0 if hh == 0 else 0.0, np.float32)
        m["recvmask"] = np.full((128, 1), 1.0 if hh == 1 else 0.0, np.float32)
        m["ownermask"] = np.full((8, 1), 1.0 if core == 0 else 0.0, np.float32)
        in_maps.append(m)
    return in_maps


_BF16_KEYS = {"onescol"}


def _cast_in_maps(in_maps):
    import ml_dtypes
    out = []
    for m in in_maps:
        mm = {}
        for k, v in m.items():
            if k in _BF16_KEYS:
                mm[k] = np.ascontiguousarray(v.astype(ml_dtypes.bfloat16))
            else:
                mm[k] = np.ascontiguousarray(v)
        out.append(mm)
    return out


def kernel(**inputs):
    in_maps = _cast_in_maps(prepare_inputs(inputs))
    nc, _ = build_program(DEBUG_TAPS)
    res = run_bass_kernel_spmd(nc, in_maps, core_ids=list(range(NCORES)))
    out = np.empty((4, 2048, DM), np.float32)
    for core in range(NCORES):
        b, hh = divmod(core, 2)
        out[b, hh * T:(hh + 1) * T, :] = np.asarray(res.results[core]["out"])
    return out


# version 05
# speedup vs baseline: 1.1308x; 1.0499x over previous
import contextlib
import numpy as np
import concourse.bass as bass
import concourse.mybir as mybir
from concourse.bass_utils import run_bass_kernel_spmd

F32 = mybir.dt.float32
BF16 = mybir.dt.bfloat16
AF = mybir.ActivationFunctionType
ALU = mybir.AluOpType

NCORES = 8
T = 1024
CH = 128
NC_CH = T // CH
DM = 2048
DI = 4096
NH = 64
HD = 64
NG = 8
DS = 128
CONV_DIM = 6144
EPS = 1e-5
PEN = -30000.0
TCOL = T + 3
CSROW = T + 16

DEBUG_TAPS = ()


class Builder:
    def __init__(self, nc, taps=()):
        self.nc = nc
        self.taps = set(taps)
        self.stack = contextlib.ExitStack()
        self.sems = {}
        self.cnt = {}
        self.lastwait = {}
        for s in ("te", "ve", "sc", "cc", "out"):
            self.sems[s] = self.stack.enter_context(nc.semaphore(s))
            self.cnt[s] = 0
        self.t = {}

    def ev(self, instr, sem, n=1):
        self.cnt[sem] += n
        instr.then_inc(self.sems[sem], n)
        return (sem, self.cnt[sem])

    def now(self, sem):
        return (sem, self.cnt[sem])

    def wait(self, eng, engname, *evs):
        for e in evs:
            if e is None:
                continue
            sem, val = e
            if val <= 0:
                continue
            if self.lastwait.get((engname, sem), -1) >= val:
                continue
            eng.wait_ge(self.sems[sem], val)
            self.lastwait[(engname, sem)] = val

    def dsem(self, name):
        if name not in self.sems:
            self.sems[name] = self.stack.enter_context(self.nc.semaphore(name))
            self.cnt[name] = 0
        return name

    def dev(self, ins, sem):
        self.dsem(sem)
        return self.ev(ins, sem, 16)

    def wait_all_dma(self, eng, engname="sy"):
        for s, v in list(self.cnt.items()):
            if s in ("te", "ve", "sc", "gp"):
                continue
            self.wait(eng, engname, (s, v))

    def vchain(self, ins):
        e = self.ev(ins, "ve")
        self.nc.vector.wait_ge(self.sems["ve"], e[1])
        self.lastwait[("ve", "ve")] = e[1]
        return e

    def barrier(self):
        self.nc.all_engine_barrier()
        for s in ("te", "ve", "sc"):
            for e in ("te", "ve", "sc", "gp", "sy"):
                self.lastwait[(e, s)] = self.cnt[s]

    def sb(self, name, shape, dtype, scope=None):
        t = (scope or self.stack).enter_context(self.nc.sbuf_tensor(name, shape, dtype))
        self.t[name] = t
        return t

    def ps(self, name, shape, dtype, scope=None):
        t = (scope or self.stack).enter_context(self.nc.psum_tensor(name, shape, dtype))
        self.t[name] = t
        return t

    def dram(self, name, shape, dtype, tap=False):
        kind = "ExternalOutput" if (tap and name in self.taps) else "Internal"
        t = self.nc.dram_tensor(name, shape, dtype, kind=kind)
        self.t[name] = t
        return t


def build_program(taps=(), phases=99):
    nc = bass.Bass("TRN2", target_bir_lowering=False)
    B = Builder(nc, taps)
    t = B.t
    te, ve, sc, gp, sy = nc.tensor, nc.vector, nc.scalar, nc.gpsimd, nc.sync

    inputs = {
        "xfm": ([DM, TCOL], F32),
        "xres": ([T, DM], F32),
        "winT": ([DM, 10304], F32),
        "woutT": ([DI, DM], F32),
        "w1T": ([DM, DM], F32),
        "w3T": ([DM, DM], F32),
        "w2T": ([DM, DM], F32),
        "wrT": ([DM, 8], F32),
        "convw": ([CONV_DIM, 4], F32),
        "convb": ([CONV_DIM, 1], F32),
        "acol": ([NH, 1], F32),
        "drep": ([128, NH], F32),
        "dtbcol": ([NH, 1], F32),
        "gcol": ([128, 32], F32),
        "ident": ([128, 128], F32),
        "pen": ([128, 128], F32),
        "onescol": ([128, 1], BF16),
        "sendmask": ([128, 1], F32),
        "recvmask": ([128, 1], F32),
        "ownermask": ([8, 1], F32),
    }
    for nm, (shp, dt_) in inputs.items():
        t[nm] = nc.dram_tensor(nm, shp, dt_, kind="ExternalInput")
    out_ext = nc.dram_tensor("out", [T, DM], F32, kind="ExternalOutput")

    xpart_h = B.dram("xpart_h", [DI, T], BF16, tap=True)
    b_h = B.dram("b_h", [NG * DS, T], BF16, tap=True)
    c_h = B.dram("c_h", [NG * DS, T], BF16, tap=True)
    cs_h = B.dram("cs_h", [NH, CSROW], F32, tap=True)
    csT_h = B.dram("csT_h", [NC_CH, 128, NH], F32)
    csr_h = B.dram("csr_h", [NC_CH, NH, CH], F32)
    gcs_h = B.dram("gcs_h", [NC_CH, NH], F32, tap=True)
    p_h = B.dram("p_h", [NC_CH, 128, DI], BF16, tap=True)
    y_h = B.dram("y_h", [T, DI], BF16, tap=True)
    yn_h = B.dram("yn_h", [DI, T], BF16, tap=True)
    x1_h = B.dram("x1_h", [T, DM], BF16, tap=True)
    x1lo_h = B.dram("x1lo_h", [T, DM], BF16, tap=True)
    x1f_h = B.dram("x1f_h", [T, DM], F32, tap=True)
    rstd_h = B.dram("rstd_h", [NG, T], F32, tap=True)
    wg_h = B.dram("wg_h", [128, 8 * NC_CH], BF16, tap=True)
    g_h = B.dram("g_h", [8, DM], F32, tap=True)
    o8_h = B.dram("o8_h", [8, DM], F32, tap=True)
    h1_h = B.dram("h1_h", [DM, T], BF16, tap=True)
    h2_h = B.dram("h2_h", [T, DM], BF16, tap=True)
    flush_h = B.dram("flush_h", [128, 4], F32)
    ccin = B.dram("ccin", [128, DI], BF16)
    ccout = B.dram("ccout", [128, DI], BF16)
    gin = B.dram("gin", [8, DM], F32)
    gout = B.dram("gout", [8, DM], F32)

    dt_s = B.sb("dt_s", [NH, T], F32)
    adt_s = B.sb("adt_s", [NH, T], F32)
    cs_s = B.sb("cs_s", [NH, CSROW], F32)
    dtT_s = B.sb("dtT_s", [128, NC_CH * NH], BF16)
    csT_s = B.sb("csT_s", [128, NC_CH * NH], F32)
    negcsT_s = B.sb("negcsT_s", [128, NC_CH * NH], F32)
    expcsT_s = B.sb("expcsT_s", [128, NC_CH * NH], F32)
    ident_s = B.sb("ident_s", [128, 128], F32)
    convw_s = B.sb("convw_s", [128, 48 * 4], F32)
    convb_s = B.sb("convb_s", [128, 48], F32)
    acol_s = B.sb("acol_s", [NH, 1], F32)
    drep_s = B.sb("drep_s", [128, NH], F32)
    dtb_s = B.sb("dtb_s", [NH, 1], F32)
    gcol_s = B.sb("gcol_s", [128, 32], F32)
    ones_s = B.sb("ones_s", [128, 1], BF16)
    smask_s = B.sb("smask_s", [128, 1], F32)
    rmask_s = B.sb("rmask_s", [128, 1], F32)
    omask_s = B.sb("omask_s", [8, 1], F32)
    wr_s = B.sb("wr_s", [128, 16 * 8], F32)
    eps_s = B.sb("eps_s", [128, 1], F32)
    pen_s = B.sb("pen_s", [128, 128], F32)
    main_scope = contextlib.ExitStack()
    xfm_s = B.sb("xfm_s", [128, 16 * TCOL], BF16, main_scope)

    for nm, dst in (("ident", ident_s), ("pen", pen_s), ("acol", acol_s), ("drep", drep_s),
                    ("dtbcol", dtb_s), ("gcol", gcol_s), ("onescol", ones_s),
                    ("sendmask", smask_s), ("recvmask", rmask_s), ("ownermask", omask_s)):
        B.dev(sy.dma_start(out=dst[:, :], in_=t[nm].ap()), "cst")
    B.dev(sy.dma_start(out=convb_s[:, :],
                      in_=t["convb"].ap().rearrange("(c p) o -> p (c o)", p=128),
                      allow_slow_non_contiguous=True), "cst")
    B.dev(sy.dma_start(out=convw_s[:].rearrange("p (c k) -> p c k", k=4),
                      in_=t["convw"].ap().rearrange("(c p) k -> p c k", p=128)), "cst")
    B.dev(sy.dma_start(out=wr_s[:].rearrange("p (k e) -> p k e", e=8),
                      in_=t["wrT"].ap().rearrange("(k p) e -> p k e", p=128)), "cst")
    ev_consts = B.now("cst")
    B.ev(nc.vector.memset(eps_s[:, :], EPS), "ve")
    ev_xfm = B.dev(gp.dma_start(
        out=xfm_s[:].rearrange("p (k c) -> p k c", c=TCOL),
        in_=t["xfm"].ap().rearrange("(k p) c -> p k c", p=128)), "xfm")

    p1 = contextlib.ExitStack()
    wbuf = [B.sb(f"wbuf{i}", [128, 16 * 128], BF16, p1) for i in range(2)]
    cvb = [B.sb(f"cvb{i}", [128, TCOL], BF16, p1) for i in range(2)]
    cvo = [B.sb(f"cvo{i}", [128, T], BF16, p1) for i in range(2)]
    dtraw_s = B.sb("dtraw_s", [NH, TCOL], F32, p1)
    dttmp_s = B.sb("dttmp_s", [NH, T], F32, p1)
    pp = [B.ps(f"pp{i}", [128, 343], F32, p1) for i in range(6)]

    NT = ((0, 343), (343, 342), (685, 342))
    wlast_use = [None, None]
    cv_last_use = [None, None]
    cvo_last_use = [None, None]
    pp_last = [None, None]

    B.wait(te, "te", ev_xfm)
    for fo in range(49):
        par = fo % 2
        cols = 128 if fo < 48 else 64
        src = bass.AP(t["winT"], fo * 128, [[10304, 128], [128 * 10304, 16], [1, cols]])
        B.wait(gp, "gp", wlast_use[par])
        ev_w = B.dev(gp.dma_start(
            out=wbuf[par][:].rearrange("p (k f) -> p k f", f=128)[:, :, 0:cols], in_=src), f"wb{par}")
        B.wait(te, "te", ev_w, pp_last[par])
        ppset = pp[par * 3: par * 3 + 3]
        for k in range(16):
            for nt in range(3):
                c0, ncols = NT[nt]
                ins = te.matmul(ppset[nt][0:cols, 0:ncols],
                                wbuf[par][:, k * 128:k * 128 + cols],
                                xfm_s[:, k * TCOL + c0: k * TCOL + c0 + ncols],
                                start=(k == 0), stop=(k == 15))
        ev_mm = B.ev(ins, "te")
        wlast_use[par] = ev_mm
        for nt in range(3):
            c0, ncols = NT[nt]
            if fo < 48:
                B.wait(sc, "sc", ev_mm, cv_last_use[par] if nt == 0 else None)
                ev_ev = B.ev(sc.copy(cvb[par][:, c0:c0 + ncols], ppset[nt][:, 0:ncols]), "sc")
            else:
                B.wait(sc, "sc", ev_mm)
                ev_ev = B.ev(sc.copy(dtraw_s[:, c0:c0 + ncols], ppset[nt][0:64, 0:ncols]), "sc")
        pp_last[par] = ev_ev
        if fo < 48:
            B.wait(ve, "ve", ev_ev, cvo_last_use[par], ev_consts if fo == 0 else None)
            ve.tensor_scalar(cvo[par][:, :], cvb[par][:, 0:T],
                             convw_s[:, fo * 4: fo * 4 + 1],
                             convb_s[:, fo: fo + 1], ALU.mult, ALU.add)
            for k in range(1, 4):
                ins = ve.scalar_tensor_tensor(cvo[par][:, :], cvb[par][:, k:k + T],
                                              convw_s[:, fo * 4 + k: fo * 4 + k + 1],
                                              cvo[par][:, :], ALU.mult, ALU.add)
            ev_conv = B.ev(ins, "ve")
            cv_last_use[par] = ev_conv
            B.wait(sc, "sc", ev_conv)
            ev_silu = B.ev(sc.activation(out=cvo[par][:, :], in_=cvo[par][:, :], func=AF.Silu), "sc")
            B.wait(sy, "sy", ev_silu)
            if fo < 32:
                dst = xpart_h.ap()[fo * 128:(fo + 1) * 128, :]
            elif fo < 40:
                dst = b_h.ap()[(fo - 32) * 128:(fo - 31) * 128, :]
            else:
                dst = c_h.ap()[(fo - 40) * 128:(fo - 39) * 128, :]
            cvo_last_use[par] = B.dev(sy.dma_start(out=dst, in_=cvo[par][:, :]), f"cv{par}")
        else:
            B.wait(sc, "sc", ev_ev, ev_consts)
            sc.activation(out=dt_s[:, :], in_=dtraw_s[:, 3:TCOL], func=AF.Relu, bias=dtb_s[:, :])
            sc.activation(out=dttmp_s[:, :], in_=dtraw_s[:, 3:TCOL], func=AF.Abs, bias=dtb_s[:, :])
            sc.activation(out=dttmp_s[:, :], in_=dttmp_s[:, :], func=AF.Exp, scale=-1.0)
            ev_l = B.ev(sc.activation(out=dttmp_s[:, :], in_=dttmp_s[:, :], func=AF.Ln, bias=1.0), "sc")
            B.wait(ve, "ve", ev_l)
            B.ev(ve.tensor_add(dt_s[:, :], dt_s[:, :], dttmp_s[:, :]), "ve")

    if "dt_tap" in B.taps:
        dt_tap = B.dram("dt_tap", [NH, T], F32, tap=True)
        B.wait(sy, "sy", B.now("ve"))
        B.dev(sy.dma_start(out=dt_tap.ap(), in_=dt_s[:, :]), "tap")
    B.barrier()
    p1.close()
    if phases <= 1:
        B.wait_all_dma(sy)
        return nc, B

    p2 = contextlib.ExitStack()
    tps = B.ps("tps", [128, 128], F32, p2)
    g8_s = B.sb("g8_s", [NH, 8], F32, p2)
    g8b_s = B.sb("g8b_s", [NH, 8], F32, p2)
    dtTf_s = B.sb("dtTf_s", [128, NH], F32, p2)

    ve.tensor_scalar_mul(adt_s[:, :], dt_s[:, :], acol_s[:, :])
    for c in range(NC_CH):
        ve.tensor_tensor_scan(cs_s[:, c * CH:(c + 1) * CH], adt_s[:, c * CH:(c + 1) * CH],
                              adt_s[:, c * CH:(c + 1) * CH], 0.0, ALU.add, ALU.bypass)
    alast_ap = bass.AP(cs_s, CH - 1, [[CSROW, NH], [CH, NC_CH]])
    B.vchain(ve.tensor_copy(g8_s[:, :], alast_ap))
    ve.tensor_copy(g8b_s[:, 0:1], g8_s[:, 0:1])
    B.vchain(ve.tensor_add(g8b_s[:, 1:8], g8_s[:, 1:8], g8_s[:, 0:7]))
    ve.tensor_copy(g8_s[:, 0:2], g8b_s[:, 0:2])
    B.vchain(ve.tensor_add(g8_s[:, 2:8], g8b_s[:, 2:8], g8b_s[:, 0:6]))
    ve.tensor_copy(g8b_s[:, 0:4], g8_s[:, 0:4])
    B.vchain(ve.tensor_add(g8b_s[:, 4:8], g8_s[:, 4:8], g8_s[:, 0:4]))
    ve.memset(cs_s[:, T:T + 1], 0.0)
    ev_cs = B.ev(ve.tensor_copy(cs_s[:, T + 1:T + 8], g8b_s[:, 0:7]), "ve")
    B.wait(sy, "sy", ev_cs)
    ev_csh = B.dev(sy.dma_start(out=cs_h.ap(), in_=cs_s[:, :]), "csh")

    ev_evac = None
    for c in range(NC_CH):
        B.wait(te, "te", ev_cs if c == 0 else None, ev_evac)
        ev1 = B.ev(te.transpose(tps[:, 0:NH], cs_s[0:NH, c * CH:(c + 1) * CH], ident_s[0:NH, 0:NH]), "te")
        B.wait(sc, "sc", ev1)
        ev2 = B.ev(sc.copy(csT_s[:, c * NH:(c + 1) * NH], tps[:, 0:NH]), "sc")
        B.wait(te, "te", ev2)
        ev3 = B.ev(te.transpose(tps[:, 0:NH], dt_s[0:NH, c * CH:(c + 1) * CH], ident_s[0:NH, 0:NH]), "te")
        B.wait(sc, "sc", ev3)
        ev4 = B.ev(sc.copy(dtTf_s[:, 0:NH], tps[:, 0:NH]), "sc")
        ev_evac = ev4
        B.wait(ve, "ve", ev4)
        B.ev(ve.tensor_copy(dtT_s[:, c * NH:(c + 1) * NH], dtTf_s[:, 0:NH]), "ve")
    ev_csT_done = B.now("sc")
    B.wait(sc, "sc", ev_csT_done)
    sc.activation(out=expcsT_s[:, :], in_=csT_s[:, :], func=AF.Exp)
    B.ev(sc.mul(negcsT_s[:, :], csT_s[:, :], -1.0), "sc")
    B.wait(sy, "sy", ev_csT_done)
    ev_csTh = B.dev(sy.dma_start(out=csT_h.ap().rearrange("c s h -> s c h"),
                                 in_=csT_s[:].rearrange("s (c h) -> s c h", h=NH)), "cth")
    B.dev(sy.dma_start(out=csr_h.ap().rearrange("c h l -> h c l"),
                       in_=bass.AP(cs_s, 0, [[CSROW, NH], [CH, NC_CH], [1, CH]])), "csh")
    B.wait(te, "te", ev_cs, ev_evac)
    ev_gt = B.ev(te.transpose(tps[0:8, 0:NH], cs_s[0:NH, T:T + 8], ident_s[0:NH, 0:NH]), "te")
    B.wait(sc, "sc", ev_gt)
    ev_gc8 = B.ev(sc.copy(dtTf_s[0:8, 0:NH], tps[0:8, 0:NH]), "sc")
    B.wait(sy, "sy", ev_gc8)
    ev_gcsh = B.dev(sy.dma_start(out=gcs_h.ap(), in_=dtTf_s[0:8, 0:NH]), "gch")
    if "cs_tap" in B.taps:
        cs_tap = B.dram("cs_tap", [128, NC_CH * NH], F32, tap=True)
        B.wait(sy, "sy", B.now("sc"))
        B.dev(sy.dma_start(out=cs_tap.ap(), in_=csT_s[:, :]), "tap")
    B.barrier()
    p2.close()
    ev_csT = B.now("sc")
    if phases <= 2:
        B.wait_all_dma(sy)
        return nc, B

    p3 = contextlib.ExitStack()
    xtmA = [B.sb(f"xtmA{i}", [128, DI], BF16, p3) for i in range(2)]
    btm = [B.sb(f"btm{i}", [128, NG * DS], BF16, p3) for i in range(2)]
    xdd = B.sb("xdd", [128, DI], BF16, p3)
    P_s = B.sb("P_s", [128, DI], F32, p3)
    Ptmp = B.sb("Ptmp", [128, DI], F32, p3)
    ab_s = B.sb("ab_s", [128, NH], F32, p3)
    alx_s = B.sb("alx_s", [128, NH], F32, p3)
    dec_s = B.sb("dec_s", [128, NH], F32, p3)
    spsA = [B.ps(f"spsA{i}", [128, 1024], F32, p3) for i in range(2)]

    ev_p = B.ev(ve.memset(P_s[:, :], 0.0), "ve")
    ev_xuse = [None, None]
    ev_buse = [None, None]
    sps_last = [None, None]
    ev_xdduse = None
    ev_abuse = None
    ev_alxprev = None
    for c in range(NC_CH):
        par = c % 2
        if c == 0:
            B.wait(sy, "sy", ("cv0", B.cnt.get("cv0", 0)), ("cv1", B.cnt.get("cv1", 0)))
        B.wait(sy, "sy", ev_xuse[par])
        ev_x = B.dev(sy.dma_start(out=xtmA[par][:, :], in_=xpart_h.ap()[:, c * CH:(c + 1) * CH],
                                  transpose=True), f"xa{par}")
        B.wait(sy, "sy", ev_buse[par])
        ev_b = B.dev(sy.dma_start(out=btm[par][:, :], in_=b_h.ap()[:, c * CH:(c + 1) * CH],
                                  transpose=True), f"ba{par}")
        src = bass.AP(csT_h, (c * 128 + CH - 1) * NH, [[0, 128], [1, NH]])
        B.wait(gp, "gp", ev_csTh, ev_abuse, ev_alxprev)
        ev_ab = B.dev(gp.dma_start(out=ab_s[:, :], in_=src), "ab")
        B.wait(sc, "sc", ev_ab, ev_p if c > 0 else None)
        ev_alx = B.ev(sc.activation(out=alx_s[:, :], in_=ab_s[:, :], func=AF.Exp), "sc")
        ev_alxprev = ev_alx
        B.wait(ve, "ve", ev_ab, ev_csT)
        ev_sub = B.ev(ve.tensor_sub(dec_s[:, :], ab_s[:, :], csT_s[:, c * NH:(c + 1) * NH]), "ve")
        ev_abuse = ev_sub
        B.wait(sc, "sc", ev_sub)
        ev_dec = B.ev(sc.activation(out=dec_s[:, :], in_=dec_s[:, :], func=AF.Exp), "sc")
        dtT_ap = bass.AP(dtT_s, c * NH, [[NC_CH * NH, 128], [1, NH], [0, HD]])
        dec_ap = bass.AP(dec_s, 0, [[NH, 128], [1, NH], [0, HD]])
        B.wait(ve, "ve", ev_x, ev_dec, ev_xdduse)
        xdd3 = xdd[:].rearrange("p (h d) -> p h d", d=HD)
        xtm3 = xtmA[par][:].rearrange("p (h d) -> p h d", d=HD)
        ve.tensor_tensor(xdd3, xtm3, dtT_ap, ALU.mult)
        ev_xdd = B.ev(ve.tensor_tensor(xdd3, xdd3, dec_ap, ALU.mult), "ve")
        ev_xuse[par] = ev_xdd
        B.wait(gp, "gp", ev_p)
        ev_spill = B.dev(gp.dma_start(out=p_h.ap()[c], in_=P_s[:, :]), "sp")
        for sb_i in range(4):
            spar = sb_i % 2
            B.wait(te, "te", ev_xdd, ev_b, sps_last[spar])
            for hh in range(16):
                h = sb_i * 16 + hh
                g = h // 8
                ins = te.matmul(spsA[spar][:, hh * HD:(hh + 1) * HD],
                                btm[par][:, g * DS:(g + 1) * DS],
                                xdd[:, h * HD:(h + 1) * HD], start=True, stop=True)
            ev_mm = B.ev(ins, "te")
            ev_buse[par] = ev_mm
            ev_xdduse = ev_mm
            sl = slice(sb_i * 1024, (sb_i + 1) * 1024)
            alx_ap = bass.AP(alx_s, sb_i * 16, [[NH, 128], [1, 16], [0, HD]])
            B.wait(ve, "ve", ev_mm, ev_alx, ev_spill)
            ve.tensor_tensor(Ptmp[:].rearrange("p (h d) -> p h d", d=HD)[:, sb_i * 16:(sb_i + 1) * 16, :],
                             P_s[:].rearrange("p (h d) -> p h d", d=HD)[:, sb_i * 16:(sb_i + 1) * 16, :],
                             alx_ap, ALU.mult)
            ins = ve.tensor_add(P_s[:, sl], Ptmp[:, sl], spsA[spar][:, :])
            sps_last[spar] = ev_p = B.ev(ins, "ve")

    ev_msk = B.ev(ve.tensor_scalar_mul(Ptmp[:, :], P_s[:, :], smask_s[:, :]), "ve")
    B.wait(gp, "gp", ev_msk)
    ev_in = B.dev(gp.dma_start(out=ccin.ap(), in_=Ptmp[:, :]), "cci")
    B.wait(gp, "gp", ev_in)
    ev_cc = B.ev(gp.collective_compute(
        "AllReduce", ALU.add, replica_groups=[[0, 1], [2, 3], [4, 5], [6, 7]],
        ins=[ccin.ap().opt()], outs=[ccout.ap().opt()]), "cc")
    B.barrier()
    p3.close()
    if phases <= 3:
        B.wait_all_dma(sy)
        B.wait(sy, "sy", ev_cc)
        return nc, B

    p4 = contextlib.ExitStack()
    xtmB = [B.sb(f"xtmB{i}", [128, DI], BF16, p4) for i in range(2)]
    pcb = [B.sb(f"pcb{i}", [128, DI], BF16, p4) for i in range(2)]
    bcs = [B.sb(f"bcs{i}", [128, NG * CH], BF16, p4) for i in range(2)]
    ccs = [B.sb(f"ccs{i}", [128, NG * CH], BF16, p4) for i in range(2)]
    bigb = [B.sb(f"bigb{i}", [128, 32 * CH], F32, p4) for i in range(2)]
    rhs2 = B.sb("rhs2", [128, DI], BF16, p4)
    xdt = B.sb("xdt", [128, DI], BF16, p4)
    p0b = B.sb("p0b", [128, DI], BF16, p4)
    gcb_s = B.sb("gcb_s", [128, NH], F32, p4)
    dcx_s = B.sb("dcx_s", [128, NH], F32, p4)
    ych = [B.sb(f"ych{i}", [128, DI], BF16, p4) for i in range(2)]
    expb = [B.sb(f"expb{i}", [128, 32 * CH], BF16, p4) for i in range(2)]
    gb = [B.sb(f"gb{i}", [128, CH], BF16, p4) for i in range(2)]
    m8 = [B.sb(f"m8_{i}", [128, 8 * CH], BF16, p4) for i in range(2)]
    tmp8 = B.sb("tmp8", [128, 512], BF16, p4)
    gps = [B.ps(f"gps{i}", [128, CH], F32, p4) for i in range(2)]
    yps = [B.ps(f"yps{i}", [128, 512], F32, p4) for i in range(2)]

    B.wait(sy, "sy", ev_cc, ("sp", B.cnt.get("sp", 0)))
    ev_r = B.dev(sy.dma_start(out=p0b[:, :], in_=ccout.ap()), "p0r")
    B.wait(ve, "ve", ev_r)
    ev_p0 = B.ev(ve.tensor_scalar_mul(p0b[:, :], p0b[:, :], rmask_s[:, :]), "ve")

    ev_xuse = [None, None]
    ev_pcuse = [None, None]
    ev_bcuse = [None, None]
    ev_ccuse = [None, None]
    ev_bg_sc = [None, None]
    ev_bg_ve = [None, None]
    ev_expb_use = [None, None]
    ev_yuse = [None, None]
    gps_free = [None, None]
    gb_free = [None, None]
    m8_free = [None, None]
    yps_free = [None, None]
    ev_r2use = None
    ev_xdtuse = None
    ev_gcbuse = None
    ev_r2prev = None

    for c in range(NC_CH):
        par = c % 2
        B.wait(sy, "sy", ev_xuse[par])
        ev_x = B.dev(sy.dma_start(out=xtmB[par][:, :], in_=xpart_h.ap()[:, c * CH:(c + 1) * CH],
                                  transpose=True), f"xb{par}")
        B.wait(sy, "sy", ev_pcuse[par])
        ev_pc = B.dev(sy.dma_start(out=pcb[par][:, :], in_=p_h.ap()[c]), f"pb{par}")
        B.wait(sy, "sy", ev_bcuse[par])
        ev_bc = B.dev(sy.dma_start(
            out=bcs[par][:].rearrange("p (g s) -> p g s", s=CH),
            in_=b_h.ap().rearrange("(g p) t -> p g t", p=128)[:, :, c * CH:(c + 1) * CH]), f"bk{par}")
        B.wait(sy, "sy", ev_ccuse[par])
        ev_cc2 = B.dev(sy.dma_start(
            out=ccs[par][:].rearrange("p (g s) -> p g s", s=CH),
            in_=c_h.ap().rearrange("(g p) t -> p g t", p=128)[:, :, c * CH:(c + 1) * CH]), f"ck{par}")
        src = bass.AP(gcs_h, c * NH, [[0, 128], [1, NH]])
        B.wait(gp, "gp", ev_gcsh if c == 0 else None, ev_gcbuse)
        ev_gc = B.dev(gp.dma_start(out=gcb_s[:, :], in_=src), "gcb")
        B.wait(sc, "sc", ev_gc, ev_r2prev)
        ev_dc = B.ev(sc.activation(out=dcx_s[:, :], in_=gcb_s[:, :], func=AF.Exp), "sc")
        ev_gcbuse = ev_dc
        dc_ap = bass.AP(dcx_s, 0, [[NH, 128], [1, NH], [0, HD]])
        B.wait(ve, "ve", ev_dc, ev_p0, ev_r2use)
        ev_r2 = B.ev(ve.tensor_tensor(rhs2[:].rearrange("p (h d) -> p h d", d=HD),
                                      p0b[:].rearrange("p (h d) -> p h d", d=HD), dc_ap, ALU.mult), "ve")
        ev_r2prev = ev_r2
        dtT_ap = bass.AP(dtT_s, c * NH, [[NC_CH * NH, 128], [1, NH], [0, HD]])
        B.wait(ve, "ve", ev_x, ev_xdtuse)
        ev_xd = B.ev(ve.tensor_tensor(xdt[:].rearrange("p (h d) -> p h d", d=HD),
                                      xtmB[par][:].rearrange("p (h d) -> p h d", d=HD),
                                      dtT_ap, ALU.mult), "ve")
        ev_bg = [None, None]
        penrep = bass.AP(pen_s, 0, [[128, 128], [0, 32], [1, CH]])
        for half in range(2):
            B.wait(gp, "gp", ev_bg_sc[half], ev_bg_ve[half])
            src1 = bass.AP(csr_h, (c * NH + half * 32) * CH, [[0, 128], [1, 32 * CH]])
            ev_ld = B.dev(gp.dma_start(out=bigb[half][:, :], in_=src1), f"bg{half}")
            B.wait(ve, "ve", ev_ld, ev_bg_sc[half])
            ev_bg[half] = B.ev(ve.tensor_tensor(bigb[half][:].rearrange("p (h l) -> p h l", l=CH),
                                                bigb[half][:].rearrange("p (h l) -> p h l", l=CH),
                                                penrep, ALU.add), "ve")
            ev_bg_ve[half] = ev_bg[half]
        ev_y = None
        for g in range(NG):
            half = g // 4
            gpar = g % 2
            gsl = slice(g * CH, (g + 1) * CH)
            B.wait(te, "te", ev_bc, ev_cc2, gps_free[gpar])
            ev_sc_mm = B.ev(te.matmul(gps[gpar][:, :], bcs[par][:, gsl],
                                      ccs[par][:, gsl], start=True, stop=True), "te")
            ev_bcuse[par] = ev_sc_mm
            B.wait(sc, "sc", ev_sc_mm, gb_free[gpar])
            ev_gb = B.ev(sc.copy(gb[gpar][:, :], gps[gpar][:, :]), "sc")
            gps_free[gpar] = ev_gb
            B.wait(sc, "sc", ev_bg[half], ev_expb_use[half], ev_csT if c == 0 else None)
            for hh8 in range(8):
                h = g * 8 + hh8
                slot = h % 32
                ins = sc.activation(out=expb[half][:, slot * CH:(slot + 1) * CH],
                                    in_=bigb[half][:, slot * CH:(slot + 1) * CH],
                                    func=AF.Exp, bias=negcsT_s[:, c * NH + h: c * NH + h + 1])
            ev_exp8 = B.ev(ins, "sc")
            ev_bg_sc[half] = ev_exp8
            grep_ap = bass.AP(gb[gpar], 0, [[CH, 128], [0, 8], [1, CH]])
            B.wait(ve, "ve", ev_gb, ev_exp8, m8_free[gpar])
            ev_m = B.ev(ve.tensor_tensor(m8[gpar][:].rearrange("p (h l) -> p h l", l=CH),
                                         grep_ap,
                                         expb[half][:].rearrange("p (h l) -> p h l", l=CH)[:, (g % 4) * 8:(g % 4) * 8 + 8, :],
                                         ALU.mult), "ve")
            gb_free[gpar] = ev_m
            ev_expb_use[half] = ev_m
            B.wait(te, "te", ev_pc, ev_r2, yps_free[gpar])
            for hh8 in range(8):
                h = g * 8 + hh8
                te.matmul(yps[gpar][:, hh8 * HD:(hh8 + 1) * HD], ccs[par][:, gsl],
                          pcb[par][:, h * HD:(h + 1) * HD], start=(hh8 == 0), stop=False)
                ins = te.matmul(yps[gpar][:, hh8 * HD:(hh8 + 1) * HD], ccs[par][:, gsl],
                                rhs2[:, h * HD:(h + 1) * HD], start=False, stop=False)
            ev_yo = B.ev(ins, "te")
            ev_pcuse[par] = ev_yo
            ev_ccuse[par] = ev_yo
            ev_r2use = ev_yo
            ecs_ap = bass.AP(expcsT_s, c * NH + g * 8, [[NC_CH * NH, 128], [1, 8], [0, HD]])
            B.wait(ve, "ve", ev_yo)
            ev_sc_y = B.ev(ve.tensor_tensor(yps[gpar][:].rearrange("p (h d) -> p h d", d=HD),
                                            yps[gpar][:].rearrange("p (h d) -> p h d", d=HD),
                                            ecs_ap, ALU.mult), "ve")
            B.wait(te, "te", ev_m, ev_xd, ev_sc_y)
            for hh8 in range(8):
                h = g * 8 + hh8
                ins = te.matmul(yps[gpar][:, hh8 * HD:(hh8 + 1) * HD],
                                m8[gpar][:, hh8 * CH:(hh8 + 1) * CH],
                                xdt[:, h * HD:(h + 1) * HD], start=False, stop=True)
            ev_yd = B.ev(ins, "te")
            ev_xdtuse = ev_yd
            m8_free[gpar] = ev_yd
            drep_ap = bass.AP(drep_s, g * 8, [[NH, 128], [1, 8], [0, HD]])
            B.wait(ve, "ve", ev_yd, ev_yuse[par] if g == 0 else None)
            ve.tensor_tensor(tmp8[:].rearrange("p (h d) -> p h d", d=HD),
                             xtmB[par][:].rearrange("p (h d) -> p h d", d=HD)[:, g * 8:(g + 1) * 8, :],
                             drep_ap, ALU.mult)
            ins = ve.tensor_add(ych[par][:, g * 512:(g + 1) * 512], tmp8[:, :], yps[gpar][:, :])
            ev_y = B.ev(ins, "ve")
            yps_free[gpar] = ev_y
        ev_xuse[par] = ev_y
        B.wait(sy, "sy", ev_y)
        ev_yuse[par] = B.dev(sy.dma_start(out=y_h.ap()[c * CH:(c + 1) * CH, :], in_=ych[par][:, :]), f"yo{par}")
        if "bigb_tap" in B.taps and c == 7:
            bigb_tap = B.dram("bigb_tap", [128, NH * CH], F32, tap=True)
            B.wait(sy, "sy", B.now("ve"))
            B.dev(sy.dma_start(out=bigb_tap.ap()[:, 0:32 * CH], in_=bigb[0][:, :]), "tap")
            B.dev(sy.dma_start(out=bigb_tap.ap()[:, 32 * CH:], in_=bigb[1][:, :]), "tap")
        if "r2_tap" in B.taps and c == 7:
            r2_tap = B.dram("r2_tap", [128, DI], BF16, tap=True)
            p0_tap = B.dram("p0_tap", [128, DI], BF16, tap=True)
            dc_tap = B.dram("dc_tap", [128, NH], F32, tap=True)
            B.wait(sy, "sy", B.now("ve"))
            B.dev(sy.dma_start(out=r2_tap.ap(), in_=rhs2[:, :]), "tap")
            B.dev(sy.dma_start(out=p0_tap.ap(), in_=p0b[:, :]), "tap")
            B.dev(sy.dma_start(out=dc_tap.ap(), in_=dcx_s[:, :]), "tap")

    B.barrier()
    p4.close()
    if phases <= 4:
        B.wait_all_dma(sy)
        return nc, B

    p5 = contextlib.ExitStack()
    p5ps = contextlib.ExitStack()
    yfm = B.sb("yfm", [128, 32 * T], BF16, p5)
    zw = [B.sb(f"zw{i}", [128, 16 * 128], BF16, p5ps) for i in range(2)]
    zs = [B.sb(f"zs{i}", [128, T], BF16, p5ps) for i in range(2)]
    sqt = B.sb("sqt", [128, T], BF16, p5ps)
    v_s = B.sb("v_s", [1, T], F32, p5ps)
    rstd8 = B.sb("rstd8", [NG, T], F32, p5ps)
    rb = [B.sb(f"rb{i}", [128, T], F32, p5ps) for i in range(2)]
    zp = [B.ps(f"zp{i}", [128, 512], F32, p5ps) for i in range(4)]
    stp = B.ps("stp", [1, 1024], F32, p5ps)

    zw_use = [None, None]
    zs_use = [None, None]
    zp_last = [None, None]
    rb_use = [None, None]
    rbg_use = [None, None]
    ev_sqtuse = None
    ev_gate = {}
    ev_norm = {}
    ev_ytgate = {}
    for zc in range(32):
        par = zc % 2
        if zc == 0:
            B.wait(sy, "sy", ("yo0", B.cnt.get("yo0", 0)), ("yo1", B.cnt.get("yo1", 0)))
        B.wait(sy, "sy", ev_ytgate.get(zc - 2))
        ev_yt = B.dev(sy.dma_start(out=yfm[:, zc * T:(zc + 1) * T],
                                   in_=y_h.ap()[:, zc * 128:(zc + 1) * 128], transpose=True), f"yt{par}")
        src = bass.AP(t["winT"], 6208 + zc * 128, [[10304, 128], [128 * 10304, 16], [1, 128]])
        B.wait(gp, "gp", zw_use[par])
        ev_w = B.dev(gp.dma_start(out=zw[par][:].rearrange("p (k f) -> p k f", f=128), in_=src), f"zw{par}")
        B.wait(te, "te", ev_w, zp_last[par])
        zpset = zp[par * 2: par * 2 + 2]
        for k in range(16):
            for nt in range(2):
                ins = te.matmul(zpset[nt][:, :], zw[par][:, k * 128:(k + 1) * 128],
                                xfm_s[:, k * TCOL + 3 + nt * 512: k * TCOL + 3 + (nt + 1) * 512],
                                start=(k == 0), stop=(k == 15))
        ev_mm = B.ev(ins, "te")
        zw_use[par] = ev_mm
        ev_sl = None
        for nt in range(2):
            B.wait(sc, "sc", ev_mm, zs_use[par] if nt == 0 else None)
            ev_sl = B.ev(sc.activation(out=zs[par][:, nt * 512:(nt + 1) * 512],
                                       in_=zpset[nt][:, :], func=AF.Silu), "sc")
        zp_last[par] = ev_sl
        B.wait(ve, "ve", ev_sl, ev_yt)
        ev_g = B.ev(ve.tensor_mul(yfm[:, zc * T:(zc + 1) * T], yfm[:, zc * T:(zc + 1) * T],
                                  zs[par][:, :]), "ve")
        ev_ytgate[zc] = ev_g
        zs_use[par] = ev_g
        ev_gate[zc] = ev_g
        B.wait(sc, "sc", ev_g, ev_sqtuse)
        ev_sq = B.ev(sc.activation(out=sqt[:, :], in_=yfm[:, zc * T:(zc + 1) * T], func=AF.Square), "sc")
        B.wait(te, "te", ev_sq)
        for nt in range(2):
            ins = te.matmul(stp[:, nt * 512:(nt + 1) * 512], ones_s[:, :],
                            sqt[:, nt * 512:(nt + 1) * 512], start=(zc % 4 == 0), stop=(zc % 4 == 3),
                            skip_group_check=True)
        ev_st = B.ev(ins, "te")
        ev_sqtuse = ev_st
        if zc % 4 == 3:
            gidx = zc // 4
            gpar = gidx % 2
            B.wait(ve, "ve", ev_st, rbg_use[gpar])
            ev_v = B.ev(ve.tensor_scalar(v_s[:, :], stp[:, :], 1.0 / 512, EPS, ALU.mult, ALU.add), "ve")
            B.wait(sc, "sc", ev_v)
            sc.activation(out=v_s[:, :], in_=v_s[:, :], func=AF.Ln)
            ev_rs = B.ev(sc.activation(out=v_s[:, :], in_=v_s[:, :],
                                       func=AF.Exp, scale=-0.5), "sc")
            B.wait(sy, "sy", ev_rs, rbg_use[gpar])
            ev_rh = B.dev(sy.dma_start(out=rstd_h.ap()[gidx:gidx + 1, :],
                                       in_=v_s[:, :]), f"rs{gpar}")
            B.wait(gp, "gp", ev_rh, rb_use[gpar])
            ev_rb = B.dev(gp.dma_start(out=rb[gpar][:, :],
                                       in_=bass.AP(rstd_h, gidx * T, [[0, 128], [1, T]])), f"rb{gpar}")
            rbg_use[gpar] = ev_rb
            for zc2 in range(gidx * 4, gidx * 4 + 4):
                B.wait(ve, "ve", ev_rb, ev_gate[zc2], ev_consts if gidx == 0 else None)
                ins = ve.scalar_tensor_tensor(yfm[:, zc2 * T:(zc2 + 1) * T], yfm[:, zc2 * T:(zc2 + 1) * T],
                                              gcol_s[:, zc2:zc2 + 1], rb[gpar][:, :], ALU.mult, ALU.mult)
                ev_norm[zc2] = B.ev(ins, "ve")
            rb_use[gpar] = ev_norm[gidx * 4 + 3]
    if "yn_h" in B.taps:
        B.wait(sy, "sy", B.now("ve"))
        for zc in range(32):
            B.dev(sy.dma_start(out=yn_h.ap()[zc * 128:(zc + 1) * 128, :],
                               in_=yfm[:, zc * T:(zc + 1) * T]), "tap")
    B.barrier()
    p5ps.close()
    if phases <= 5:
        B.wait_all_dma(sy)
        return nc, B

    p6 = contextlib.ExitStack()
    r1 = B.sb("r1", [128, 8 * DM], F32, p6)
    wo = [B.sb(f"wo{i}", [128, 512], BF16, p6) for i in range(2)]
    xrs = [B.sb(f"xrs{i}", [128, 512], F32, p6) for i in range(2)]
    mcol = B.sb("mcol", [128, 1], F32, p6)
    scol = B.sb("scol", [128, 1], F32, p6)
    vcol = B.sb("vcol", [128, 1], F32, p6)
    tcol = B.sb("tcol", [128, 1], F32, p6)
    snk = B.sb("snk", [128, DM], BF16, p6)
    x1t = B.sb("x1t", [128, DM], BF16, p6)
    x1lo = B.sb("x1lo", [128, DM], BF16, p6)
    op_ps = [B.ps(f"op_ps{i}", [128, 512], F32, p6) for i in range(8)]

    wo_use = [None, None]
    xrs_use = [None, None]
    ev_r1 = {}
    for dmq in range(4):
        ev_mm = None
        for chx in range(32):
            par = chx % 2
            src = bass.AP(t["woutT"], chx * 128 * DM + dmq * 512, [[DM, 128], [1, 512]])
            B.wait(gp, "gp", wo_use[par])
            ev_w = B.dev(gp.dma_start(out=wo[par][:, :], in_=src), f"wo{par}")
            B.wait(te, "te", ev_w)
            for tt in range(8):
                if chx == 0:
                    B.wait(te, "te", ev_r1.get((dmq - 1, tt)), ev_norm.get(31) if dmq == 0 else None)
                ins = te.matmul(op_ps[tt][:, :], yfm[:, chx * T + tt * 128: chx * T + (tt + 1) * 128],
                                wo[par][:, :], start=(chx == 0), stop=(chx == 31))
            ev_chx = B.ev(ins, "te")
            wo_use[par] = ev_chx
            if chx == 31:
                ev_mm = ev_chx
        for tt in range(8):
            par = tt % 2
            B.wait(sy, "sy", xrs_use[par])
            ev_xr = B.dev(sy.dma_start(out=xrs[par][:, :],
                                       in_=t["xres"].ap()[tt * 128:(tt + 1) * 128,
                                                          dmq * 512:(dmq + 1) * 512]), f"xr{par}")
            B.wait(ve, "ve", ev_mm, ev_xr)
            ev = B.ev(ve.tensor_add(r1[:, tt * DM + dmq * 512: tt * DM + (dmq + 1) * 512],
                                    op_ps[tt][:, :], xrs[par][:, :]), "ve")
            ev_r1[(dmq, tt)] = ev
            xrs_use[par] = ev
    x1t_use = None
    x1lo_use = None
    ev_colsuse = None
    for tt in range(8):
        r1t = r1[:, tt * DM:(tt + 1) * DM]
        B.wait(sc, "sc", ev_r1[(3, tt)], ev_colsuse)
        sc.activation(out=snk[:, :], in_=r1t, func=AF.Copy, accum_out=mcol[:, :])
        sc.activation(out=snk[:, :], in_=r1t, func=AF.Square, accum_out=scol[:, :])
        sc.mul(mcol[:, :], mcol[:, :], 1.0 / DM)
        sc.activation(out=vcol[:, :], in_=scol[:, :], func=AF.Identity, scale=1.0 / DM, bias=eps_s[:, :])
        sc.activation(out=tcol[:, :], in_=mcol[:, :], func=AF.Square)
        sc.activation(out=vcol[:, :], in_=tcol[:, :], func=AF.Identity, scale=-1.0, bias=vcol[:, :])
        sc.activation(out=vcol[:, :], in_=vcol[:, :], func=AF.Ln)
        ev_rs = B.ev(sc.activation(out=vcol[:, :], in_=vcol[:, :], func=AF.Exp, scale=-0.5), "sc")
        B.wait(ve, "ve", ev_rs, x1t_use)
        ev_x1 = B.ev(ve.tensor_scalar(x1t[:, :], r1t, mcol[:, :], vcol[:, :],
                                      ALU.subtract, ALU.mult), "ve")
        ev_x1f = B.ev(ve.tensor_scalar(r1t, r1t, mcol[:, :], vcol[:, :],
                                       ALU.subtract, ALU.mult), "ve")
        B.wait(ve, "ve", x1lo_use)
        ev_lo = B.ev(ve.tensor_sub(x1lo[:, :], r1t, x1t[:, :]), "ve")
        ev_colsuse = ev_lo
        B.wait(sy, "sy", ev_x1, ev_x1f, ev_lo)
        x1t_use = B.dev(sy.dma_start(out=x1_h.ap()[tt * 128:(tt + 1) * 128, :], in_=x1t[:, :]), "x1t")
        x1lo_use = B.dev(sy.dma_start(out=x1lo_h.ap()[tt * 128:(tt + 1) * 128, :], in_=x1lo[:, :]), "x1l")
        B.dev(sy.dma_start(out=x1f_h.ap()[tt * 128:(tt + 1) * 128, :], in_=r1t), "x1f")
    B.barrier()
    p6.close()
    p5.close()
    main_scope.close()
    if phases <= 6:
        B.wait_all_dma(sy)
        return nc, B

    p78 = contextlib.ExitStack()
    o8_s = B.sb("o8_s", [8, DM], F32, p78)
    p7 = contextlib.ExitStack()
    x1fm = B.sb("x1fm", [128, 16 * T], BF16, p7)
    x1lofm = B.sb("x1lofm", [128, 16 * T], BF16, p7)
    h1 = B.sb("h1", [128, 16 * T], BF16, p7)
    h2 = B.sb("h2", [128, 8 * DM], BF16, p7)
    wf = [B.sb(f"wf{i}", [128, 2048], BF16, p7) for i in range(2)]
    wrhl = B.sb("wrhl", [128, 16 * 16], BF16, p7)
    wrhi = B.sb("wrhi", [128, 16 * 8], BF16, p7)
    rt_sb = B.sb("rt_sb", [16, 1024], F32, p7)
    rt_sb2 = B.sb("rt_sb2", [8, 1024], F32, p7)
    lgt = B.sb("lgt", [128, 24], F32, p7)
    lsum = B.sb("lsum", [128, 8], F32, p7)
    e8 = B.sb("e8", [128, 8], F32, p7)
    msk8 = B.sb("msk8", [128, 8], F32, p7)
    tm8 = B.sb("tm8", [128, 8], F32, p7)
    m1c = B.sb("m1c", [128, 1], F32, p7)
    nm1c = B.sb("nm1c", [128, 1], F32, p7)
    m2c = B.sb("m2c", [128, 1], F32, p7)
    e2c = B.sb("e2c", [128, 1], F32, p7)
    rdc = B.sb("rdc", [128, 1], F32, p7)
    wgate = B.sb("wgate", [128, 8 * NC_CH], BF16, p7)
    gl_s = B.sb("gl_s", [8, DM], F32, p7)
    gs_s = B.sb("gs_s", [8, DM], F32, p7)
    gT = B.sb("gT", [128, 16 * 8], BF16, p7)

    ev_xf = {}
    ev_xlo = {}
    B.wait(sy, "sy", ("x1t", B.cnt.get("x1t", 0)), ("x1l", B.cnt.get("x1l", 0)))
    for k in range(16):
        ev_xf[k] = B.dev(sy.dma_start(out=x1fm[:, k * T:(k + 1) * T],
                                      in_=x1_h.ap()[:, k * 128:(k + 1) * 128], transpose=True), "xf")
        ev_xlo[k] = B.dev(sy.dma_start(out=x1lofm[:, k * T:(k + 1) * T],
                                       in_=x1lo_h.ap()[:, k * 128:(k + 1) * 128], transpose=True), "xf")
    B.wait(ve, "ve", ev_consts)
    wr3 = wr_s[:].rearrange("p (k e) -> p k e", e=8)
    wrhi3 = wrhi[:].rearrange("p (k e) -> p k e", e=8)
    wrhl3 = wrhl[:].rearrange("p (k e) -> p k e", e=16)
    B.vchain(ve.tensor_copy(wrhi3, wr3))
    ve.tensor_copy(wrhl3[:, :, 0:8], wrhi3)
    ev_wr = B.ev(ve.tensor_sub(wrhl3[:, :, 8:16], wr3, wrhi3), "ve")

    p7r = contextlib.ExitStack()
    rtA = B.ps("rtA", [16, 512], F32, p7r)
    rtB = B.ps("rtB", [8, 512], F32, p7r)
    trp = B.ps("trp", [128, 128], F32, p7r)
    ev_wguse = None
    for blk in range(2):
        B.wait(te, "te", ev_wr, ev_xf[15], ev_xlo[15])
        for k in range(16):
            te.matmul(rtA[:, :], wrhl[:, k * 16:(k + 1) * 16],
                      x1fm[:, k * T + blk * 512: k * T + (blk + 1) * 512],
                      start=(k == 0), stop=(k == 15))
        for k in range(16):
            ins = te.matmul(rtB[:, :], wrhi[:, k * 8:(k + 1) * 8],
                            x1lofm[:, k * T + blk * 512: k * T + (blk + 1) * 512],
                            start=(k == 0), stop=(k == 15))
        ev_rt = B.ev(ins, "te")
        B.wait(sc, "sc", ev_rt)
        sc.copy(rt_sb[0:16, blk * 512:(blk + 1) * 512], rtA[:, :])
        ev_cp = B.ev(sc.copy(rt_sb2[0:8, blk * 512:(blk + 1) * 512], rtB[:, :]), "sc")
        for q in range(4):
            tt = blk * 4 + q
            B.wait(te, "te", ev_cp, ev_wguse)
            te.transpose(trp[:, 0:16], rt_sb[0:16, blk * 512 + q * 128: blk * 512 + (q + 1) * 128],
                         ident_s[0:16, 0:16])
            ev_t1 = B.ev(te.transpose(trp[:, 16:24], rt_sb2[0:8, blk * 512 + q * 128: blk * 512 + (q + 1) * 128],
                                      ident_s[0:8, 0:8]), "te")
            B.wait(sc, "sc", ev_t1)
            ev_lg = B.ev(sc.copy(lgt[:, :], trp[:, 0:24]), "sc")
            B.wait(ve, "ve", ev_lg)
            B.vchain(ve.tensor_add(lsum[:, :], lgt[:, 0:8], lgt[:, 8:16]))
            B.vchain(ve.tensor_add(lsum[:, :], lsum[:, :], lgt[:, 16:24]))
            B.vchain(ve.reduce_max(m1c[:, :], lsum[:, :], axis=mybir.AxisListType.X))
            B.vchain(ve.tensor_scalar(tm8[:, :], lsum[:, :], m1c[:, :], -1e6, ALU.is_ge, ALU.mult))
            B.vchain(ve.tensor_add(tm8[:, :], tm8[:, :], lsum[:, :]))
            B.vchain(ve.reduce_max(m2c[:, :], tm8[:, :], axis=mybir.AxisListType.X))
            ev_ve1 = B.vchain(ve.tensor_scalar_mul(nm1c[:, :], m1c[:, :], -1.0))
            B.wait(sc, "sc", ev_ve1)
            sc.activation(out=e8[:, :], in_=lsum[:, :], func=AF.Exp, bias=nm1c[:, :])
            ev_e2 = B.ev(sc.activation(out=e2c[:, :], in_=m2c[:, :], func=AF.Exp, bias=nm1c[:, :]), "sc")
            B.wait(ve, "ve", ev_e2)
            B.vchain(ve.tensor_scalar_add(e2c[:, :], e2c[:, :], 1.0))
            B.vchain(ve.reciprocal(rdc[:, :], e2c[:, :]))
            B.vchain(ve.tensor_single_scalar(msk8[:, :], lsum[:, :], m2c[:, :], ALU.is_ge))
            sl = wgate[:, tt * 8:(tt + 1) * 8]
            B.vchain(ve.tensor_mul(msk8[:, :], e8[:, :], msk8[:, :]))
            ev_wg = B.ev(ve.tensor_scalar_mul(sl, msk8[:, :], rdc[:, :]), "ve")
            ev_wguse = ev_wg
    if "wg_h" in B.taps:
        B.wait(sy, "sy", B.now("ve"))
        B.dev(sy.dma_start(out=wg_h.ap(), in_=wgate[:, :]), "tap")
    B.barrier()
    p7r.close()

    p7a = contextlib.ExitStack()
    h1p = [B.ps(f"h1p{i}", [128, 512], F32, p7a) for i in range(4)]
    wf_use = [None, None]
    h1p_last = [None, None]
    for f1 in range(16):
        par = f1 % 2
        src = bass.AP(t["w1T"], f1 * 128, [[2048, 128], [128 * 2048, 16], [1, 128]])
        B.wait(gp, "gp", wf_use[par])
        ev_w = B.dev(gp.dma_start(out=wf[par][:].rearrange("p (k f) -> p k f", f=128), in_=src), f"wf{par}")
        B.wait(te, "te", ev_w, h1p_last[par])
        hpset = h1p[par * 2: par * 2 + 2]
        for k in range(16):
            for nt in range(2):
                ins = te.matmul(hpset[nt][:, :], wf[par][:, k * 128:(k + 1) * 128],
                                x1fm[:, k * T + nt * 512: k * T + (nt + 1) * 512],
                                start=(k == 0), stop=(k == 15))
        ev_mm = B.ev(ins, "te")
        wf_use[par] = ev_mm
        for nt in range(2):
            B.wait(sc, "sc", ev_mm)
            ev_sl = B.ev(sc.activation(out=h1[:, f1 * T + nt * 512: f1 * T + (nt + 1) * 512],
                                       in_=hpset[nt][:, :], func=AF.Silu), "sc")
        h1p_last[par] = ev_sl
    ev_h1 = B.now("sc")
    B.barrier()
    p7a.close()

    p7b = contextlib.ExitStack()
    h2p = [B.ps(f"h2p{i}", [128, 512], F32, p7b) for i in range(8)]
    wf_use = [None, None]
    ev_h2ev = {}
    for f2q in range(4):
        ev_mm = None
        for k in range(16):
            par = k % 2
            src = bass.AP(t["w3T"], k * 128 * 2048 + f2q * 512, [[2048, 128], [1, 512]])
            B.wait(gp, "gp", wf_use[par])
            ev_w = B.dev(gp.dma_start(out=wf[par][:, 0:512], in_=src), f"wf{par}")
            B.wait(te, "te", ev_w)
            for tt in range(8):
                if k == 0:
                    B.wait(te, "te", ev_h2ev.get((f2q - 1, tt)))
                ins = te.matmul(h2p[tt][:, :], h1[:, k * T + tt * 128: k * T + (tt + 1) * 128],
                                wf[par][:, 0:512], start=(k == 0), stop=(k == 15))
            ev_k = B.ev(ins, "te")
            wf_use[par] = ev_k
            if k == 15:
                ev_mm = ev_k
        for tt in range(8):
            B.wait(sc, "sc", ev_mm)
            ev_h2ev[(f2q, tt)] = B.ev(sc.activation(
                out=h2[:, tt * DM + f2q * 512: tt * DM + (f2q + 1) * 512],
                in_=h2p[tt][:, :], func=AF.Silu), "sc")
    B.barrier()
    p7b.close()

    p7c = contextlib.ExitStack()
    gpp = B.ps("gpp", [8, DM], F32, p7c)
    for tt in range(8):
        for q in range(4):
            ins = te.matmul(gpp[:, q * 512:(q + 1) * 512], wgate[:, tt * 8:(tt + 1) * 8],
                            h2[:, tt * DM + q * 512: tt * DM + (q + 1) * 512],
                            start=(tt == 0), stop=(tt == 7), skip_group_check=True)
    ev_g = B.ev(ins, "te")
    B.wait(sc, "sc", ev_g)
    ev_gl = B.ev(sc.copy(gl_s[:, :], gpp[:, :]), "sc")
    B.wait(gp, "gp", ev_gl)
    ev_gin = B.dev(gp.dma_start(out=gin.ap(), in_=gl_s[:, :]), "gin")
    B.wait(gp, "gp", ev_gin)
    ev_gcc = B.ev(gp.collective_compute(
        "AllReduce", ALU.add, replica_groups=[[0, 1, 2, 3, 4, 5, 6, 7]],
        ins=[gin.ap().opt()], outs=[gout.ap().opt()]), "cc")
    B.wait(sy, "sy", ev_gcc)
    ev_gs = B.dev(sy.dma_start(out=gs_s[:, :], in_=gout.ap()), "gsr")
    if "g_h" in B.taps:
        B.wait(sy, "sy", ev_gs)
        B.dev(sy.dma_start(out=g_h.ap(), in_=gs_s[:, :]), "tap")
    B.barrier()
    p7c.close()

    p7d = contextlib.ExitStack()
    trp2 = B.ps("trp2", [128, 128], F32, p7d)
    o8p = B.ps("o8p", [8, DM], F32, p7d)
    ev_gT = None
    for k in range(16):
        B.wait(te, "te", ev_gs if k == 0 else None, ev_gT)
        ev_t = B.ev(te.transpose(trp2[:, 0:8], gs_s[0:8, k * 128:(k + 1) * 128],
                                 ident_s[0:8, 0:8]), "te")
        B.wait(ve, "ve", ev_t)
        ev_gT = B.ev(ve.tensor_copy(gT[:, k * 8:(k + 1) * 8], trp2[:, 0:8]), "ve")
    wf_use = [None, None]
    ev_mm = None
    for k in range(16):
        par = k % 2
        src = bass.AP(t["w2T"], k * 128 * 2048, [[2048, 128], [1, 2048]])
        B.wait(gp, "gp", wf_use[par])
        ev_w = B.dev(gp.dma_start(out=wf[par][:, :], in_=src), f"wf{par}")
        B.wait(te, "te", ev_w, ev_gT)
        for q in range(4):
            ins = te.matmul(o8p[:, q * 512:(q + 1) * 512], gT[:, k * 8:(k + 1) * 8],
                            wf[par][:, q * 512:(q + 1) * 512], start=(k == 0), stop=(k == 15),
                            skip_group_check=True)
        ev_mm = B.ev(ins, "te")
        wf_use[par] = ev_mm
    B.wait(ve, "ve", ev_mm)
    ev_o8 = B.ev(ve.tensor_scalar_mul(o8_s[:, :], o8p[:, :], omask_s[:, :]), "ve")
    if "o8_h" in B.taps:
        B.wait(sy, "sy", ev_o8)
        B.dev(sy.dma_start(out=o8_h.ap(), in_=o8_s[:, :]), "tap")
    B.barrier()
    p7d.close()
    p7.close()

    p8 = contextlib.ExitStack()
    xtf = [B.sb(f"xtf{i}", [128, DM], F32, p8) for i in range(2)]
    mcol = B.sb("mcol8", [128, 1], F32, p8)
    scol = B.sb("scol8", [128, 1], F32, p8)
    vcol = B.sb("vcol8", [128, 1], F32, p8)
    tcol = B.sb("tcol8", [128, 1], F32, p8)
    snk2 = B.sb("snk2", [128, DM], BF16, p8)
    xtf_use = [None, None]
    ev_colsuse = None
    for tt in range(8):
        par = tt % 2
        if tt == 0:
            B.wait(sy, "sy", ("x1f", B.cnt.get("x1f", 0)))
        B.wait(sy, "sy", xtf_use[par])
        ev_x = B.dev(sy.dma_start(out=xtf[par][:, :],
                                  in_=x1f_h.ap()[tt * 128:(tt + 1) * 128, :]), f"x8{par}")
        if tt == 0:
            B.wait(ve, "ve", ev_x, ev_o8)
            ev_x = B.ev(ve.tensor_add(xtf[par][0:8, :], xtf[par][0:8, :], o8_s[:, :]), "ve")
        B.wait(sc, "sc", ev_x, ev_colsuse)
        sc.activation(out=snk2[:, :], in_=xtf[par][:, :], func=AF.Copy, accum_out=mcol[:, :])
        sc.activation(out=snk2[:, :], in_=xtf[par][:, :], func=AF.Square, accum_out=scol[:, :])
        sc.mul(mcol[:, :], mcol[:, :], 1.0 / DM)
        sc.activation(out=vcol[:, :], in_=scol[:, :], func=AF.Identity, scale=1.0 / DM, bias=eps_s[:, :])
        sc.activation(out=tcol[:, :], in_=mcol[:, :], func=AF.Square)
        sc.activation(out=vcol[:, :], in_=tcol[:, :], func=AF.Identity, scale=-1.0, bias=vcol[:, :])
        sc.activation(out=vcol[:, :], in_=vcol[:, :], func=AF.Ln)
        ev_rs = B.ev(sc.activation(out=vcol[:, :], in_=vcol[:, :], func=AF.Exp, scale=-0.5), "sc")
        B.wait(ve, "ve", ev_rs)
        ev_o = B.ev(ve.tensor_scalar(xtf[par][:, :], xtf[par][:, :], mcol[:, :], vcol[:, :],
                                     ALU.subtract, ALU.mult), "ve")
        ev_colsuse = ev_o
        B.wait(sy, "sy", ev_o)
        xtf_use[par] = B.ev(sy.dma_start(out=out_ext.ap()[tt * 128:(tt + 1) * 128, :],
                                         in_=xtf[par][:, :]), "out", 16)
    sy.wait_ge(B.sems["out"], B.cnt["out"])
    B.wait_all_dma(sy)
    return nc, B


def prepare_inputs(inputs):
    x = np.asarray(inputs["L0_attn_hidden_states"], np.float32)
    W_in = np.asarray(inputs["L0_attn_W_in"], np.float32)
    conv_w = np.asarray(inputs["L0_attn_conv_w"], np.float32)
    conv_b = np.asarray(inputs["L0_attn_conv_b"], np.float32)
    A = np.asarray(inputs["L0_attn_A"], np.float32)
    D = np.asarray(inputs["L0_attn_D"], np.float32)
    dt_bias = np.asarray(inputs["L0_attn_dt_bias"], np.float32)
    norm_w = np.asarray(inputs["L0_attn_norm_weight"], np.float32)
    W_out = np.asarray(inputs["L0_attn_W_out"], np.float32)
    w1 = np.asarray(inputs["L0_ffn_w1"], np.float32)
    w3 = np.asarray(inputs["L0_ffn_w3"], np.float32)
    w2 = np.asarray(inputs["L0_ffn_w2"], np.float32)
    wr = np.asarray(inputs["L0_ffn_wr"], np.float32)

    perm = np.concatenate([np.arange(4096, 10304), np.arange(0, 4096)])
    winT = np.ascontiguousarray(W_in.T[:, perm])
    woutT = np.ascontiguousarray(W_out.T)
    w1T = np.ascontiguousarray(w1.T)
    w3T = np.ascontiguousarray(w3.T)
    w2T = np.ascontiguousarray(w2.T)
    wrT = np.ascontiguousarray(wr.T)
    convb = np.ascontiguousarray(conv_b.reshape(-1, 1))
    acol = np.ascontiguousarray(A.reshape(-1, 1))
    drep = np.ascontiguousarray(np.broadcast_to(D.reshape(1, -1), (128, NH)))
    dtbcol = np.ascontiguousarray(dt_bias.reshape(-1, 1))
    gcol = np.ascontiguousarray(norm_w.reshape(32, 128).T)
    ident = np.eye(128, dtype=np.float32)
    pen = np.tril(np.full((128, 128), PEN, np.float32), -1)
    onescol = np.ones((128, 1), np.float32)
    shared = dict(winT=winT, woutT=woutT, w1T=w1T, w3T=w3T, w2T=w2T, wrT=wrT,
                  convw=conv_w, convb=convb, acol=acol, drep=drep, dtbcol=dtbcol,
                  gcol=gcol, ident=ident, pen=pen, onescol=onescol)

    in_maps = []
    for core in range(NCORES):
        b, hh = divmod(core, 2)
        t0 = hh * T
        halo = (x[b, t0 - 3:t0].T if hh == 1 else np.zeros((DM, 3), np.float32))
        xfm = np.ascontiguousarray(np.concatenate([halo, x[b, t0:t0 + T].T], axis=1))
        m = dict(shared)
        m["xfm"] = xfm
        m["xres"] = np.ascontiguousarray(x[b, t0:t0 + T])
        m["sendmask"] = np.full((128, 1), 1.0 if hh == 0 else 0.0, np.float32)
        m["recvmask"] = np.full((128, 1), 1.0 if hh == 1 else 0.0, np.float32)
        m["ownermask"] = np.full((8, 1), 1.0 if core == 0 else 0.0, np.float32)
        in_maps.append(m)
    return in_maps


_BF16_KEYS = {"onescol"}


def _cast_in_maps(in_maps):
    import ml_dtypes
    out = []
    for m in in_maps:
        mm = {}
        for k, v in m.items():
            if k in _BF16_KEYS:
                mm[k] = np.ascontiguousarray(v.astype(ml_dtypes.bfloat16))
            else:
                mm[k] = np.ascontiguousarray(v)
        out.append(mm)
    return out


def kernel(**inputs):
    in_maps = _cast_in_maps(prepare_inputs(inputs))
    nc, _ = build_program(DEBUG_TAPS)
    res = run_bass_kernel_spmd(nc, in_maps, core_ids=list(range(NCORES)))
    out = np.empty((4, 2048, DM), np.float32)
    for core in range(NCORES):
        b, hh = divmod(core, 2)
        out[b, hh * T:(hh + 1) * T, :] = np.asarray(res.results[core]["out"])
    return out


# version 06
# speedup vs baseline: 1.2455x; 1.1015x over previous
import contextlib
import numpy as np
import concourse.bass as bass
import concourse.mybir as mybir
from concourse.bass_utils import run_bass_kernel_spmd

F32 = mybir.dt.float32
BF16 = mybir.dt.bfloat16
AF = mybir.ActivationFunctionType
ALU = mybir.AluOpType

NCORES = 8
T = 1024
CH = 128
NC_CH = T // CH
DM = 2048
DI = 4096
NH = 64
HD = 64
NG = 8
DS = 128
CONV_DIM = 6144
EPS = 1e-5
PEN = -30000.0
TCOL = T + 3
CSROW = T + 16

DEBUG_TAPS = ()


class Builder:
    def __init__(self, nc, taps=()):
        self.nc = nc
        self.taps = set(taps)
        self.stack = contextlib.ExitStack()
        self.sems = {}
        self.cnt = {}
        self.lastwait = {}
        for s in ("te", "ve", "sc", "cc", "out"):
            self.sems[s] = self.stack.enter_context(nc.semaphore(s))
            self.cnt[s] = 0
        self.t = {}

    def ev(self, instr, sem, n=1):
        self.cnt[sem] += n
        instr.then_inc(self.sems[sem], n)
        return (sem, self.cnt[sem])

    def now(self, sem):
        return (sem, self.cnt[sem])

    def wait(self, eng, engname, *evs):
        for e in evs:
            if e is None:
                continue
            sem, val = e
            if val <= 0:
                continue
            if self.lastwait.get((engname, sem), -1) >= val:
                continue
            eng.wait_ge(self.sems[sem], val)
            self.lastwait[(engname, sem)] = val

    def dsem(self, name):
        if name not in self.sems:
            self.sems[name] = self.stack.enter_context(self.nc.semaphore(name))
            self.cnt[name] = 0
        return name

    def dev(self, ins, sem):
        self.dsem(sem)
        return self.ev(ins, sem, 16)

    def wait_all_dma(self, eng, engname="sy"):
        for s, v in list(self.cnt.items()):
            if s in ("te", "ve", "sc", "gp"):
                continue
            self.wait(eng, engname, (s, v))

    def vchain(self, ins):
        e = self.ev(ins, "ve")
        self.nc.vector.wait_ge(self.sems["ve"], e[1])
        self.lastwait[("ve", "ve")] = e[1]
        return e

    def barrier(self):
        self.nc.all_engine_barrier()
        for s in ("te", "ve", "sc"):
            for e in ("te", "ve", "sc", "gp", "sy"):
                self.lastwait[(e, s)] = self.cnt[s]

    def sb(self, name, shape, dtype, scope=None):
        t = (scope or self.stack).enter_context(self.nc.sbuf_tensor(name, shape, dtype))
        self.t[name] = t
        return t

    def ps(self, name, shape, dtype, scope=None):
        t = (scope or self.stack).enter_context(self.nc.psum_tensor(name, shape, dtype))
        self.t[name] = t
        return t

    def dram(self, name, shape, dtype, tap=False):
        kind = "ExternalOutput" if (tap and name in self.taps) else "Internal"
        t = self.nc.dram_tensor(name, shape, dtype, kind=kind)
        self.t[name] = t
        return t


def build_program(taps=(), phases=99):
    nc = bass.Bass("TRN2", target_bir_lowering=False)
    B = Builder(nc, taps)
    t = B.t
    te, ve, sc, gp, sy = nc.tensor, nc.vector, nc.scalar, nc.gpsimd, nc.sync

    inputs = {
        "xfm": ([DM, TCOL], F32),
        "xres": ([T, DM], F32),
        "winT": ([DM, 10304], F32),
        "woutT": ([DI, DM], F32),
        "w1T": ([DM, DM], F32),
        "w3T": ([DM, DM], F32),
        "w2T": ([DM, DM], F32),
        "wrT": ([DM, 8], F32),
        "convw": ([CONV_DIM, 4], F32),
        "convb": ([CONV_DIM, 1], F32),
        "acol": ([NH, 1], F32),
        "drep": ([128, NH], F32),
        "dtbcol": ([NH, 1], F32),
        "gcol": ([128, 32], F32),
        "ident": ([128, 128], F32),
        "pen": ([128, 128], F32),
        "onescol": ([128, 1], BF16),
        "sendmask": ([128, 1], F32),
        "recvmask": ([128, 1], F32),
        "ownermask": ([8, 1], F32),
    }
    for nm, (shp, dt_) in inputs.items():
        t[nm] = nc.dram_tensor(nm, shp, dt_, kind="ExternalInput")
    out_ext = nc.dram_tensor("out", [T, DM], F32, kind="ExternalOutput")

    xpart_h = B.dram("xpart_h", [DI, T], BF16, tap=True)
    b_h = B.dram("b_h", [NG * DS, T], BF16, tap=True)
    c_h = B.dram("c_h", [NG * DS, T], BF16, tap=True)
    cs_h = B.dram("cs_h", [NH, CSROW], F32, tap=True)
    csT_h = B.dram("csT_h", [NC_CH, 128, NH], F32)
    csr_h = B.dram("csr_h", [NC_CH, NH, CH], F32)
    gcs_h = B.dram("gcs_h", [NC_CH, NH], F32, tap=True)
    p_h = B.dram("p_h", [NC_CH, 128, DI], BF16, tap=True)
    y_h = B.dram("y_h", [T, DI], BF16, tap=True)
    yn_h = B.dram("yn_h", [DI, T], BF16, tap=True)
    x1_h = B.dram("x1_h", [T, DM], BF16, tap=True)
    x1lo_h = B.dram("x1lo_h", [T, DM], BF16, tap=True)
    x1f_h = B.dram("x1f_h", [T, DM], F32, tap=True)
    rstd_h = B.dram("rstd_h", [NG, T], F32, tap=True)
    wg_h = B.dram("wg_h", [128, 8 * NC_CH], BF16, tap=True)
    g_h = B.dram("g_h", [8, DM], F32, tap=True)
    o8_h = B.dram("o8_h", [8, DM], F32, tap=True)
    h1_h = B.dram("h1_h", [DM, T], BF16, tap=True)
    h2_h = B.dram("h2_h", [T, DM], BF16, tap=True)
    flush_h = B.dram("flush_h", [128, 4], F32)
    ccin = B.dram("ccin", [128, DI], BF16)
    ccout = B.dram("ccout", [128, DI], BF16)
    gin = B.dram("gin", [8, DM], F32)
    gout = B.dram("gout", [8, DM], F32)

    dt_s = B.sb("dt_s", [NH, T], F32)
    adt_s = B.sb("adt_s", [NH, T], F32)
    cs_s = B.sb("cs_s", [NH, CSROW], F32)
    dtT_s = B.sb("dtT_s", [128, NC_CH * NH], BF16)
    csT_s = B.sb("csT_s", [128, NC_CH * NH], F32)
    negcsT_s = B.sb("negcsT_s", [128, NC_CH * NH], F32)
    expcsT_s = B.sb("expcsT_s", [128, NC_CH * NH], F32)
    ident_s = B.sb("ident_s", [128, 128], F32)
    convw_s = B.sb("convw_s", [128, 48 * 4], F32)
    convb_s = B.sb("convb_s", [128, 48], F32)
    acol_s = B.sb("acol_s", [NH, 1], F32)
    drep_s = B.sb("drep_s", [128, NH], F32)
    dtb_s = B.sb("dtb_s", [NH, 1], F32)
    gcol_s = B.sb("gcol_s", [128, 32], F32)
    ones_s = B.sb("ones_s", [128, 1], BF16)
    smask_s = B.sb("smask_s", [128, 1], F32)
    rmask_s = B.sb("rmask_s", [128, 1], F32)
    omask_s = B.sb("omask_s", [8, 1], F32)
    wr_s = B.sb("wr_s", [128, 16 * 8], F32)
    eps_s = B.sb("eps_s", [128, 1], F32)
    pen_s = B.sb("pen_s", [128, 128], F32)
    main_scope = contextlib.ExitStack()
    xfm_s = B.sb("xfm_s", [128, 16 * TCOL], BF16, main_scope)

    for nm, dst in (("ident", ident_s), ("pen", pen_s), ("acol", acol_s), ("drep", drep_s),
                    ("dtbcol", dtb_s), ("gcol", gcol_s), ("onescol", ones_s),
                    ("sendmask", smask_s), ("recvmask", rmask_s), ("ownermask", omask_s)):
        B.dev(sy.dma_start(out=dst[:, :], in_=t[nm].ap()), "cst")
    B.dev(sy.dma_start(out=convb_s[:, :],
                      in_=t["convb"].ap().rearrange("(c p) o -> p (c o)", p=128),
                      allow_slow_non_contiguous=True), "cst")
    B.dev(sy.dma_start(out=convw_s[:].rearrange("p (c k) -> p c k", k=4),
                      in_=t["convw"].ap().rearrange("(c p) k -> p c k", p=128)), "cst")
    B.dev(sy.dma_start(out=wr_s[:].rearrange("p (k e) -> p k e", e=8),
                      in_=t["wrT"].ap().rearrange("(k p) e -> p k e", p=128)), "cst")
    ev_consts = B.now("cst")
    B.ev(nc.vector.memset(eps_s[:, :], EPS), "ve")
    ev_xfm = B.dev(gp.dma_start(
        out=xfm_s[:].rearrange("p (k c) -> p k c", c=TCOL),
        in_=t["xfm"].ap().rearrange("(k p) c -> p k c", p=128)), "xfm")

    p1 = contextlib.ExitStack()
    wbuf = [B.sb(f"wbuf{i}", [128, 16 * 128], BF16, p1) for i in range(2)]
    cvb = [B.sb(f"cvb{i}", [128, TCOL], BF16, p1) for i in range(2)]
    cvo = [B.sb(f"cvo{i}", [128, T], BF16, p1) for i in range(2)]
    dtraw_s = B.sb("dtraw_s", [NH, TCOL], F32, p1)
    dttmp_s = B.sb("dttmp_s", [NH, T], F32, p1)
    pp = [B.ps(f"pp{i}", [128, 343], F32, p1) for i in range(6)]

    NT = ((0, 343), (343, 342), (685, 342))
    wlast_use = [None, None]
    cv_last_use = [None, None]
    cvo_last_use = [None, None]
    pp_last = [None, None]

    B.wait(te, "te", ev_xfm)
    for fo in range(49):
        par = fo % 2
        cols = 128 if fo < 48 else 64
        src = bass.AP(t["winT"], fo * 128, [[10304, 128], [128 * 10304, 16], [1, cols]])
        B.wait(gp, "gp", wlast_use[par])
        ev_w = B.dev(gp.dma_start(
            out=wbuf[par][:].rearrange("p (k f) -> p k f", f=128)[:, :, 0:cols], in_=src), f"wb{par}")
        B.wait(te, "te", ev_w, pp_last[par])
        ppset = pp[par * 3: par * 3 + 3]
        for k in range(16):
            for nt in range(3):
                c0, ncols = NT[nt]
                ins = te.matmul(ppset[nt][0:cols, 0:ncols],
                                wbuf[par][:, k * 128:k * 128 + cols],
                                xfm_s[:, k * TCOL + c0: k * TCOL + c0 + ncols],
                                start=(k == 0), stop=(k == 15))
        ev_mm = B.ev(ins, "te")
        wlast_use[par] = ev_mm
        for nt in range(3):
            c0, ncols = NT[nt]
            if fo < 48:
                B.wait(sc, "sc", ev_mm, cv_last_use[par] if nt == 0 else None)
                ev_ev = B.ev(sc.copy(cvb[par][:, c0:c0 + ncols], ppset[nt][:, 0:ncols]), "sc")
            else:
                B.wait(sc, "sc", ev_mm)
                ev_ev = B.ev(sc.copy(dtraw_s[:, c0:c0 + ncols], ppset[nt][0:64, 0:ncols]), "sc")
        pp_last[par] = ev_ev
        if fo < 48:
            B.wait(ve, "ve", ev_ev, cvo_last_use[par], ev_consts if fo == 0 else None)
            ve.tensor_scalar(cvo[par][:, :], cvb[par][:, 0:T],
                             convw_s[:, fo * 4: fo * 4 + 1],
                             convb_s[:, fo: fo + 1], ALU.mult, ALU.add)
            for k in range(1, 4):
                ins = ve.scalar_tensor_tensor(cvo[par][:, :], cvb[par][:, k:k + T],
                                              convw_s[:, fo * 4 + k: fo * 4 + k + 1],
                                              cvo[par][:, :], ALU.mult, ALU.add)
            ev_conv = B.ev(ins, "ve")
            cv_last_use[par] = ev_conv
            B.wait(sc, "sc", ev_conv)
            ev_silu = B.ev(sc.activation(out=cvo[par][:, :], in_=cvo[par][:, :], func=AF.Silu), "sc")
            B.wait(sy, "sy", ev_silu)
            if fo < 32:
                dst = xpart_h.ap()[fo * 128:(fo + 1) * 128, :]
            elif fo < 40:
                dst = b_h.ap()[(fo - 32) * 128:(fo - 31) * 128, :]
            else:
                dst = c_h.ap()[(fo - 40) * 128:(fo - 39) * 128, :]
            cvo_last_use[par] = B.dev(sy.dma_start(out=dst, in_=cvo[par][:, :]), f"cv{par}")
        else:
            B.wait(sc, "sc", ev_ev, ev_consts)
            sc.activation(out=dt_s[:, :], in_=dtraw_s[:, 3:TCOL], func=AF.Relu, bias=dtb_s[:, :])
            sc.activation(out=dttmp_s[:, :], in_=dtraw_s[:, 3:TCOL], func=AF.Abs, bias=dtb_s[:, :])
            sc.activation(out=dttmp_s[:, :], in_=dttmp_s[:, :], func=AF.Exp, scale=-1.0)
            ev_l = B.ev(sc.activation(out=dttmp_s[:, :], in_=dttmp_s[:, :], func=AF.Ln, bias=1.0), "sc")
            B.wait(ve, "ve", ev_l)
            B.ev(ve.tensor_add(dt_s[:, :], dt_s[:, :], dttmp_s[:, :]), "ve")

    if "dt_tap" in B.taps:
        dt_tap = B.dram("dt_tap", [NH, T], F32, tap=True)
        B.wait(sy, "sy", B.now("ve"))
        B.dev(sy.dma_start(out=dt_tap.ap(), in_=dt_s[:, :]), "tap")
    B.barrier()
    p1.close()
    if phases <= 1:
        B.wait_all_dma(sy)
        return nc, B

    p2 = contextlib.ExitStack()
    tps = B.ps("tps", [128, 128], F32, p2)
    g8_s = B.sb("g8_s", [NH, 8], F32, p2)
    g8b_s = B.sb("g8b_s", [NH, 8], F32, p2)
    dtTf_s = B.sb("dtTf_s", [128, NH], F32, p2)

    ve.tensor_scalar_mul(adt_s[:, :], dt_s[:, :], acol_s[:, :])
    for c in range(NC_CH):
        ve.tensor_tensor_scan(cs_s[:, c * CH:(c + 1) * CH], adt_s[:, c * CH:(c + 1) * CH],
                              adt_s[:, c * CH:(c + 1) * CH], 0.0, ALU.add, ALU.bypass)
    alast_ap = bass.AP(cs_s, CH - 1, [[CSROW, NH], [CH, NC_CH]])
    B.vchain(ve.tensor_copy(g8_s[:, :], alast_ap))
    ve.tensor_copy(g8b_s[:, 0:1], g8_s[:, 0:1])
    B.vchain(ve.tensor_add(g8b_s[:, 1:8], g8_s[:, 1:8], g8_s[:, 0:7]))
    ve.tensor_copy(g8_s[:, 0:2], g8b_s[:, 0:2])
    B.vchain(ve.tensor_add(g8_s[:, 2:8], g8b_s[:, 2:8], g8b_s[:, 0:6]))
    ve.tensor_copy(g8b_s[:, 0:4], g8_s[:, 0:4])
    B.vchain(ve.tensor_add(g8b_s[:, 4:8], g8_s[:, 4:8], g8_s[:, 0:4]))
    ve.memset(cs_s[:, T:T + 1], 0.0)
    ev_cs = B.ev(ve.tensor_copy(cs_s[:, T + 1:T + 8], g8b_s[:, 0:7]), "ve")
    B.wait(sy, "sy", ev_cs)
    ev_csh = B.dev(sy.dma_start(out=cs_h.ap(), in_=cs_s[:, :]), "csh")

    ev_evac = None
    for c in range(NC_CH):
        B.wait(te, "te", ev_cs if c == 0 else None, ev_evac)
        ev1 = B.ev(te.transpose(tps[:, 0:NH], cs_s[0:NH, c * CH:(c + 1) * CH], ident_s[0:NH, 0:NH]), "te")
        B.wait(sc, "sc", ev1)
        ev2 = B.ev(sc.copy(csT_s[:, c * NH:(c + 1) * NH], tps[:, 0:NH]), "sc")
        B.wait(te, "te", ev2)
        ev3 = B.ev(te.transpose(tps[:, 0:NH], dt_s[0:NH, c * CH:(c + 1) * CH], ident_s[0:NH, 0:NH]), "te")
        B.wait(sc, "sc", ev3)
        ev4 = B.ev(sc.copy(dtTf_s[:, 0:NH], tps[:, 0:NH]), "sc")
        ev_evac = ev4
        B.wait(ve, "ve", ev4)
        B.ev(ve.tensor_copy(dtT_s[:, c * NH:(c + 1) * NH], dtTf_s[:, 0:NH]), "ve")
    ev_csT_done = B.now("sc")
    B.wait(sc, "sc", ev_csT_done)
    sc.activation(out=expcsT_s[:, :], in_=csT_s[:, :], func=AF.Exp)
    B.ev(sc.mul(negcsT_s[:, :], csT_s[:, :], -1.0), "sc")
    B.wait(sy, "sy", ev_csT_done)
    ev_csTh = B.dev(sy.dma_start(out=csT_h.ap().rearrange("c s h -> s c h"),
                                 in_=csT_s[:].rearrange("s (c h) -> s c h", h=NH)), "cth")
    B.dev(sy.dma_start(out=csr_h.ap().rearrange("c h l -> h c l"),
                       in_=bass.AP(cs_s, 0, [[CSROW, NH], [CH, NC_CH], [1, CH]])), "csh")
    B.wait(te, "te", ev_cs, ev_evac)
    ev_gt = B.ev(te.transpose(tps[0:8, 0:NH], cs_s[0:NH, T:T + 8], ident_s[0:NH, 0:NH]), "te")
    B.wait(sc, "sc", ev_gt)
    ev_gc8 = B.ev(sc.copy(dtTf_s[0:8, 0:NH], tps[0:8, 0:NH]), "sc")
    B.wait(sy, "sy", ev_gc8)
    ev_gcsh = B.dev(sy.dma_start(out=gcs_h.ap(), in_=dtTf_s[0:8, 0:NH]), "gch")
    if "cs_tap" in B.taps:
        cs_tap = B.dram("cs_tap", [128, NC_CH * NH], F32, tap=True)
        B.wait(sy, "sy", B.now("sc"))
        B.dev(sy.dma_start(out=cs_tap.ap(), in_=csT_s[:, :]), "tap")
    B.barrier()
    p2.close()
    ev_csT = B.now("sc")
    if phases <= 2:
        B.wait_all_dma(sy)
        return nc, B

    p3 = contextlib.ExitStack()
    xtmA = [B.sb(f"xtmA{i}", [128, DI], BF16, p3) for i in range(2)]
    btm = [B.sb(f"btm{i}", [128, NG * DS], BF16, p3) for i in range(2)]
    xdd = B.sb("xdd", [128, DI], BF16, p3)
    P_s = B.sb("P_s", [128, DI], F32, p3)
    Ptmp = B.sb("Ptmp", [128, DI], F32, p3)
    ab_s = B.sb("ab_s", [128, NH], F32, p3)
    alx_s = B.sb("alx_s", [128, NH], F32, p3)
    dec_s = B.sb("dec_s", [128, NH], F32, p3)
    spsA = [B.ps(f"spsA{i}", [128, 1024], F32, p3) for i in range(2)]

    ev_p = B.ev(ve.memset(P_s[:, :], 0.0), "ve")
    ev_xuse = [None, None]
    ev_buse = [None, None]
    sps_last = [None, None]
    ev_xdduse = None
    ev_abuse = None
    ev_alxprev = None
    for c in range(NC_CH):
        par = c % 2
        if c == 0:
            B.wait(sy, "sy", ("cv0", B.cnt.get("cv0", 0)), ("cv1", B.cnt.get("cv1", 0)))
        B.wait(sy, "sy", ev_xuse[par])
        ev_x = B.dev(sy.dma_start(out=xtmA[par][:, :], in_=xpart_h.ap()[:, c * CH:(c + 1) * CH],
                                  transpose=True), f"xa{par}")
        B.wait(sy, "sy", ev_buse[par])
        ev_b = B.dev(sy.dma_start(out=btm[par][:, :], in_=b_h.ap()[:, c * CH:(c + 1) * CH],
                                  transpose=True), f"ba{par}")
        src = bass.AP(csT_h, (c * 128 + CH - 1) * NH, [[0, 128], [1, NH]])
        B.wait(gp, "gp", ev_csTh, ev_abuse, ev_alxprev)
        ev_ab = B.dev(gp.dma_start(out=ab_s[:, :], in_=src), "ab")
        B.wait(sc, "sc", ev_ab, ev_p if c > 0 else None)
        ev_alx = B.ev(sc.activation(out=alx_s[:, :], in_=ab_s[:, :], func=AF.Exp), "sc")
        ev_alxprev = ev_alx
        B.wait(ve, "ve", ev_ab, ev_csT)
        ev_sub = B.ev(ve.tensor_sub(dec_s[:, :], ab_s[:, :], csT_s[:, c * NH:(c + 1) * NH]), "ve")
        ev_abuse = ev_sub
        B.wait(sc, "sc", ev_sub)
        ev_dec = B.ev(sc.activation(out=dec_s[:, :], in_=dec_s[:, :], func=AF.Exp), "sc")
        dtT_ap = bass.AP(dtT_s, c * NH, [[NC_CH * NH, 128], [1, NH], [0, HD]])
        dec_ap = bass.AP(dec_s, 0, [[NH, 128], [1, NH], [0, HD]])
        B.wait(ve, "ve", ev_x, ev_dec, ev_xdduse)
        xdd3 = xdd[:].rearrange("p (h d) -> p h d", d=HD)
        xtm3 = xtmA[par][:].rearrange("p (h d) -> p h d", d=HD)
        ve.tensor_tensor(xdd3, xtm3, dtT_ap, ALU.mult)
        ev_xdd = B.ev(ve.tensor_tensor(xdd3, xdd3, dec_ap, ALU.mult), "ve")
        ev_xuse[par] = ev_xdd
        B.wait(gp, "gp", ev_p)
        ev_spill = B.dev(gp.dma_start(out=p_h.ap()[c], in_=P_s[:, :]), "sp")
        for sb_i in range(4):
            spar = sb_i % 2
            B.wait(te, "te", ev_xdd, ev_b, sps_last[spar])
            for hh in range(16):
                h = sb_i * 16 + hh
                g = h // 8
                ins = te.matmul(spsA[spar][:, hh * HD:(hh + 1) * HD],
                                btm[par][:, g * DS:(g + 1) * DS],
                                xdd[:, h * HD:(h + 1) * HD], start=True, stop=True)
            ev_mm = B.ev(ins, "te")
            ev_buse[par] = ev_mm
            ev_xdduse = ev_mm
            sl = slice(sb_i * 1024, (sb_i + 1) * 1024)
            alx_ap = bass.AP(alx_s, sb_i * 16, [[NH, 128], [1, 16], [0, HD]])
            B.wait(ve, "ve", ev_mm, ev_alx, ev_spill)
            ve.tensor_tensor(Ptmp[:].rearrange("p (h d) -> p h d", d=HD)[:, sb_i * 16:(sb_i + 1) * 16, :],
                             P_s[:].rearrange("p (h d) -> p h d", d=HD)[:, sb_i * 16:(sb_i + 1) * 16, :],
                             alx_ap, ALU.mult)
            ins = ve.tensor_add(P_s[:, sl], Ptmp[:, sl], spsA[spar][:, :])
            sps_last[spar] = ev_p = B.ev(ins, "ve")

    ev_msk = B.ev(ve.tensor_scalar_mul(Ptmp[:, :], P_s[:, :], smask_s[:, :]), "ve")
    B.wait(gp, "gp", ev_msk)
    ev_in = B.dev(gp.dma_start(out=ccin.ap(), in_=Ptmp[:, :]), "cci")
    B.wait(gp, "gp", ev_in)
    ev_cc = B.ev(gp.collective_compute(
        "AllReduce", ALU.add, replica_groups=[[0, 1], [2, 3], [4, 5], [6, 7]],
        ins=[ccin.ap().opt()], outs=[ccout.ap().opt()]), "cc")
    B.barrier()
    p3.close()
    if phases <= 3:
        B.wait_all_dma(sy)
        B.wait(sy, "sy", ev_cc)
        return nc, B

    p4 = contextlib.ExitStack()
    xtmB = [B.sb(f"xtmB{i}", [128, DI], BF16, p4) for i in range(2)]
    pcb = [B.sb(f"pcb{i}", [128, DI], BF16, p4) for i in range(2)]
    bcs = [B.sb(f"bcs{i}", [128, NG * CH], BF16, p4) for i in range(2)]
    ccs = [B.sb(f"ccs{i}", [128, NG * CH], BF16, p4) for i in range(2)]
    bigb = [B.sb(f"bigb{i}", [128, 32 * CH], F32, p4) for i in range(2)]
    rhs2 = B.sb("rhs2", [128, DI], BF16, p4)
    xdt = B.sb("xdt", [128, DI], BF16, p4)
    p0b = B.sb("p0b", [128, DI], BF16, p4)
    gcb_s = B.sb("gcb_s", [128, NH], F32, p4)
    dcx_s = B.sb("dcx_s", [128, NH], F32, p4)
    ych = [B.sb(f"ych{i}", [128, DI], BF16, p4) for i in range(2)]
    expb = [B.sb(f"expb{i}", [128, 32 * CH], BF16, p4) for i in range(2)]
    gb = [B.sb(f"gb{i}", [128, CH], BF16, p4) for i in range(2)]
    m8 = [B.sb(f"m8_{i}", [128, 8 * CH], BF16, p4) for i in range(2)]
    tmp8 = B.sb("tmp8", [128, 512], BF16, p4)
    gps = [B.ps(f"gps{i}", [128, CH], F32, p4) for i in range(2)]
    yps = [B.ps(f"yps{i}", [128, 512], F32, p4) for i in range(2)]

    B.wait(sy, "sy", ev_cc, ("sp", B.cnt.get("sp", 0)))
    ev_r = B.dev(sy.dma_start(out=p0b[:, :], in_=ccout.ap()), "p0r")
    B.wait(ve, "ve", ev_r)
    ev_p0 = B.ev(ve.tensor_scalar_mul(p0b[:, :], p0b[:, :], rmask_s[:, :]), "ve")

    ev_xuse = [None, None]
    ev_pcuse = [None, None]
    ev_bcuse = [None, None]
    ev_ccuse = [None, None]
    ev_bg_sc = [None, None]
    ev_bg_ve = [None, None]
    ev_expb_use = [None, None]
    ev_yuse = [None, None]
    gps_free = [None, None]
    gb_free = [None, None]
    m8_free = [None, None]
    yps_free = [None, None]
    ev_r2use = None
    ev_xdtuse = None
    ev_gcbuse = None
    ev_r2prev = None

    for c in range(NC_CH):
        par = c % 2
        B.wait(sy, "sy", ev_xuse[par])
        ev_x = B.dev(sy.dma_start(out=xtmB[par][:, :], in_=xpart_h.ap()[:, c * CH:(c + 1) * CH],
                                  transpose=True), f"xb{par}")
        B.wait(sy, "sy", ev_pcuse[par])
        ev_pc = B.dev(sy.dma_start(out=pcb[par][:, :], in_=p_h.ap()[c]), f"pb{par}")
        B.wait(sy, "sy", ev_bcuse[par])
        ev_bc = B.dev(sy.dma_start(
            out=bcs[par][:].rearrange("p (g s) -> p g s", s=CH),
            in_=b_h.ap().rearrange("(g p) t -> p g t", p=128)[:, :, c * CH:(c + 1) * CH]), f"bk{par}")
        B.wait(sy, "sy", ev_ccuse[par])
        ev_cc2 = B.dev(sy.dma_start(
            out=ccs[par][:].rearrange("p (g s) -> p g s", s=CH),
            in_=c_h.ap().rearrange("(g p) t -> p g t", p=128)[:, :, c * CH:(c + 1) * CH]), f"ck{par}")
        src = bass.AP(gcs_h, c * NH, [[0, 128], [1, NH]])
        B.wait(gp, "gp", ev_gcsh if c == 0 else None, ev_gcbuse)
        ev_gc = B.dev(gp.dma_start(out=gcb_s[:, :], in_=src), "gcb")
        B.wait(sc, "sc", ev_gc, ev_r2prev)
        ev_dc = B.ev(sc.activation(out=dcx_s[:, :], in_=gcb_s[:, :], func=AF.Exp), "sc")
        ev_gcbuse = ev_dc
        dc_ap = bass.AP(dcx_s, 0, [[NH, 128], [1, NH], [0, HD]])
        B.wait(ve, "ve", ev_dc, ev_p0, ev_r2use)
        ev_r2 = B.ev(ve.tensor_tensor(rhs2[:].rearrange("p (h d) -> p h d", d=HD),
                                      p0b[:].rearrange("p (h d) -> p h d", d=HD), dc_ap, ALU.mult), "ve")
        ev_r2prev = ev_r2
        dtT_ap = bass.AP(dtT_s, c * NH, [[NC_CH * NH, 128], [1, NH], [0, HD]])
        B.wait(ve, "ve", ev_x, ev_xdtuse)
        ev_xd = B.ev(ve.tensor_tensor(xdt[:].rearrange("p (h d) -> p h d", d=HD),
                                      xtmB[par][:].rearrange("p (h d) -> p h d", d=HD),
                                      dtT_ap, ALU.mult), "ve")
        ev_bg = [None, None]
        penrep = bass.AP(pen_s, 0, [[128, 128], [0, 32], [1, CH]])
        for half in range(2):
            B.wait(gp, "gp", ev_bg_sc[half], ev_bg_ve[half])
            src1 = bass.AP(csr_h, (c * NH + half * 32) * CH, [[0, 128], [1, 32 * CH]])
            ev_ld = B.dev(gp.dma_start(out=bigb[half][:, :], in_=src1), f"bg{half}")
            B.wait(ve, "ve", ev_ld, ev_bg_sc[half])
            ev_bg[half] = B.ev(ve.tensor_tensor(bigb[half][:].rearrange("p (h l) -> p h l", l=CH),
                                                bigb[half][:].rearrange("p (h l) -> p h l", l=CH),
                                                penrep, ALU.add), "ve")
            ev_bg_ve[half] = ev_bg[half]
        ev_y = None
        pend = None

        def finish_group(gq, evscq, evmq):
            nonlocal ev_y, ev_xdtuse
            gp_q = gq % 2
            B.wait(te, "te", evmq, ev_xd, evscq)
            for hh8 in range(8):
                h = gq * 8 + hh8
                ins = te.matmul(yps[gp_q][:, hh8 * HD:(hh8 + 1) * HD],
                                m8[gp_q][:, hh8 * CH:(hh8 + 1) * CH],
                                xdt[:, h * HD:(h + 1) * HD], start=False, stop=True)
            ev_yd = B.ev(ins, "te")
            ev_xdtuse = ev_yd
            m8_free[gp_q] = ev_yd
            drep_ap = bass.AP(drep_s, gq * 8, [[NH, 128], [1, 8], [0, HD]])
            B.wait(ve, "ve", ev_yd, ev_yuse[par] if gq == 0 else None)
            ve.tensor_tensor(tmp8[:].rearrange("p (h d) -> p h d", d=HD),
                             xtmB[par][:].rearrange("p (h d) -> p h d", d=HD)[:, gq * 8:(gq + 1) * 8, :],
                             drep_ap, ALU.mult)
            ins = ve.tensor_add(ych[par][:, gq * 512:(gq + 1) * 512], tmp8[:, :], yps[gp_q][:, :])
            ev_y = B.ev(ins, "ve")
            yps_free[gp_q] = ev_y

        for g in range(NG):
            half = g // 4
            gpar = g % 2
            gsl = slice(g * CH, (g + 1) * CH)
            B.wait(te, "te", ev_bc, ev_cc2, gps_free[gpar])
            ev_sc_mm = B.ev(te.matmul(gps[gpar][:, :], bcs[par][:, gsl],
                                      ccs[par][:, gsl], start=True, stop=True), "te")
            ev_bcuse[par] = ev_sc_mm
            B.wait(sc, "sc", ev_bg[half], ev_expb_use[half], ev_csT if c == 0 else None)
            for hh8 in range(8):
                h = g * 8 + hh8
                slot = h % 32
                ins = sc.activation(out=expb[half][:, slot * CH:(slot + 1) * CH],
                                    in_=bigb[half][:, slot * CH:(slot + 1) * CH],
                                    func=AF.Exp, bias=negcsT_s[:, c * NH + h: c * NH + h + 1])
            ev_exp8 = B.ev(ins, "sc")
            ev_bg_sc[half] = ev_exp8
            B.wait(sc, "sc", ev_sc_mm, gb_free[gpar])
            ev_gb = B.ev(sc.copy(gb[gpar][:, :], gps[gpar][:, :]), "sc")
            gps_free[gpar] = ev_gb
            grep_ap = bass.AP(gb[gpar], 0, [[CH, 128], [0, 8], [1, CH]])
            B.wait(ve, "ve", ev_gb, ev_exp8, m8_free[gpar])
            ev_m = B.ev(ve.tensor_tensor(m8[gpar][:].rearrange("p (h l) -> p h l", l=CH),
                                         grep_ap,
                                         expb[half][:].rearrange("p (h l) -> p h l", l=CH)[:, (g % 4) * 8:(g % 4) * 8 + 8, :],
                                         ALU.mult), "ve")
            gb_free[gpar] = ev_m
            ev_expb_use[half] = ev_m
            B.wait(te, "te", ev_pc, ev_r2, yps_free[gpar])
            for hh8 in range(8):
                h = g * 8 + hh8
                te.matmul(yps[gpar][:, hh8 * HD:(hh8 + 1) * HD], ccs[par][:, gsl],
                          pcb[par][:, h * HD:(h + 1) * HD], start=(hh8 == 0), stop=False)
                ins = te.matmul(yps[gpar][:, hh8 * HD:(hh8 + 1) * HD], ccs[par][:, gsl],
                                rhs2[:, h * HD:(h + 1) * HD], start=False, stop=False)
            ev_yo = B.ev(ins, "te")
            ev_pcuse[par] = ev_yo
            ev_ccuse[par] = ev_yo
            ev_r2use = ev_yo
            ecs_ap = bass.AP(expcsT_s, c * NH + g * 8, [[NC_CH * NH, 128], [1, 8], [0, HD]])
            B.wait(ve, "ve", ev_yo)
            ev_sc_y = B.ev(ve.tensor_tensor(yps[gpar][:].rearrange("p (h d) -> p h d", d=HD),
                                            yps[gpar][:].rearrange("p (h d) -> p h d", d=HD),
                                            ecs_ap, ALU.mult), "ve")
            if pend is not None:
                finish_group(*pend)
            pend = (g, ev_sc_y, ev_m)
        finish_group(*pend)
        ev_xuse[par] = ev_y
        B.wait(sy, "sy", ev_y)
        ev_yuse[par] = B.dev(sy.dma_start(out=y_h.ap()[c * CH:(c + 1) * CH, :], in_=ych[par][:, :]), f"yo{par}")
        if "bigb_tap" in B.taps and c == 7:
            bigb_tap = B.dram("bigb_tap", [128, NH * CH], F32, tap=True)
            B.wait(sy, "sy", B.now("ve"))
            B.dev(sy.dma_start(out=bigb_tap.ap()[:, 0:32 * CH], in_=bigb[0][:, :]), "tap")
            B.dev(sy.dma_start(out=bigb_tap.ap()[:, 32 * CH:], in_=bigb[1][:, :]), "tap")
        if "r2_tap" in B.taps and c == 7:
            r2_tap = B.dram("r2_tap", [128, DI], BF16, tap=True)
            p0_tap = B.dram("p0_tap", [128, DI], BF16, tap=True)
            dc_tap = B.dram("dc_tap", [128, NH], F32, tap=True)
            B.wait(sy, "sy", B.now("ve"))
            B.dev(sy.dma_start(out=r2_tap.ap(), in_=rhs2[:, :]), "tap")
            B.dev(sy.dma_start(out=p0_tap.ap(), in_=p0b[:, :]), "tap")
            B.dev(sy.dma_start(out=dc_tap.ap(), in_=dcx_s[:, :]), "tap")

    B.barrier()
    p4.close()
    if phases <= 4:
        B.wait_all_dma(sy)
        return nc, B

    p5 = contextlib.ExitStack()
    p5ps = contextlib.ExitStack()
    yfm = B.sb("yfm", [128, 32 * T], BF16, p5)
    zw = [B.sb(f"zw{i}", [128, 16 * 128], BF16, p5ps) for i in range(2)]
    zs = [B.sb(f"zs{i}", [128, T], BF16, p5ps) for i in range(2)]
    sqt = B.sb("sqt", [128, T], BF16, p5ps)
    v_s = [B.sb(f"v_s{i}", [1, T], F32, p5ps) for i in range(2)]
    rstd8 = B.sb("rstd8", [NG, T], F32, p5ps)
    rb = [B.sb(f"rb{i}", [128, T], F32, p5ps) for i in range(2)]
    zp = [B.ps(f"zp{i}", [128, 512], F32, p5ps) for i in range(4)]
    stp = [B.ps(f"stp{i}", [1, 1024], F32, p5ps) for i in range(2)]

    zw_use = [None, None]
    zs_use = [None, None]
    zp_last = [None, None]
    rb_use = [None, None]
    rbg_use = [None, None]
    ev_sqtuse = None
    ev_gate = {}
    ev_norm = {}
    ev_ytgate = {}
    for zc in range(32):
        par = zc % 2
        if zc == 0:
            B.wait(sy, "sy", ("yo0", B.cnt.get("yo0", 0)), ("yo1", B.cnt.get("yo1", 0)))
        B.wait(sy, "sy", ev_ytgate.get(zc - 2))
        ev_yt = B.dev(sy.dma_start(out=yfm[:, zc * T:(zc + 1) * T],
                                   in_=y_h.ap()[:, zc * 128:(zc + 1) * 128], transpose=True), f"yt{par}")
        src = bass.AP(t["winT"], 6208 + zc * 128, [[10304, 128], [128 * 10304, 16], [1, 128]])
        B.wait(gp, "gp", zw_use[par])
        ev_w = B.dev(gp.dma_start(out=zw[par][:].rearrange("p (k f) -> p k f", f=128), in_=src), f"zw{par}")
        B.wait(te, "te", ev_w, zp_last[par])
        zpset = zp[par * 2: par * 2 + 2]
        for k in range(16):
            for nt in range(2):
                ins = te.matmul(zpset[nt][:, :], zw[par][:, k * 128:(k + 1) * 128],
                                xfm_s[:, k * TCOL + 3 + nt * 512: k * TCOL + 3 + (nt + 1) * 512],
                                start=(k == 0), stop=(k == 15))
        ev_mm = B.ev(ins, "te")
        zw_use[par] = ev_mm
        ev_sl = None
        for nt in range(2):
            B.wait(sc, "sc", ev_mm, zs_use[par] if nt == 0 else None)
            ev_sl = B.ev(sc.activation(out=zs[par][:, nt * 512:(nt + 1) * 512],
                                       in_=zpset[nt][:, :], func=AF.Silu), "sc")
        zp_last[par] = ev_sl
        B.wait(ve, "ve", ev_sl, ev_yt)
        ev_g = B.ev(ve.tensor_mul(yfm[:, zc * T:(zc + 1) * T], yfm[:, zc * T:(zc + 1) * T],
                                  zs[par][:, :]), "ve")
        ev_ytgate[zc] = ev_g
        zs_use[par] = ev_g
        ev_gate[zc] = ev_g
    for gidx in range(NG):
        gpar = gidx % 2
        for j in range(4):
            zc = gidx * 4 + j
            B.wait(sc, "sc", ev_gate[zc], ev_sqtuse)
            ev_sq = B.ev(sc.activation(out=sqt[:, :], in_=yfm[:, zc * T:(zc + 1) * T],
                                       func=AF.Square), "sc")
            B.wait(te, "te", ev_sq)
            for nt in range(2):
                ins = te.matmul(stp[gpar][:, nt * 512:(nt + 1) * 512], ones_s[:, :],
                                sqt[:, nt * 512:(nt + 1) * 512], start=(j == 0), stop=(j == 3),
                                skip_group_check=True)
            ev_st = B.ev(ins, "te")
            ev_sqtuse = ev_st
        B.wait(ve, "ve", ev_st, rbg_use[gpar])
        ev_v = B.ev(ve.tensor_scalar(v_s[gpar][:, :], stp[gpar][:, :], 1.0 / 512, EPS,
                                     ALU.mult, ALU.add), "ve")
        B.wait(sc, "sc", ev_v)
        sc.activation(out=v_s[gpar][:, :], in_=v_s[gpar][:, :], func=AF.Ln)
        ev_rs = B.ev(sc.activation(out=v_s[gpar][:, :], in_=v_s[gpar][:, :],
                                   func=AF.Exp, scale=-0.5), "sc")
        B.wait(sy, "sy", ev_rs, rbg_use[gpar])
        ev_rh = B.dev(sy.dma_start(out=rstd_h.ap()[gidx:gidx + 1, :],
                                   in_=v_s[gpar][:, :]), f"rs{gpar}")
        B.wait(gp, "gp", ev_rh, rb_use[gpar])
        ev_rb = B.dev(gp.dma_start(out=rb[gpar][:, :],
                                   in_=bass.AP(rstd_h, gidx * T, [[0, 128], [1, T]])), f"rb{gpar}")
        rbg_use[gpar] = ev_rb
        for zc2 in range(gidx * 4, gidx * 4 + 4):
            B.wait(ve, "ve", ev_rb, ev_consts if gidx == 0 else None)
            ins = ve.scalar_tensor_tensor(yfm[:, zc2 * T:(zc2 + 1) * T], yfm[:, zc2 * T:(zc2 + 1) * T],
                                          gcol_s[:, zc2:zc2 + 1], rb[gpar][:, :], ALU.mult, ALU.mult)
            ev_norm[zc2] = B.ev(ins, "ve")
        rb_use[gpar] = ev_norm[gidx * 4 + 3]
    if "yn_h" in B.taps:
        B.wait(sy, "sy", B.now("ve"))
        for zc in range(32):
            B.dev(sy.dma_start(out=yn_h.ap()[zc * 128:(zc + 1) * 128, :],
                               in_=yfm[:, zc * T:(zc + 1) * T]), "tap")
    B.barrier()
    p5ps.close()
    if phases <= 5:
        B.wait_all_dma(sy)
        return nc, B

    p6 = contextlib.ExitStack()
    r1 = B.sb("r1", [128, 8 * DM], F32, p6)
    wo = [B.sb(f"wo{i}", [128, 512], BF16, p6) for i in range(2)]
    xrs = [B.sb(f"xrs{i}", [128, 512], F32, p6) for i in range(2)]
    mcol = B.sb("mcol", [128, 1], F32, p6)
    scol = B.sb("scol", [128, 1], F32, p6)
    vcol = B.sb("vcol", [128, 1], F32, p6)
    tcol = B.sb("tcol", [128, 1], F32, p6)
    snk = B.sb("snk", [128, DM], BF16, p6)
    x1t = B.sb("x1t", [128, DM], BF16, p6)
    x1lo = B.sb("x1lo", [128, DM], BF16, p6)
    op_ps = [B.ps(f"op_ps{i}", [128, 512], F32, p6) for i in range(8)]

    wo_use = [None, None]
    xrs_use = [None, None]
    ev_r1 = {}
    for dmq in range(4):
        ev_mm = None
        for chx in range(32):
            par = chx % 2
            src = bass.AP(t["woutT"], chx * 128 * DM + dmq * 512, [[DM, 128], [1, 512]])
            B.wait(gp, "gp", wo_use[par])
            ev_w = B.dev(gp.dma_start(out=wo[par][:, :], in_=src), f"wo{par}")
            B.wait(te, "te", ev_w)
            for tt in range(8):
                if chx == 0:
                    B.wait(te, "te", ev_r1.get((dmq - 1, tt)), ev_norm.get(31) if dmq == 0 else None)
                ins = te.matmul(op_ps[tt][:, :], yfm[:, chx * T + tt * 128: chx * T + (tt + 1) * 128],
                                wo[par][:, :], start=(chx == 0), stop=(chx == 31))
            ev_chx = B.ev(ins, "te")
            wo_use[par] = ev_chx
            if chx == 31:
                ev_mm = ev_chx
        for tt in range(8):
            par = tt % 2
            B.wait(sy, "sy", xrs_use[par])
            ev_xr = B.dev(sy.dma_start(out=xrs[par][:, :],
                                       in_=t["xres"].ap()[tt * 128:(tt + 1) * 128,
                                                          dmq * 512:(dmq + 1) * 512]), f"xr{par}")
            B.wait(ve, "ve", ev_mm, ev_xr)
            ev = B.ev(ve.tensor_add(r1[:, tt * DM + dmq * 512: tt * DM + (dmq + 1) * 512],
                                    op_ps[tt][:, :], xrs[par][:, :]), "ve")
            ev_r1[(dmq, tt)] = ev
            xrs_use[par] = ev
    x1t_use = None
    x1lo_use = None
    ev_colsuse = None
    for tt in range(8):
        r1t = r1[:, tt * DM:(tt + 1) * DM]
        B.wait(sc, "sc", ev_r1[(3, tt)], ev_colsuse)
        sc.activation(out=snk[:, :], in_=r1t, func=AF.Copy, accum_out=mcol[:, :])
        sc.activation(out=snk[:, :], in_=r1t, func=AF.Square, accum_out=scol[:, :])
        sc.mul(mcol[:, :], mcol[:, :], 1.0 / DM)
        sc.activation(out=vcol[:, :], in_=scol[:, :], func=AF.Identity, scale=1.0 / DM, bias=eps_s[:, :])
        sc.activation(out=tcol[:, :], in_=mcol[:, :], func=AF.Square)
        sc.activation(out=vcol[:, :], in_=tcol[:, :], func=AF.Identity, scale=-1.0, bias=vcol[:, :])
        sc.activation(out=vcol[:, :], in_=vcol[:, :], func=AF.Ln)
        ev_rs = B.ev(sc.activation(out=vcol[:, :], in_=vcol[:, :], func=AF.Exp, scale=-0.5), "sc")
        B.wait(ve, "ve", ev_rs, x1t_use)
        ev_x1 = B.ev(ve.tensor_scalar(x1t[:, :], r1t, mcol[:, :], vcol[:, :],
                                      ALU.subtract, ALU.mult), "ve")
        ev_x1f = B.ev(ve.tensor_scalar(r1t, r1t, mcol[:, :], vcol[:, :],
                                       ALU.subtract, ALU.mult), "ve")
        B.wait(ve, "ve", x1lo_use)
        ev_lo = B.ev(ve.tensor_sub(x1lo[:, :], r1t, x1t[:, :]), "ve")
        ev_colsuse = ev_lo
        B.wait(sy, "sy", ev_x1, ev_x1f, ev_lo)
        x1t_use = B.dev(sy.dma_start(out=x1_h.ap()[tt * 128:(tt + 1) * 128, :], in_=x1t[:, :]), "x1t")
        x1lo_use = B.dev(sy.dma_start(out=x1lo_h.ap()[tt * 128:(tt + 1) * 128, :], in_=x1lo[:, :]), "x1l")
        B.dev(sy.dma_start(out=x1f_h.ap()[tt * 128:(tt + 1) * 128, :], in_=r1t), "x1f")
    B.barrier()
    p6.close()
    p5.close()
    main_scope.close()
    if phases <= 6:
        B.wait_all_dma(sy)
        return nc, B

    p78 = contextlib.ExitStack()
    o8_s = B.sb("o8_s", [8, DM], F32, p78)
    p7 = contextlib.ExitStack()
    x1fm = B.sb("x1fm", [128, 16 * T], BF16, p7)
    x1lofm = B.sb("x1lofm", [128, 16 * T], BF16, p7)
    h1 = B.sb("h1", [128, 16 * T], BF16, p7)
    h2 = B.sb("h2", [128, 8 * DM], BF16, p7)
    wf = [B.sb(f"wf{i}", [128, 2048], BF16, p7) for i in range(2)]
    wrhl = B.sb("wrhl", [128, 16 * 16], BF16, p7)
    wrhi = B.sb("wrhi", [128, 16 * 8], BF16, p7)
    rt_sb = B.sb("rt_sb", [16, 1024], F32, p7)
    rt_sb2 = B.sb("rt_sb2", [8, 1024], F32, p7)
    lgt = B.sb("lgt", [128, 24], F32, p7)
    lsum = B.sb("lsum", [128, 8], F32, p7)
    e8 = B.sb("e8", [128, 8], F32, p7)
    msk8 = B.sb("msk8", [128, 8], F32, p7)
    tm8 = B.sb("tm8", [128, 8], F32, p7)
    m1c = B.sb("m1c", [128, 1], F32, p7)
    nm1c = B.sb("nm1c", [128, 1], F32, p7)
    m2c = B.sb("m2c", [128, 1], F32, p7)
    e2c = B.sb("e2c", [128, 1], F32, p7)
    rdc = B.sb("rdc", [128, 1], F32, p7)
    wgate = B.sb("wgate", [128, 8 * NC_CH], BF16, p7)
    gl_s = B.sb("gl_s", [8, DM], F32, p7)
    gs_s = B.sb("gs_s", [8, DM], F32, p7)
    gT = B.sb("gT", [128, 16 * 8], BF16, p7)

    ev_xf = {}
    ev_xlo = {}
    B.wait(sy, "sy", ("x1t", B.cnt.get("x1t", 0)), ("x1l", B.cnt.get("x1l", 0)))
    for k in range(16):
        ev_xf[k] = B.dev(sy.dma_start(out=x1fm[:, k * T:(k + 1) * T],
                                      in_=x1_h.ap()[:, k * 128:(k + 1) * 128], transpose=True), "xf")
        ev_xlo[k] = B.dev(sy.dma_start(out=x1lofm[:, k * T:(k + 1) * T],
                                       in_=x1lo_h.ap()[:, k * 128:(k + 1) * 128], transpose=True), "xf")
    B.wait(ve, "ve", ev_consts)
    wr3 = wr_s[:].rearrange("p (k e) -> p k e", e=8)
    wrhi3 = wrhi[:].rearrange("p (k e) -> p k e", e=8)
    wrhl3 = wrhl[:].rearrange("p (k e) -> p k e", e=16)
    B.vchain(ve.tensor_copy(wrhi3, wr3))
    ve.tensor_copy(wrhl3[:, :, 0:8], wrhi3)
    ev_wr = B.ev(ve.tensor_sub(wrhl3[:, :, 8:16], wr3, wrhi3), "ve")

    p7r = contextlib.ExitStack()
    rtA = B.ps("rtA", [16, 512], F32, p7r)
    rtB = B.ps("rtB", [8, 512], F32, p7r)
    trp = B.ps("trp", [128, 128], F32, p7r)
    ev_wguse = None
    for blk in range(2):
        B.wait(te, "te", ev_wr, ev_xf[15], ev_xlo[15])
        for k in range(16):
            te.matmul(rtA[:, :], wrhl[:, k * 16:(k + 1) * 16],
                      x1fm[:, k * T + blk * 512: k * T + (blk + 1) * 512],
                      start=(k == 0), stop=(k == 15))
        for k in range(16):
            ins = te.matmul(rtB[:, :], wrhi[:, k * 8:(k + 1) * 8],
                            x1lofm[:, k * T + blk * 512: k * T + (blk + 1) * 512],
                            start=(k == 0), stop=(k == 15))
        ev_rt = B.ev(ins, "te")
        B.wait(sc, "sc", ev_rt)
        sc.copy(rt_sb[0:16, blk * 512:(blk + 1) * 512], rtA[:, :])
        ev_cp = B.ev(sc.copy(rt_sb2[0:8, blk * 512:(blk + 1) * 512], rtB[:, :]), "sc")
        for q in range(4):
            tt = blk * 4 + q
            B.wait(te, "te", ev_cp, ev_wguse)
            te.transpose(trp[:, 0:16], rt_sb[0:16, blk * 512 + q * 128: blk * 512 + (q + 1) * 128],
                         ident_s[0:16, 0:16])
            ev_t1 = B.ev(te.transpose(trp[:, 16:24], rt_sb2[0:8, blk * 512 + q * 128: blk * 512 + (q + 1) * 128],
                                      ident_s[0:8, 0:8]), "te")
            B.wait(sc, "sc", ev_t1)
            ev_lg = B.ev(sc.copy(lgt[:, :], trp[:, 0:24]), "sc")
            B.wait(ve, "ve", ev_lg)
            B.vchain(ve.tensor_add(lsum[:, :], lgt[:, 0:8], lgt[:, 8:16]))
            B.vchain(ve.tensor_add(lsum[:, :], lsum[:, :], lgt[:, 16:24]))
            B.vchain(ve.reduce_max(m1c[:, :], lsum[:, :], axis=mybir.AxisListType.X))
            B.vchain(ve.tensor_scalar(tm8[:, :], lsum[:, :], m1c[:, :], -1e6, ALU.is_ge, ALU.mult))
            B.vchain(ve.tensor_add(tm8[:, :], tm8[:, :], lsum[:, :]))
            B.vchain(ve.reduce_max(m2c[:, :], tm8[:, :], axis=mybir.AxisListType.X))
            ev_ve1 = B.vchain(ve.tensor_scalar_mul(nm1c[:, :], m1c[:, :], -1.0))
            B.wait(sc, "sc", ev_ve1)
            sc.activation(out=e8[:, :], in_=lsum[:, :], func=AF.Exp, bias=nm1c[:, :])
            ev_e2 = B.ev(sc.activation(out=e2c[:, :], in_=m2c[:, :], func=AF.Exp, bias=nm1c[:, :]), "sc")
            B.wait(ve, "ve", ev_e2)
            B.vchain(ve.tensor_scalar_add(e2c[:, :], e2c[:, :], 1.0))
            B.vchain(ve.reciprocal(rdc[:, :], e2c[:, :]))
            B.vchain(ve.tensor_single_scalar(msk8[:, :], lsum[:, :], m2c[:, :], ALU.is_ge))
            sl = wgate[:, tt * 8:(tt + 1) * 8]
            B.vchain(ve.tensor_mul(msk8[:, :], e8[:, :], msk8[:, :]))
            ev_wg = B.ev(ve.tensor_scalar_mul(sl, msk8[:, :], rdc[:, :]), "ve")
            ev_wguse = ev_wg
    if "wg_h" in B.taps:
        B.wait(sy, "sy", B.now("ve"))
        B.dev(sy.dma_start(out=wg_h.ap(), in_=wgate[:, :]), "tap")
    B.barrier()
    p7r.close()

    p7a = contextlib.ExitStack()
    h1p = [B.ps(f"h1p{i}", [128, 512], F32, p7a) for i in range(4)]
    wf_use = [None, None]
    h1p_last = [None, None]
    for f1 in range(16):
        par = f1 % 2
        src = bass.AP(t["w1T"], f1 * 128, [[2048, 128], [128 * 2048, 16], [1, 128]])
        B.wait(gp, "gp", wf_use[par])
        ev_w = B.dev(gp.dma_start(out=wf[par][:].rearrange("p (k f) -> p k f", f=128), in_=src), f"wf{par}")
        B.wait(te, "te", ev_w, h1p_last[par])
        hpset = h1p[par * 2: par * 2 + 2]
        for k in range(16):
            for nt in range(2):
                ins = te.matmul(hpset[nt][:, :], wf[par][:, k * 128:(k + 1) * 128],
                                x1fm[:, k * T + nt * 512: k * T + (nt + 1) * 512],
                                start=(k == 0), stop=(k == 15))
        ev_mm = B.ev(ins, "te")
        wf_use[par] = ev_mm
        for nt in range(2):
            B.wait(sc, "sc", ev_mm)
            ev_sl = B.ev(sc.activation(out=h1[:, f1 * T + nt * 512: f1 * T + (nt + 1) * 512],
                                       in_=hpset[nt][:, :], func=AF.Silu), "sc")
        h1p_last[par] = ev_sl
    ev_h1 = B.now("sc")
    B.barrier()
    p7a.close()

    p7b = contextlib.ExitStack()
    h2p = [B.ps(f"h2p{i}", [128, 512], F32, p7b) for i in range(8)]
    wf_use = [None, None]
    ev_h2ev = {}
    for f2q in range(4):
        ev_mm = None
        for k in range(16):
            par = k % 2
            src = bass.AP(t["w3T"], k * 128 * 2048 + f2q * 512, [[2048, 128], [1, 512]])
            B.wait(gp, "gp", wf_use[par])
            ev_w = B.dev(gp.dma_start(out=wf[par][:, 0:512], in_=src), f"wf{par}")
            B.wait(te, "te", ev_w)
            for tt in range(8):
                if k == 0:
                    B.wait(te, "te", ev_h2ev.get((f2q - 1, tt)))
                ins = te.matmul(h2p[tt][:, :], h1[:, k * T + tt * 128: k * T + (tt + 1) * 128],
                                wf[par][:, 0:512], start=(k == 0), stop=(k == 15))
            ev_k = B.ev(ins, "te")
            wf_use[par] = ev_k
            if k == 15:
                ev_mm = ev_k
        for tt in range(8):
            B.wait(sc, "sc", ev_mm)
            ev_h2ev[(f2q, tt)] = B.ev(sc.activation(
                out=h2[:, tt * DM + f2q * 512: tt * DM + (f2q + 1) * 512],
                in_=h2p[tt][:, :], func=AF.Silu), "sc")
    B.barrier()
    p7b.close()

    p7c = contextlib.ExitStack()
    gpp = B.ps("gpp", [8, DM], F32, p7c)
    for tt in range(8):
        for q in range(4):
            ins = te.matmul(gpp[:, q * 512:(q + 1) * 512], wgate[:, tt * 8:(tt + 1) * 8],
                            h2[:, tt * DM + q * 512: tt * DM + (q + 1) * 512],
                            start=(tt == 0), stop=(tt == 7), skip_group_check=True)
    ev_g = B.ev(ins, "te")
    B.wait(sc, "sc", ev_g)
    ev_gl = B.ev(sc.copy(gl_s[:, :], gpp[:, :]), "sc")
    B.wait(gp, "gp", ev_gl)
    ev_gin = B.dev(gp.dma_start(out=gin.ap(), in_=gl_s[:, :]), "gin")
    B.wait(gp, "gp", ev_gin)
    ev_gcc = B.ev(gp.collective_compute(
        "AllReduce", ALU.add, replica_groups=[[0, 1, 2, 3, 4, 5, 6, 7]],
        ins=[gin.ap().opt()], outs=[gout.ap().opt()]), "cc")
    B.wait(sy, "sy", ev_gcc)
    ev_gs = B.dev(sy.dma_start(out=gs_s[:, :], in_=gout.ap()), "gsr")
    if "g_h" in B.taps:
        B.wait(sy, "sy", ev_gs)
        B.dev(sy.dma_start(out=g_h.ap(), in_=gs_s[:, :]), "tap")
    B.barrier()
    p7c.close()

    p7d = contextlib.ExitStack()
    trp2 = B.ps("trp2", [128, 128], F32, p7d)
    o8p = B.ps("o8p", [8, DM], F32, p7d)
    ev_gT = None
    for k in range(16):
        B.wait(te, "te", ev_gs if k == 0 else None, ev_gT)
        ev_t = B.ev(te.transpose(trp2[:, 0:8], gs_s[0:8, k * 128:(k + 1) * 128],
                                 ident_s[0:8, 0:8]), "te")
        B.wait(ve, "ve", ev_t)
        ev_gT = B.ev(ve.tensor_copy(gT[:, k * 8:(k + 1) * 8], trp2[:, 0:8]), "ve")
    wf_use = [None, None]
    ev_mm = None
    for k in range(16):
        par = k % 2
        src = bass.AP(t["w2T"], k * 128 * 2048, [[2048, 128], [1, 2048]])
        B.wait(gp, "gp", wf_use[par])
        ev_w = B.dev(gp.dma_start(out=wf[par][:, :], in_=src), f"wf{par}")
        B.wait(te, "te", ev_w, ev_gT)
        for q in range(4):
            ins = te.matmul(o8p[:, q * 512:(q + 1) * 512], gT[:, k * 8:(k + 1) * 8],
                            wf[par][:, q * 512:(q + 1) * 512], start=(k == 0), stop=(k == 15),
                            skip_group_check=True)
        ev_mm = B.ev(ins, "te")
        wf_use[par] = ev_mm
    B.wait(ve, "ve", ev_mm)
    ev_o8 = B.ev(ve.tensor_scalar_mul(o8_s[:, :], o8p[:, :], omask_s[:, :]), "ve")
    if "o8_h" in B.taps:
        B.wait(sy, "sy", ev_o8)
        B.dev(sy.dma_start(out=o8_h.ap(), in_=o8_s[:, :]), "tap")
    B.barrier()
    p7d.close()
    p7.close()

    p8 = contextlib.ExitStack()
    xtf = [B.sb(f"xtf{i}", [128, DM], F32, p8) for i in range(2)]
    mcol = B.sb("mcol8", [128, 1], F32, p8)
    scol = B.sb("scol8", [128, 1], F32, p8)
    vcol = B.sb("vcol8", [128, 1], F32, p8)
    tcol = B.sb("tcol8", [128, 1], F32, p8)
    snk2 = B.sb("snk2", [128, DM], BF16, p8)
    xtf_use = [None, None]
    ev_colsuse = None
    for tt in range(8):
        par = tt % 2
        if tt == 0:
            B.wait(sy, "sy", ("x1f", B.cnt.get("x1f", 0)))
        B.wait(sy, "sy", xtf_use[par])
        ev_x = B.dev(sy.dma_start(out=xtf[par][:, :],
                                  in_=x1f_h.ap()[tt * 128:(tt + 1) * 128, :]), f"x8{par}")
        if tt == 0:
            B.wait(ve, "ve", ev_x, ev_o8)
            ev_x = B.ev(ve.tensor_add(xtf[par][0:8, :], xtf[par][0:8, :], o8_s[:, :]), "ve")
        B.wait(sc, "sc", ev_x, ev_colsuse)
        sc.activation(out=snk2[:, :], in_=xtf[par][:, :], func=AF.Copy, accum_out=mcol[:, :])
        sc.activation(out=snk2[:, :], in_=xtf[par][:, :], func=AF.Square, accum_out=scol[:, :])
        sc.mul(mcol[:, :], mcol[:, :], 1.0 / DM)
        sc.activation(out=vcol[:, :], in_=scol[:, :], func=AF.Identity, scale=1.0 / DM, bias=eps_s[:, :])
        sc.activation(out=tcol[:, :], in_=mcol[:, :], func=AF.Square)
        sc.activation(out=vcol[:, :], in_=tcol[:, :], func=AF.Identity, scale=-1.0, bias=vcol[:, :])
        sc.activation(out=vcol[:, :], in_=vcol[:, :], func=AF.Ln)
        ev_rs = B.ev(sc.activation(out=vcol[:, :], in_=vcol[:, :], func=AF.Exp, scale=-0.5), "sc")
        B.wait(ve, "ve", ev_rs)
        ev_o = B.ev(ve.tensor_scalar(xtf[par][:, :], xtf[par][:, :], mcol[:, :], vcol[:, :],
                                     ALU.subtract, ALU.mult), "ve")
        ev_colsuse = ev_o
        B.wait(sy, "sy", ev_o)
        xtf_use[par] = B.ev(sy.dma_start(out=out_ext.ap()[tt * 128:(tt + 1) * 128, :],
                                         in_=xtf[par][:, :]), "out", 16)
    sy.wait_ge(B.sems["out"], B.cnt["out"])
    B.wait_all_dma(sy)
    return nc, B


def prepare_inputs(inputs):
    x = np.asarray(inputs["L0_attn_hidden_states"], np.float32)
    W_in = np.asarray(inputs["L0_attn_W_in"], np.float32)
    conv_w = np.asarray(inputs["L0_attn_conv_w"], np.float32)
    conv_b = np.asarray(inputs["L0_attn_conv_b"], np.float32)
    A = np.asarray(inputs["L0_attn_A"], np.float32)
    D = np.asarray(inputs["L0_attn_D"], np.float32)
    dt_bias = np.asarray(inputs["L0_attn_dt_bias"], np.float32)
    norm_w = np.asarray(inputs["L0_attn_norm_weight"], np.float32)
    W_out = np.asarray(inputs["L0_attn_W_out"], np.float32)
    w1 = np.asarray(inputs["L0_ffn_w1"], np.float32)
    w3 = np.asarray(inputs["L0_ffn_w3"], np.float32)
    w2 = np.asarray(inputs["L0_ffn_w2"], np.float32)
    wr = np.asarray(inputs["L0_ffn_wr"], np.float32)

    perm = np.concatenate([np.arange(4096, 10304), np.arange(0, 4096)])
    winT = np.ascontiguousarray(W_in.T[:, perm])
    woutT = np.ascontiguousarray(W_out.T)
    w1T = np.ascontiguousarray(w1.T)
    w3T = np.ascontiguousarray(w3.T)
    w2T = np.ascontiguousarray(w2.T)
    wrT = np.ascontiguousarray(wr.T)
    convb = np.ascontiguousarray(conv_b.reshape(-1, 1))
    acol = np.ascontiguousarray(A.reshape(-1, 1))
    drep = np.ascontiguousarray(np.broadcast_to(D.reshape(1, -1), (128, NH)))
    dtbcol = np.ascontiguousarray(dt_bias.reshape(-1, 1))
    gcol = np.ascontiguousarray(norm_w.reshape(32, 128).T)
    ident = np.eye(128, dtype=np.float32)
    pen = np.tril(np.full((128, 128), PEN, np.float32), -1)
    onescol = np.ones((128, 1), np.float32)
    shared = dict(winT=winT, woutT=woutT, w1T=w1T, w3T=w3T, w2T=w2T, wrT=wrT,
                  convw=conv_w, convb=convb, acol=acol, drep=drep, dtbcol=dtbcol,
                  gcol=gcol, ident=ident, pen=pen, onescol=onescol)

    in_maps = []
    for core in range(NCORES):
        b, hh = divmod(core, 2)
        t0 = hh * T
        halo = (x[b, t0 - 3:t0].T if hh == 1 else np.zeros((DM, 3), np.float32))
        xfm = np.ascontiguousarray(np.concatenate([halo, x[b, t0:t0 + T].T], axis=1))
        m = dict(shared)
        m["xfm"] = xfm
        m["xres"] = np.ascontiguousarray(x[b, t0:t0 + T])
        m["sendmask"] = np.full((128, 1), 1.0 if hh == 0 else 0.0, np.float32)
        m["recvmask"] = np.full((128, 1), 1.0 if hh == 1 else 0.0, np.float32)
        m["ownermask"] = np.full((8, 1), 1.0 if core == 0 else 0.0, np.float32)
        in_maps.append(m)
    return in_maps


_BF16_KEYS = {"onescol"}


def _cast_in_maps(in_maps):
    import ml_dtypes
    out = []
    for m in in_maps:
        mm = {}
        for k, v in m.items():
            if k in _BF16_KEYS:
                mm[k] = np.ascontiguousarray(v.astype(ml_dtypes.bfloat16))
            else:
                mm[k] = np.ascontiguousarray(v)
        out.append(mm)
    return out


def kernel(**inputs):
    in_maps = _cast_in_maps(prepare_inputs(inputs))
    nc, _ = build_program(DEBUG_TAPS)
    res = run_bass_kernel_spmd(nc, in_maps, core_ids=list(range(NCORES)))
    out = np.empty((4, 2048, DM), np.float32)
    for core in range(NCORES):
        b, hh = divmod(core, 2)
        out[b, hh * T:(hh + 1) * T, :] = np.asarray(res.results[core]["out"])
    return out
